# version 14
# baseline (speedup 1.0000x reference)
import math
import numpy as np
import concourse.bass as bass
import concourse.mybir as mybir
from concourse.bass_utils import run_bass_kernel_spmd

F32 = mybir.dt.float32
F16 = mybir.dt.float16
AF = mybir.ActivationFunctionType
ALU = mybir.AluOpType

D = 1024
DI = 2048
DST = 64
HD = 64
NH = 32
DIPP = 35 * 128
DFF = 4096
LSEQ = 2048
NSEQ = 2
T = NSEQ * LSEQ
TT = 512
NTT = T // TT
Q = 128
EPS = 1e-5
NCORE = 8
NC8, NC16, NC18, NC32 = 8, 16, 18, 32
DBG_SKIP = set()
DBG_NQ = [4]
OFFS = {}
XBCW = 515


class Ctr:
    def __init__(self, handle, name):
        self.h = handle
        self.name = name
        self.n = 0


class Bump:
    def __init__(self, cap, base=0):
        self.cap = cap
        self.top = base

    def take(self, w, align=8):
        self.top = (self.top + align - 1) // align * align
        o = self.top
        self.top += w
        assert self.top <= self.cap, f"arena overflow: {self.top} > {self.cap}"
        return o


class KB:
    SERIAL = []

    def __init__(self, nc):
        self.nc = nc
        self.ops = {e: [] for e in ("t", "s", "v", "g", "y")}
        self.esem = {}

    def sem(self, name):
        return Ctr(self.nc.alloc_semaphore(name), name)

    def E(self, e, fn, waits=(), inc=None, inci=1):
        waits = tuple(waits) + tuple((c, c.n) for c in KB.SERIAL)
        ws = tuple((c, v) for (c, v) in waits if v > 0)
        if inc is None and e in self.esem:
            inc = self.esem[e]
        val = None
        if inc is not None:
            inc.n += inci
            val = inc.n
        self.ops[e].append((ws, fn, (inc.h if inc else None), inci))
        return val

    def W(self, e, waits):
        self.ops[e].append((tuple((c, v) for (c, v) in waits if v > 0), None, None, 0))

    def dr(self, e):
        if e in self.esem:
            c = self.esem[e]
            self.ops[e].append((((c, c.n),), None, None, 0))
        else:
            self.ops[e].append(((), lambda eng: eng.drain(), None, 0))

    def finalize(self, blk):
        def runner(ename):
            def body(eng):
                for ws, fn, semh, inci in self.ops[ename]:
                    for (c, v) in ws:
                        eng.wait_ge(c.h, v)
                    if fn is not None:
                        ins = fn(eng)
                        if semh is not None:
                            ins.then_inc(semh, inci)
            return body
        blk.tensor(runner("t"))
        blk.scalar(runner("s"))
        blk.vector(runner("v"))
        blk.gpsimd(runner("g"))
        blk.sync(runner("y"))


def pack_w(Wm):
    Din, Dout = Wm.shape
    nK = Din // 128
    assert Dout % 128 == 0
    cols = []
    for mc in range(Dout // 128):
        blk = Wm[:, mc * 128:(mc + 1) * 128]
        for kc in range(nK):
            cols.append(blk[kc * 128:(kc + 1) * 128, :])
    return np.ascontiguousarray(np.concatenate(cols, axis=1), dtype=np.float32)


def fm_vec(v, nch):
    return np.ascontiguousarray(np.asarray(v, np.float32).reshape(nch, 128).T)


def _mk_DP(I):
    Dv = np.asarray(I["L0_attn_D"], np.float32).reshape(32)
    DP = np.zeros((128, 16), np.float32)
    for p in range(16):
        DP[0:64, p] = Dv[2 * p]
        DP[64:128, p] = Dv[2 * p + 1]
    return DP


def build_consts():
    ones16 = np.ones((128, 128), np.float32)
    iden = np.eye(128, dtype=np.float32)
    tri = np.triu(np.ones((128, 128), np.float32))
    ind = np.zeros((32, 32 * 128), np.float32)
    for h in range(32):
        ind[h, h * 128:(h + 1) * 128] = 1.0
    sel127 = np.zeros((128, 64), np.float32)
    sel127[127, :] = 1.0
    return ones16, iden, tri, ind, sel127


def build_program(stages=("ffn1", "mamba", "mlp", "ffn2")):
    nc = bass.Bass("TRN2", target_bir_lowering=False)
    dram = lambda n, sh, dt, k="ExternalInput": nc.dram_tensor(n, sh, dt, kind=k).ap()

    xin = dram("xin", [128, NC8 * T], F32)
    wsrc = {
        "ffn1": dram("wffn1", [128, 8 * DFF], F32),
        "ffn2": dram("wffn2", [128, 32 * D], F32),
        "ip": dram("wip", [128, 8 * DIPP], F32),
        "op": dram("wop", [128, 16 * D], F32),
        "m1": dram("wm1", [128, 8 * DFF], F32),
        "m2": dram("wm2", [128, 32 * D], F32),
    }
    c_ones = dram("c_ones", [128, 128], F32)
    c_iden = dram("c_iden", [128, 128], F32)
    c_tri = dram("c_tri", [128, 128], F32)
    c_ind = dram("c_ind", [32, 4096], F32)
    c_sel = dram("c_sel", [128, 64], F32)
    psrc = {n: dram("p_" + n, sh, F32) for n, sh in [
        ("ln1", [128, 16]), ("ln2", [128, 16]), ("b1", [128, 32]),
        ("b2", [128, 8]), ("th", [128, 32]), ("wr", [128, 32]),
        ("wi", [128, 32]), ("mb1", [128, 32]), ("mb2", [128, 8]),
        ("nw", [128, 16]), ("cw", [128, 72]), ("cb", [128, 18]),
        ("D", [128, 32]), ("DP", [128, 16])]}
    p_dtb = dram("p_dtb", [32, 1], F32)
    p_al = dram("p_al", [32, 1], F32)

    xout = dram("xout", [128, NC8 * T], F32, k="ExternalOutput")
    cache = {
        "ffn1": dram("cffn1", [128, 8 * DFF], F16, k="Internal"),
        "ffn2": dram("cffn2", [128, 32 * D], F16, k="Internal"),
        "ip": dram("cip", [128, 8 * DIPP], F16, k="Internal"),
        "op": dram("cop", [128, 16 * D], F16, k="Internal"),
        "m1": dram("cm1", [128, 8 * DFF], F16, k="Internal"),
        "m2": dram("cm2", [128, 32 * D], F16, k="Internal"),
    }
    vbuf = dram("vbuf", [128, NC8 * T], F16, k="Internal")
    dbg = dram("dbg", [128, 16384], F32, k="Internal")

    B = KB(nc)
    al = nc.alloc_sbuf_tensor

    x = al("x", [128, NC8 * T], F16)
    ones16 = al("ones16", [128, 128], F16)
    iden16 = al("iden16", [128, 128], F16)
    iden32 = al("iden32", [32, 32], F32)
    tri32 = al("tri32", [128, 128], F32)
    ind16 = al("ind16", [32, 4096], F16)
    hl2 = al("hl2", [32, 384], F16)
    sel16 = al("sel16", [128, 64], F16)
    prm = al("prm", [128, 504], F32)
    prm32r = al("prm32r", [32, 4], F32)
    hstate = al("hstate", [64, NH * HD], F16)
    zer32 = al("zer32", [32, 128], F32)
    zer16 = al("zer16", [64, 64], F16)
    zer16h = al("zer16h", [64, 64], F16)
    epsb = al("epsb", [128, 1], F32)
    oneb = al("oneb", [128, 1], F32)
    eps2b = al("eps2b", [128, 1], F32)
    wk8 = [al(f"wk8{i}", [128, 2048], F16) for i in range(3)]
    wk32 = [al(f"wk32{i}", [128, 2048], F16) for i in range(3)]

    PV = {}
    _pb = Bump(504)
    for name, w in [("ln1w", 8), ("ln1b", 8), ("ln2w", 8), ("ln2b", 8),
                    ("b1", 32), ("b2", 8), ("th", 32), ("wr", 32), ("wi", 32),
                    ("mb1", 32), ("mb2", 8), ("nw", 16), ("cw", 72), ("cb", 18),
                    ("Drep", 32), ("b01s", 32), ("b01c", 32), ("Ac", 32),
                    ("Bc", 32), ("hb2", 8), ("DrepP", 16)]:
        PV[name] = _pb.take(w, align=1)
    P = lambda name, w: prm[:, PV[name]:PV[name] + w]
    Pc = lambda name, c: prm[:, PV[name] + c:PV[name] + c + 1]

    NA16 = 38912
    NA32 = 6912
    A16 = al("arena16", [128, NA16], F16)
    A32 = al("arena32", [128, NA32], F32)
    CAST_OFF = 29696
    caststg = [A16[:, CAST_OFF + i * 4096:CAST_OFF + (i + 1) * 4096] for i in range(2)]

    ps = [nc.alloc_psum_tensor(f"ps{i}", [128, 512], F32) for i in range(6)]
    p16 = [nc.alloc_psum_tensor(f"p16{i}", [128, 1024], F16) for i in range(2)]

    s_t = B.sem("s_t")
    s_s = B.sem("s_s")
    s_v = B.sem("s_v")
    g_init = B.sem("g_init")
    g_cast = [B.sem("g_cast0"), B.sem("g_cast1")]
    y_c = B.sem("y_c")
    y_wb = [B.sem("y_wb0"), B.sem("y_wb1")]
    y_k8 = [B.sem(f"y_k8{i}") for i in range(3)]
    y_k32 = [B.sem(f"y_k32{i}") for i in range(3)]
    y_out = B.sem("y_out")
    y_vw = B.sem("y_vw")
    y_vr = B.sem("y_vr")
    y_vr2 = B.sem("y_vr2")
    g_xt = [B.sem(f"g_xt{i}") for i in range(NTT)]

    B.esem = {"t": s_t, "s": s_s, "v": s_v}
    if DBG_SKIP and "globser" in DBG_SKIP:
        KB.SERIAL = [s_t, s_s, s_v, g_init, y_c] + y_k8 + y_k32 + y_wb + g_cast + [y_vw, y_vr, y_out]
    else:
        KB.SERIAL = []
    bankg = {i: [] for i in range(8)}

    def t_mm(bank_id, out_ap, lhsT, rhs, start, stop, waits=(), inc=None, tile_position=None):
        g = tuple(bankg[bank_id]) if start else ()
        if start:
            bankg[bank_id] = []
        return B.E("t", lambda e, a=out_ap, l=lhsT, r=rhs, st=start, sp=stop, tp=tile_position:
                   e.matmul(a, l, r, start=st, stop=sp, tile_position=tp),
                   waits=tuple(waits) + g, inc=inc)

    def t_tr(bank_id, out_ap, in_ap, ident, waits=(), inc=None):
        g = tuple(bankg[bank_id])
        bankg[bank_id] = []
        return B.E("t", lambda e, a=out_ap, i=in_ap, d=ident:
                   e.transpose(a, i, d), waits=tuple(waits) + g, inc=inc)

    def guard(bank_id, ctr, val):
        bankg[bank_id].append((ctr, val))

    def yd(dst, src, waits=(), inc=None):
        return B.E("y", lambda e, d=dst, s=src: e.dma_start(out=d, in_=s),
                   waits=waits, inc=inc, inci=16)

    def gd(dst, src, waits=(), inc=None):
        return B.E("g", lambda e, d=dst, s=src: e.dma_start(out=d, in_=s),
                   waits=waits, inc=inc, inci=16)

    yd(iden32[:, :], c_iden[0:32, 0:32], inc=y_c)
    yd(tri32[:, :], c_tri, inc=y_c)
    for nm, key, w in [("ln1w", "ln1", 16), ("ln2w", "ln2", 16), ("b1", "b1", 32),
                       ("b2", "b2", 8), ("th", "th", 32), ("wr", "wr", 32),
                       ("wi", "wi", 32), ("mb1", "mb1", 32), ("mb2", "mb2", 8),
                       ("nw", "nw", 16), ("cw", "cw", 72), ("cb", "cb", 18),
                       ("Drep", "D", 32), ("DrepP", "DP", 16)]:
        yd(P(nm, w), psrc[key], inc=y_c)
    yd(prm32r[:, 0:1], p_dtb, inc=y_c)
    yd(prm32r[:, 1:2], p_al, inc=y_c)
    YC0 = y_c.n

    gd(ones16[:, :], c_ones, inc=g_init)
    gd(iden16[:, :], c_iden, inc=g_init)
    gd(ind16[:, :], c_ind, inc=g_init)
    gd(sel16[:, :], c_sel, inc=g_init)
    GCONST = g_init.n
    xtile_ready = {}
    GI0 = None

    tmpa = A32[:, 0:32]
    tmpb = A32[:, 32:64]
    s0t = A32[:, 64:96]
    c0t = A32[:, 96:128]
    q1 = A32[:, 128:160]
    q2 = A32[:, 160:192]
    B.W("v", [(y_c, YC0)])
    B.E("v", lambda e: e.tensor_scalar(P("b01s", 32), P("b1", 32), 0.1, None, ALU.mult), inc=s_v)
    B.E("v", lambda e: e.tensor_scalar(P("b01c", 32), P("b1", 32), 0.1, math.pi / 2, ALU.mult, ALU.add), inc=s_v)
    B.E("v", lambda e: e.tensor_scalar(P("hb2", 8), P("b2", 8), 0.5, None, ALU.mult), inc=s_v)
    B.E("v", lambda e: e.memset(A16[:, 0:CAST_OFF], 0.0), inc=s_v)
    B.E("v", lambda e: e.memset(A16[:, CAST_OFF + 8192:NA16], 0.0), inc=s_v)
    B.E("v", lambda e: e.memset(A32[:, :], 0.0), inc=s_v)
    B.dr("v")
    B.E("v", lambda e: e.memset(zer32[:, :], 0.0), inc=s_v)
    B.E("v", lambda e: e.memset(epsb[:, :], EPS), inc=s_v)
    B.E("v", lambda e: e.memset(oneb[:, :], 1.0), inc=s_v)
    B.E("v", lambda e: e.memset(eps2b[:, :], EPS * EPS), inc=s_v)
    B.E("v", lambda e: e.memset(zer16[:, :], 0.0), inc=s_v)
    B.E("v", lambda e: e.memset(zer16h[:, :], 0.0), inc=s_v)
    B.E("v", lambda e: e.memset(hstate[:, :], 0.0), inc=s_v)
    VP0 = s_v.n
    B.W("s", [(s_v, VP0), (y_c, YC0)])
    B.E("s", lambda e: e.activation(tmpa, P("th", 32), AF.Sin, scale=1.0 / 32), inc=s_s)
    B.E("s", lambda e: e.activation(s0t, P("th", 32), AF.Sin, scale=1.0 / 16), inc=s_s)
    B.E("s", lambda e: e.activation(prm32r[:, 2:3], prm32r[:, 1:2], AF.Exp), inc=s_s)
    SP0 = s_s.n
    B.W("v", [(s_s, SP0)])
    B.E("v", lambda e: e.tensor_tensor(tmpb, tmpa, tmpa, ALU.mult), inc=s_v)
    B.dr("v")
    B.E("v", lambda e: e.tensor_scalar(c0t, tmpb, -2.0, 1.0, ALU.mult, ALU.add), inc=s_v)
    B.dr("v")
    sc_pair = [(s0t, c0t), (tmpa, tmpb)]
    for it in range(4):
        so, co = sc_pair[it % 2]
        sn, cn = sc_pair[(it + 1) % 2]
        B.E("v", lambda e, so=so, co=co, sn=sn: e.scalar_tensor_tensor(
            sn, so, 2.0, co, ALU.mult, ALU.mult), inc=s_v)
        B.E("v", lambda e, so=so, cn=cn: e.tensor_tensor(cn, so, so, ALU.mult), inc=s_v)
        B.dr("v")
        B.E("v", lambda e, cn=cn: e.tensor_scalar(cn, cn, -2.0, 1.0, ALU.mult, ALU.add), inc=s_v)
        B.dr("v")
    B.E("v", lambda e: e.tensor_scalar(prm32r[:, 2:3], prm32r[:, 2:3], -1.0, None, ALU.mult), inc=s_v)
    B.E("v", lambda e: e.tensor_tensor(q1, c0t, P("wr", 32), ALU.mult), inc=s_v)
    B.E("v", lambda e: e.tensor_tensor(q2, s0t, P("wi", 32), ALU.mult), inc=s_v)
    B.dr("v")
    B.E("v", lambda e: e.tensor_tensor(q1, q1, q2, ALU.add), inc=s_v)
    B.dr("v")
    B.E("v", lambda e: e.tensor_tensor(q1, q1, P("wi", 32), ALU.mult), inc=s_v)
    B.dr("v")
    B.E("v", lambda e: e.tensor_scalar(q1, q1, 0.1, None, ALU.mult), inc=s_v)
    B.dr("v")
    B.E("v", lambda e: e.tensor_tensor(P("Ac", 32), q1, c0t, ALU.mult), inc=s_v)
    B.E("v", lambda e: e.tensor_tensor(P("Bc", 32), q1, s0t, ALU.mult), inc=s_v)
    VPREP = s_v.n
    Aneg = prm32r[:, 2:3]
    dtbias = prm32r[:, 0:1]

    cast_jobs = []
    for key, tot in [("ffn1", 8 * DFF), ("ffn2", 32 * D), ("ip", 8 * DIPP),
                     ("op", 16 * D), ("m1", 8 * DFF), ("m2", 32 * D)]:
        o = 0
        while o < tot:
            w = min(4096, tot - o)
            cast_jobs.append((key, o, w))
            o += w
    cache_ready = {}
    cache_marks = {k: [] for k in cache}
    xq = list(range(NTT))
    def emit_xtile():
        if xq:
            tt = xq.pop(0)
            for c in range(NC8):
                gd(x[:, c * T + tt * TT:c * T + (tt + 1) * TT],
                   xin[:, c * T + tt * TT:c * T + (tt + 1) * TT], inc=g_xt[tt])
            xtile_ready[tt] = (g_xt[tt], g_xt[tt].n)
    emit_xtile()
    for i, (key, o, w) in enumerate(cast_jobs):
        sl = i % 2
        gwait = [(y_wb[sl], y_wb[sl].n)] if i >= 2 else []
        gd(caststg[sl][:, 0:w], wsrc[key][:, o:o + w], waits=gwait, inc=g_cast[sl])
        dv = yd(cache[key][:, o:o + w], caststg[sl][:, 0:w],
                waits=[(g_cast[sl], g_cast[sl].n)], inc=y_wb[sl])
        cache_marks[key].append((o + w, y_wb[sl], dv))
        cache_ready[key] = ((y_wb[0], y_wb[0].n), (y_wb[1], y_wb[1].n))
        emit_xtile()
    while xq:
        emit_xtile()
    GI0 = g_init.n
    CASTS_DONE = ((y_wb[0], y_wb[0].n), (y_wb[1], y_wb[1].n))

    def cache_wait(key, off, w):
        for end, ctr, val in cache_marks[key]:
            if end >= off + w:
                return [(ctr, val)]
        raise AssertionError("no cache mark")

    class Stream:
        def __init__(self, bufs, sems, cachekey, first_waits):
            self.bufs = bufs
            self.sems = sems
            self.key = cachekey
            self.cons = {0: None, 1: None, 2: None}
            self.i = 0
            self.pend = []
            self.emitted = []
            self.first_waits = first_waits
            self.seen = 0

        def queue(self, off, w):
            self.pend.append((off, w))
            self._try_emit()

        def _try_emit(self):
            while self.pend:
                i = self.i
                sl = i % 3
                if i >= 3 and self.cons[sl] is None:
                    break
                off, w = self.pend.pop(0)
                waits = []
                if i < 3:
                    waits += list(self.first_waits)
                waits += cache_wait(self.key, off, w)
                if self.cons[sl] is not None:
                    waits.append(self.cons[sl])
                    self.cons[sl] = None
                dv = yd(self.bufs[sl][:, 0:w], cache[self.key][:, off:off + w],
                        waits=waits, inc=self.sems[sl])
                self.emitted.append((sl, dv, off))
                self.i += 1

        def unit(self, i):
            assert i < len(self.emitted), "unit not yet emitted"
            sl, dv, off = self.emitted[i]
            return sl, (self.sems[sl], dv)

        def consumed(self, i, ctr, val):
            sl, dv, off = self.emitted[i]
            self.cons[sl] = (ctr, val)
            self._try_emit()

    LN_BASE = NA32 - 2048
    LNm = A32[:, LN_BASE:LN_BASE + 512]
    LNmsq = A32[:, LN_BASE + 512:LN_BASE + 1024]
    LNvar = A32[:, LN_BASE + 1024:LN_BASE + 1536]
    LNrstd = A32[:, LN_BASE + 1536:LN_BASE + 2048]
    SQ_BASE = NA16 - 1024
    sqp = [A16[:, SQ_BASE + i * 512:SQ_BASE + (i + 1) * 512] for i in range(2)]

    LNr12 = A32[:, LN_BASE - 512:LN_BASE]

    def emit_ln(src_fn, out_fn, nch, w, aff=None, b_sum=4, b_sq=5, src_ready=None,
                out_guard=None, also_ln2=None):
        nd = float(nch * 128)
        sr = [src_ready] if src_ready else []
        tm = None
        for c in range(nch):
            tm = t_mm(b_sum, ps[b_sum][:, 0:w], ones16[:, :], src_fn(c),
                      start=(c == 0), stop=(c == nch - 1),
                      waits=(sr if c == 0 else []),
                      inc=(s_t if c == nch - 1 else None))
        sqm = []
        for c in range(nch):
            sl = c % 2
            swaits = [(s_t, sqm[c - 2])] if c >= 2 else list(sr)
            sv = B.E("s", lambda e, c=c, sl=sl: e.activation(sqp[sl][:, 0:w], src_fn(c), AF.Square),
                     waits=swaits, inc=s_s)
            tmq = t_mm(b_sq, ps[b_sq][:, 0:w], ones16[:, :], sqp[sl][:, 0:w],
                       start=(c == 0), stop=(c == nch - 1),
                       waits=[(s_s, sv)], inc=s_t)
            sqm.append(tmq)
        vm = B.E("v", lambda e: e.tensor_scalar(LNm[:, 0:w], ps[b_sum][:, 0:w], 1.0 / nd, None, ALU.mult),
                 waits=[(s_t, tm)], inc=s_v)
        guard(b_sum, s_v, vm)
        B.dr("v")
        B.E("v", lambda e: e.tensor_tensor(LNmsq[:, 0:w], LNm[:, 0:w], LNm[:, 0:w], ALU.mult), inc=s_v)
        B.dr("v")
        vv = B.E("v", lambda e: e.scalar_tensor_tensor(LNvar[:, 0:w], ps[b_sq][:, 0:w], 1.0 / nd,
                                                       LNmsq[:, 0:w], ALU.mult, ALU.subtract),
                 waits=[(s_t, sqm[-1])], inc=s_v)
        guard(b_sq, s_v, vv)
        sv = B.E("s", lambda e: e.activation(LNrstd[:, 0:w], LNvar[:, 0:w], AF.Ln, bias=epsb[:, 0:1]),
                 waits=[(s_v, vv)], inc=s_s)
        B.dr("s")
        sv = B.E("s", lambda e: e.activation(LNrstd[:, 0:w], LNrstd[:, 0:w], AF.Exp, scale=-0.5), inc=s_s)
        if also_ln2 is not None:
            sv2b = B.E("s", lambda e: e.activation(LNr12[:, 0:w], LNvar[:, 0:w], AF.Ln,
                                                   scale=1.0 + EPS, bias=eps2b[:, 0:1]),
                       waits=[(s_v, vv)], inc=s_s)
            B.dr("s")
            sv2b = B.E("s", lambda e: e.activation(LNr12[:, 0:w], LNr12[:, 0:w], AF.Exp, scale=-0.5), inc=s_s)
            B.W("v", [(s_s, sv2b)])
        B.W("v", [(s_s, sv)])
        applies = []
        last = None
        for c in range(nch):
            og = [out_guard] if (out_guard is not None and c == 0) else []
            B.E("v", lambda e, c=c: e.tensor_tensor(LNmsq[:, 0:w], src_fn(c), LNm[:, 0:w], ALU.subtract),
                waits=og, inc=s_v)
            B.dr("v")
            if aff is None:
                last = B.E("v", lambda e, c=c: e.tensor_tensor(out_fn(c), LNmsq[:, 0:w], LNrstd[:, 0:w], ALU.mult), inc=s_v)
            else:
                wn, bn = aff
                B.E("v", lambda e: e.tensor_tensor(LNvar[:, 0:w], LNmsq[:, 0:w], LNrstd[:, 0:w], ALU.mult), inc=s_v)
                B.dr("v")
                last = B.E("v", lambda e, c=c, wn=wn, bn=bn: e.tensor_scalar(
                    out_fn(c), LNvar[:, 0:w], Pc(wn, c), Pc(bn, c), ALU.mult, ALU.add), inc=s_v)
            if also_ln2 is not None:
                out2_fn, wn2, bn2 = also_ln2
                B.E("v", lambda e, c=c: e.tensor_tensor(LNvar[:, 0:w], LNmsq[:, 0:w], LNr12[:, 0:w], ALU.mult), inc=s_v)
                B.dr("v")
                last = B.E("v", lambda e, c=c, out2_fn=out2_fn, wn2=wn2, bn2=bn2: e.tensor_scalar(
                    out2_fn(c), LNvar[:, 0:w], Pc(wn2, c), Pc(bn2, c), ALU.mult, ALU.add), inc=s_v)
            B.dr("v")
            applies.append(last)
        return applies

    def x_sl(c, tt, w=TT):
        return x[:, c * T + tt * TT:c * T + tt * TT + w]

    def ffn_stage(key1, key2, final):
        a16 = Bump(CAST_OFF if not final else SQ_BASE)
        a32 = Bump(LN_BASE - 512, base=256)
        xln2 = []
        for _ in range(2):
            xln = []
            for _ in range(NC8):
                o = a16.take(TT)
                xln.append(A16[:, o:o + TT])
            xln2.append(xln)
        act = []
        for _ in range(NC32):
            o = a16.take(TT)
            act.append(A16[:, o:o + TT])
        s1p, c1p = [], []
        for _ in range(2):
            o = a32.take(TT); s1p.append(A32[:, o:o + TT])
        for _ in range(2):
            o = a32.take(TT); c1p.append(A32[:, o:o + TT])
        o = a32.take(TT); v1 = A32[:, o:o + TT]
        o = a32.take(TT); ev = A32[:, o:o + TT]
        if final:
            o = a32.take(1024)
            ost = A32[:, o:o + 1024]
            xo_r = xout.rearrange("p (c t) -> p c t", c=NC8)

        st1 = Stream(wk8, y_k8, key1, [(s_t, s_t.n)])
        st2 = Stream(wk32, y_k32, key2, [(s_t, s_t.n)])
        q_cons = {0: 0, 1: 0}
        B.W("t", [(g_init, GCONST), (y_c, YC0), (s_v, s_v.n), (s_s, s_s.n)])
        B.W("s", [(g_init, GCONST), (y_c, YC0)])
        vln = {}
        hid_end = {}
        if "nopipe" not in DBG_SKIP:
            emit_ln(lambda c: x_sl(c, 0), lambda c: xln2[0][c], NC8, TT,
                    src_ready=xtile_ready[0])
            vln[0] = s_v.n
        for tt in range(NTT):
            xln = xln2[tt % 2]
            if "nopipe" in DBG_SKIP:
                emit_ln(lambda c, tt=tt: x_sl(c, tt), lambda c, tt=tt: xln2[tt % 2][c],
                        NC8, TT, out_guard=(s_t, hid_end.get(tt - 1, 0)))
                vln[tt] = s_v.n
            for gi in range(16):
                st1.queue(gi * 2048, 2048)
            for fc in range(NC32):
                gi = fc // 2
                sl, (sem, dv) = st1.unit(tt * 16 + gi)
                boff = (fc % 2) * 1024
                bank = fc % 4
                for kc in range(8):
                    w0 = []
                    if kc == 0:
                        w0 = [(sem, dv), (s_v, vln[tt])]
                    t_mm(bank, ps[bank][:, 0:TT],
                         wk8[sl][:, boff + kc * 128:boff + kc * 128 + 128],
                         xln[kc][:, :], start=(kc == 0), stop=(kc == 7),
                         waits=w0, inc=(s_t if kc == 7 else None))
                tmv = s_t.n
                if fc % 2 == 1:
                    st1.consumed(tt * 16 + gi, s_t, tmv)
                sw = [(s_t, tmv)]
                if fc >= 2:
                    sw.append((s_v, q_cons[fc % 2]))
                sv1 = B.E("s", lambda e, b=bank, fc=fc: e.activation(
                    s1p[fc % 2], ps[b][:, 0:TT], AF.Sin, bias=Pc("b01s", fc), scale=0.1),
                    waits=sw, inc=s_s)
                sv2 = B.E("s", lambda e, b=bank, fc=fc: e.activation(
                    c1p[fc % 2], ps[b][:, 0:TT], AF.Sin, bias=Pc("b01c", fc), scale=0.1), inc=s_s)
                vv = B.E("v", lambda e, b=bank, fc=fc: e.scalar_tensor_tensor(
                    v1, s1p[fc % 2], Pc("Ac", fc), ps[b][:, 0:TT], ALU.mult, ALU.add),
                    waits=[(s_s, sv2)], inc=s_v)
                guard(bank, s_v, vv)
                B.dr("v")
                vv2 = B.E("v", lambda e, fc=fc: e.scalar_tensor_tensor(
                    v1, c1p[fc % 2], Pc("Bc", fc), v1, ALU.mult, ALU.mult if False else ALU.add), inc=s_v)
                q_cons[fc % 2] = vv2
                B.dr("v")
                B.E("v", lambda e, fc=fc: e.tensor_scalar(
                    act[fc][:, :], v1, Pc("b1", fc), 0.0, ALU.add, ALU.max), inc=s_v)
                B.dr("v")
            vact = s_v.n
            hid_end[tt] = s_t.n
            if tt + 1 < NTT and "nopipe" not in DBG_SKIP:
                emit_ln(lambda c, tt=tt: x_sl(c, tt + 1), lambda c, tt=tt: xln2[(tt + 1) % 2][c],
                        NC8, TT, out_guard=(s_t, hid_end.get(tt - 1, 0)),
                        src_ready=xtile_ready[tt + 1])
                vln[tt + 1] = s_v.n
            for hdc in range(16):
                st2.queue(hdc * 2048, 2048)
            if final:
                og = (y_out, y_out.n)
            for dc in range(NC8):
                bank = dc % 4
                for kc in range(32):
                    hdc = dc * 2 + kc // 16
                    sl, (sem, dv) = st2.unit(tt * 16 + hdc)
                    w0 = []
                    if kc % 16 == 0:
                        w0 = [(sem, dv)]
                    if kc == 0:
                        w0 = w0 + [(s_v, vact)]
                    t_mm(bank, ps[bank][:, 0:TT],
                         wk32[sl][:, (kc % 16) * 128:(kc % 16) * 128 + 128],
                         act[kc][:, :], start=(kc == 0), stop=(kc == 31),
                         waits=w0, inc=(s_t if kc in (15, 31) else None))
                    if kc == 15:
                        st2.consumed(tt * 16 + dc * 2, s_t, s_t.n)
                tmv = s_t.n
                st2.consumed(tt * 16 + dc * 2 + 1, s_t, tmv)
                vv = B.E("v", lambda e, b=bank, dc=dc: e.tensor_scalar(
                    ev, ps[b][:, 0:TT], 0.5, Pc("hb2", dc), ALU.mult, ALU.add),
                    waits=[(s_t, tmv)], inc=s_v)
                guard(bank, s_v, vv)
                B.dr("v")
                if not final:
                    B.E("v", lambda e, dc=dc, tt=tt: e.tensor_tensor(
                        x_sl(dc, tt), ev, x_sl(dc, tt), ALU.add), inc=s_v)
                else:
                    B.E("v", lambda e, dc=dc, tt=tt: e.tensor_tensor(
                        ost[:, (dc % 2) * TT:(dc % 2) * TT + TT], ev, x_sl(dc, tt), ALU.add),
                        waits=([og] if dc % 2 == 0 else []), inc=s_v)
                    if dc % 2 == 1:
                        yd(xo_r[:, dc - 1:dc + 1, tt * TT:(tt + 1) * TT],
                           ost.rearrange("p (c t) -> p c t", c=2),
                           waits=[(s_v, s_v.n)], inc=y_out)
                        og = (y_out, y_out.n)
                B.dr("v")

    def mamba_stage():
        a16 = Bump(NA16 - 1024)
        o = a16.take(NC8 * TT); u = A16[:, o:o + NC8 * TT]
        o = a16.take(NC8 * TT); uln = A16[:, o:o + NC8 * TT]
        o = a16.take(NC16 * TT); zsilu = A16[:, o:o + NC16 * TT]
        o = a16.take(NC18 * XBCW); xbc0 = o; OFFS["xbc"] = o
        xbc = [A16[:, xbc0 + c * XBCW:xbc0 + (c + 1) * XBCW] for c in range(NC18)]
        o = a16.take(NC18 * 3); sbar = A16[:, o:o + NC18 * 3]
        o = a16.take(NC16 * TT); yn = A16[:, o:o + NC16 * TT]; OFFS["yn"] = o
        o = a16.take(256); cumhi16 = A16[0:32, o:o + 128]
        expcum16 = A16[0:32, o + 128:o + 256]
        o = a16.take(128); cumlo16 = A16[0:32, o:o + 128]
        hiP = [cumhi16, hl2[0:32, 0:128]]
        expP16 = [expcum16, hl2[0:32, 128:256]]
        loP = [cumlo16, hl2[0:32, 256:384]]
        o = a16.take(2048); yq = A16[:, o:o + 2048]; OFFS["yq"] = o

        o = a16.take(32); nthi16 = A16[:, o:o + 32]
        o = a16.take(32); ntlo16 = A16[:, o:o + 32]
        o = a16.take(64); btm16 = A16[:, o:o + 64]
        o = a16.take(256); G16P = A16[:, o:o + 256]
        G16 = [A16[:, o:o + 128], A16[:, o + 128:o + 256]]
        Cd16 = []
        for _ in range(2):
            o = a16.take(128); Cd16.append(A16[0:64, o:o + 128])
        dxp = []
        for _ in range(2):
            o = a16.take(128); dxp.append(A16[:, o:o + 128])
        dxw = []
        for _ in range(2):
            o = a16.take(128); dxw.append(A16[:, o:o + 128])
        sqq = []
        for _ in range(2):
            o = a16.take(128); sqq.append(A16[:, o:o + 128])

        a32 = Bump(LN_BASE, base=256)
        cvp = []
        for _ in range(2):
            o = a32.take(512); cvp.append(A32[:, o:o + 512])
        o = a32.take(512); dtb = A32[0:32, o:o + 512]
        o = a32.take(512); dtraw = A32[0:32, o:o + 512]
        o = a32.take(128); acum = A32[0:32, o:o + 128]
        o = a32.take(128); cum = A32[0:32, o:o + 128]
        o = a32.take(128); rvi = A32[0:32, o:o + 128]
        o = a32.take(128); rve = A32[0:32, o:o + 128]
        o = a32.take(32); nctm = A32[:, o:o + 32]
        o = a32.take(32); wdtm = A32[:, o:o + 32]
        o = a32.take(32); dttm = A32[:, o:o + 32]
        o = a32.take(32); dAt = A32[0:64, o:o + 32]
        o = a32.take(256); spm2 = A32[:, o:o + 256]
        spm = A32[:, o:o + 128]
        spmB = A32[:, o + 128:o + 256]
        e1 = []
        for _ in range(2):
            o = a32.take(128); e1.append(A32[:, o:o + 128])
        o = a32.take(256); DexP = A32[:, o:o + 256]
        Dex = [A32[:, o:o + 128], A32[:, o + 128:o + 256]]
        o = a32.take(128); rstd2 = A32[:, o:o + 128]

        stip = Stream(wk8, y_k8, "ip", [(s_t, s_t.n)])
        stop_ = Stream(wk32, y_k32, "op", [(s_t, s_t.n)])
        B.W("t", [(s_v, s_v.n), (s_s, s_s.n), (g_init, GI0), (y_c, YC0)] + list(CASTS_DONE))
        B.W("v", list(CASTS_DONE))
        B.W("s", list(CASTS_DONE) + [(g_init, GI0), (y_c, YC0)])

        B.E("v", lambda e: e.memset(yn[:, :], 0.0), inc=s_v)
        B.dr("v")
        nunit_ip = 18
        vstate_prev = {p: VP0 for p in range(16)}
        rms_prev = 0
        sv4_prev = 0
        for sc in range(NTT):
            seq_start = (sc % 4 == 0)
            if sc > 0:
                B.W("v", [(y_vw, y_vw.n)])
            uln_ap = emit_ln(lambda c, sc=sc: x_sl(c, sc), lambda c: u[:, c * TT:(c + 1) * TT],
                             NC8, TT, also_ln2=(lambda c: uln[:, c * TT:(c + 1) * TT], "ln1w", "ln1b"),
                             src_ready=xtile_ready[sc])
            vuln = s_v.n
            if seq_start:
                B.E("v", lambda e: e.memset(sbar[:, :], 0.0), inc=s_v)
                B.dr("v")
            for c in range(NC18):
                B.E("v", lambda e, c=c: e.tensor_copy(
                    xbc[c][:, 0:3], sbar[:, c * 3:c * 3 + 3]), inc=s_v)
            B.dr("v")
            for ui in range(nunit_ip):
                w = 2048 if ui < 17 else 1024
                stip.queue(ui * 2048, w)
            ip_evac = {}
            dt_done = None
            for oc in range(35):
                ui = oc // 2
                sl, (sem, dv) = stip.unit(sc * nunit_ip + ui)
                boff = (oc % 2) * 1024
                bank = oc % 4
                for kc in range(8):
                    w0 = [(s_v, uln_ap[kc])]
                    if kc == 0:
                        w0.append((sem, dv))
                    t_mm(bank, ps[bank][:, 0:TT],
                         wk8[sl][:, boff + kc * 128:boff + kc * 128 + 128],
                         uln[:, kc * TT:(kc + 1) * TT], start=(kc == 0), stop=(kc == 7),
                         waits=w0, inc=(s_t if kc == 7 else None))
                tmv = s_t.n
                if oc % 2 == 1 or oc == 34:
                    stip.consumed(sc * nunit_ip + ui, s_t, tmv)
                if oc < 16:
                    sv = B.E("s", lambda e, b=bank, oc=oc: e.activation(
                        zsilu[:, oc * TT:(oc + 1) * TT], ps[b][:, 0:TT], AF.Silu),
                        waits=[(s_t, tmv)], inc=s_s)
                elif oc < 34:
                    c = oc - 16
                    sv = B.E("s", lambda e, b=bank, c=c: e.activation(
                        xbc[c][:, 3:XBCW], ps[b][:, 0:TT], AF.Copy),
                        waits=[(s_t, tmv)], inc=s_s)
                    ip_evac[c] = sv
                else:
                    vv = B.E("v", lambda e, b=bank: e.tensor_scalar(
                        dtraw, ps[b][0:32, 0:TT], dtbias, 80.0, ALU.add, ALU.min),
                        waits=[(s_t, tmv)], inc=s_v)
                    sv = B.E("s", lambda e: e.activation(dtraw, dtraw, AF.Exp),
                             waits=[(s_v, vv)], inc=s_s)
                    B.dr("s")
                    sv = B.E("s", lambda e: e.activation(dtb, dtraw, AF.Ln, bias=oneb[0:32, 0:1]),
                             inc=s_s)
                    dt_done = sv
                    guard(bank, s_v, vv)
                if oc < 34:
                    guard(bank, s_s, sv)
            conv_done = {}
            for c in range(NC18):
                cva = cvp[c % 2]
                w0 = [(s_s, ip_evac[c])]
                if c >= 2:
                    w0.append((s_s, conv_done[c - 2]))
                B.E("v", lambda e, c=c, cva=cva: e.tensor_scalar(
                    cva, xbc[c][:, 0:512], Pc("cw", c * 4 + 0), None, ALU.mult),
                    waits=w0, inc=s_v)
                for k in range(1, 4):
                    B.dr("v")
                    B.E("v", lambda e, c=c, k=k, cva=cva: e.scalar_tensor_tensor(
                        cva, xbc[c][:, k:k + 512], Pc("cw", c * 4 + k), cva,
                        ALU.mult, ALU.add), inc=s_v)
                B.dr("v")
                vv2 = B.E("v", lambda e, c=c: e.tensor_copy(
                    sbar[:, c * 3:c * 3 + 3], xbc[c][:, 512:515]), inc=s_v)
                sv = B.E("s", lambda e, c=c, cva=cva: e.activation(
                    xbc[c][:, 3:XBCW], cva, AF.Silu, bias=Pc("cb", c)),
                    waits=[(s_v, vv2)], inc=s_s)
                conv_done[c] = sv
            xs_done = [conv_done[c] for c in range(16)]
            bc_done = max(conv_done[16], conv_done[17])

            def emit_prefix(qq, w_first):
                IB = qq % 2
                B.E("v", lambda e, qq=qq: e.tensor_scalar(
                    acum, dtb[:, qq * 128:(qq + 1) * 128], Aneg, None, ALU.mult),
                    waits=w_first, inc=s_v)
                B.dr("v")
                B.E("v", lambda e: e.tensor_tensor_scan(
                    cum, acum, zer32[:, :], 0.0, ALU.add, ALU.add), inc=s_v)
                B.dr("v")
                B.E("v", lambda e: e.tensor_tensor_scan(
                    rvi[:, 127::-1], acum[:, 127::-1], zer32[:, :], 0.0, ALU.add, ALU.add), inc=s_v)
                B.dr("v")
                B.E("v", lambda e: e.tensor_tensor(rve, rvi, acum, ALU.subtract), inc=s_v)
                B.dr("v")
                B.E("v", lambda e, IB=IB: e.tensor_copy(hiP[IB], cum), inc=s_v)
                B.dr("v")
                vs = B.E("v", lambda e, IB=IB: e.tensor_tensor(
                    loP[IB], cum, hiP[IB], ALU.subtract), inc=s_v)
                B.dr("v")
                return vs

            vset_ms = {}
            vset_ms[0] = emit_prefix(0, [(s_s, dt_done), (s_t, s_t.n), (s_s, sv4_prev)])
            vdx_prev = {}
            for q in range(DBG_NQ[0]):
                if "dumph" in DBG_SKIP and sc == 0:
                    gd(dbg[0:64, q * 2048:q * 2048 + 2048], hstate[:, :],
                       waits=[(s_v, s_v.n), (s_t, s_t.n)], inc=g_init)
                    B.W("v", [(g_init, g_init.n)])
                    B.W("t", [(g_init, g_init.n)])
                if "dumpyn01" in DBG_SKIP and sc == 0 and q == min(2, DBG_NQ[0] - 1) and False:
                    gd(dbg[:, 0:8192], yn[:, :], waits=[(s_v, s_v.n)], inc=g_init)
                    B.W("v", [(g_init, g_init.n)])
                    B.W("s", [(g_init, g_init.n)])
                if "qfence" in DBG_SKIP:
                    B.W("t", [(s_v, s_v.n), (s_s, s_s.n)])
                    B.W("v", [(s_t, s_t.n), (s_s, s_s.n)])
                    B.W("s", [(s_t, s_t.n), (s_v, s_v.n)])
                qo = 3 + q * 128
                vsetup = vset_ms[q]
                t_tr(2, ps[2][:, 0:32], cum, iden32[0:32, 0:32], waits=[(s_v, vsetup)], inc=s_t)
                t_tr(2, ps[2][:, 32:64], rve, iden32[0:32, 0:32], inc=s_t)
                tma = t_tr(2, ps[2][:, 64:96], dtb[:, q * 128:(q + 1) * 128], iden32[0:32, 0:32], inc=s_t)
                w0 = [(s_s, bc_done)]
                if vdx_prev.get(14) is not None:
                    w0.append((s_v, vdx_prev[14]))
                tmb = t_tr(6, p16[0][:, 256:320], xbc[16][0:64, qo:qo + 128], iden16[0:64, 0:64],
                           waits=w0, inc=s_t)
                sv1 = B.E("s", lambda e: e.activation(nctm, ps[2][:, 0:32], AF.Copy),
                          waits=[(s_t, tma)], inc=s_s)
                sv2 = B.E("s", lambda e: e.activation(wdtm, ps[2][:, 32:64], AF.Exp), inc=s_s)
                sv3 = B.E("s", lambda e: e.activation(dttm, ps[2][:, 64:96], AF.Copy), inc=s_s)
                sv4 = B.E("s", lambda e, q=q: e.activation(expP16[q % 2], cum, AF.Exp),
                          waits=[(s_v, vsetup)], inc=s_s)
                vbt = B.E("v", lambda e: e.tensor_copy(btm16, p16[0][:, 256:320]),
                          waits=[(s_t, tmb)], inc=s_v)
                guard(6, s_v, vbt)
                guard(2, s_s, sv3)
                tms = t_mm(2, ps[2][:, 128:256], xbc[16][0:64, qo:qo + 128],
                           xbc[17][0:64, qo:qo + 128], start=True, stop=True,
                           waits=[(s_s, bc_done)], inc=s_t)
                B.E("v", lambda e: e.tensor_tensor(spm, ps[2][:, 128:256], tri32[:, :], ALU.mult),
                    waits=[(s_t, tms)], inc=s_v)
                B.dr("v")
                vsp = B.E("v", lambda e: e.tensor_tensor(spmB, ps[2][:, 128:256], tri32[:, :], ALU.mult),
                          inc=s_v)
                B.dr("v")
                vv = B.E("v", lambda e: e.tensor_copy(nthi16, nctm),
                         waits=[(s_s, sv1)], inc=s_v)
                B.dr("v")
                vnt = B.E("v", lambda e: e.tensor_tensor(ntlo16, nctm, nthi16, ALU.subtract), inc=s_v)
                guard(2, s_v, vsp)
                t_mm(2, ps[2][0:64, 256:288], sel16[:, :], nthi16,
                     start=True, stop=False, waits=[(s_v, vnt)])
                tmd = t_mm(2, ps[2][0:64, 256:288], sel16[:, :], ntlo16,
                           start=False, stop=True, inc=s_t)
                svA = B.E("s", lambda e: e.activation(dAt, ps[2][0:64, 256:288], AF.Exp),
                          waits=[(s_t, tmd)], inc=s_s)
                guard(2, s_s, svA)

                tslot = {}
                tD = {}
                tTR = {}
                tY = {}
                tH = {}
                vG = {}
                vCd = {}
                vdx = {}
                vdxw = {}
                vevac = {}
                vstate = {}
                sD = {}
                ve1 = {}
                for s in range(34):
                    serR = any(f.startswith("serR") and
                               int(f[4:].split("_")[0]) <= s < int(f[4:].split("_")[1])
                               for f in DBG_SKIP)
                    if "serial" in DBG_SKIP or serR:
                        B.W("t", [(s_v, s_v.n), (s_s, s_s.n)])
                        B.W("v", [(s_t, s_t.n), (s_s, s_s.n)])
                        B.W("s", [(s_t, s_t.n), (s_v, s_v.n)])
                    if "sert" in DBG_SKIP:
                        B.W("t", [(s_s, s_s.n)])
                    if "servt" in DBG_SKIP:
                        B.W("t", [(s_v, s_v.n)])
                    if "serv" in DBG_SKIP:
                        B.W("v", [(s_t, s_t.n), (s_s, s_s.n)])
                    if "servT" in DBG_SKIP:
                        B.W("v", [(s_t, s_t.n)])
                    if "servS" in DBG_SKIP:
                        B.W("v", [(s_s, s_s.n)])
                    if "sers" in DBG_SKIP:
                        B.W("s", [(s_t, s_t.n), (s_v, s_v.n)])
                    if s < 32:
                        bk = s % 2
                        w0 = [(s_v, vsetup), (s_s, sv4)]
                        if s >= 2:
                            w0.append((s_v, vCd[s - 2]))
                            w0.append((s_s, sD[s - 2]))
                            if "srelu" not in DBG_SKIP:
                                w0.append((s_v, ve1[s - 2]))
                        t_mm(bk, ps[bk][:, 0:128], ind16[:, s * 128:(s + 1) * 128],
                             hiP[q % 2], start=True, stop=False, waits=w0)
                        t_mm(bk, ps[bk][:, 0:128], ind16[:, s * 128:(s + 1) * 128],
                             loP[q % 2], start=False, stop=True)
                        t_mm(bk, ps[bk][:, 128:256], ind16[:, s * 128:(s + 1) * 128],
                             expP16[q % 2], start=True, stop=True, inc=s_t)
                        tD[s] = s_t.n
                    if s % 2 == 0 and s < 32:
                        p = s // 2
                        pbk = 6 + p % 2
                        w0 = [(s_s, xs_done[p])]
                        if p >= 2:
                            w0.append((s_v, vdx[p - 2]))
                        elif vdx_prev.get(p + 14) is not None:
                            w0.append((s_v, vdx_prev[p + 14]))
                        t_tr(pbk, p16[p % 2][:, 0:128], xbc[p][:, qo:qo + 128],
                             iden16[:, :], waits=w0, inc=s_t)
                        tTR[p] = s_t.n
                    if s >= 2 and (s - 2) % 2 == 0 and s - 2 < 32:
                        vv = B.E("v", lambda e: e.tensor_tensor(
                            G16P, DexP, spm2, ALU.mult),
                            waits=[(s_s, sD[s - 1]), (s_v, vsp)], inc=s_v)
                        vG[s - 2] = vG[s - 1] = vv
                    if s >= 2:
                        hh = s - 2
                        ybk = 4 + (hh // 2) % 2
                        rows = slice(64, 128) if hh % 2 == 1 else slice(0, 64)
                        tp = (0, 64) if hh % 2 == 1 else None
                        w0 = [(s_v, vG[hh])]
                        if hh >= 4:
                            w0.append((s_v, vevac[hh - 4]))
                        skip2 = "y2" in DBG_SKIP
                        skip1 = "y1" in DBG_SKIP
                        if skip1:
                            B.W("t", w0)
                            B.E("t", lambda e, bk2=ybk, rows2=rows, tp2=tp: e.matmul(
                                ps[bk2][rows2, 0:128],
                                zer16h[:, 0:64], zer16[0:64, 0:128],
                                start=True, stop=False, tile_position=tp2))
                        else:
                            t_mm(ybk, ps[ybk][rows, 0:128],
                                 dxp[(hh // 2) % 2][:, (hh % 2) * 64:(hh % 2) * 64 + 64],
                                 G16[hh % 2][:, :], start=True, stop=skip2,
                                 waits=w0, tile_position=tp)
                        if skip2:
                            w1 = [(s_v, vCd[hh]), (s_v, vstate_prev[hh // 2])]
                            t_mm(ybk, ps[ybk][rows, 0:128], hstate[:, hh * 64:(hh + 1) * 64],
                                 Cd16[hh % 2][:, :], start=False, stop=True,
                                 waits=w1, inc=s_t, tile_position=tp) if False else None
                            B.W("t", w1)
                            s_t.n += 0
                            tdummy = B.E("t", lambda e, bk2=ybk, rows2=rows, tp2=tp: e.matmul(
                                ps[bk2][rows2, 256:384], zer16h[:, 0:64], zer16[0:64, 0:128],
                                start=True, stop=True, tile_position=tp2), inc=s_t)
                        else:
                            w1 = [(s_v, vCd[hh]), (s_v, vstate_prev[hh // 2])]
                            t_mm(ybk, ps[ybk][rows, 0:128], hstate[:, hh * 64:(hh + 1) * 64],
                                 Cd16[hh % 2][:, :], start=False, stop=True,
                                 waits=w1, inc=s_t, tile_position=tp)
                        tY[hh] = s_t.n
                        if hh % 2 == 1:
                            p = hh // 2
                            hbk = 2 + p % 2
                            t_mm(hbk, ps[hbk][0:64, 384:512], btm16[:, :],
                                 dxw[p % 2][:, :], start=True, stop=True,
                                 waits=[(s_v, vdxw[p])], inc=s_t)
                            tH[p] = s_t.n
                    tslot[s] = s_t.n

                    if s < 32 and "srelu" not in DBG_SKIP:
                        bk = s % 2
                        w0 = [(s_t, tD[s]), (s_s, sv1)]
                        if s >= 2:
                            w0.append((s_s, sD[s - 2]))
                            w0.append((s_v, vG[s - 2]))
                        ve1[s] = B.E("v", lambda e, bk=bk, s=s: e.tensor_scalar(
                            e1[s % 2], ps[bk][:, 0:128], nctm[:, s:s + 1], 0.0,
                            ALU.subtract, ALU.min), waits=w0, inc=s_v)
                        B.dr("v")
                    if s < 32:
                        bk = s % 2
                        w0 = [(s_t, tD[s])]
                        if s >= 2:
                            w0.append((s_t, tY[s - 2]))
                        vCd[s] = B.E("v", lambda e, bk=bk, s=s, qo=qo: e.tensor_tensor(
                            Cd16[s % 2], xbc[17][0:64, qo:qo + 128], ps[bk][0:64, 128:256],
                            ALU.mult), waits=w0, inc=s_v)
                    if s % 2 == 0 and s < 32:
                        p = s // 2
                        w0 = [(s_t, tTR[p]), (s_s, sv3)]
                        if p >= 2:
                            w0.append((s_t, tY[2 * p - 3]))
                        for par in range(2):
                            vdx0 = B.E("v", lambda e, p=p, par=par: e.tensor_scalar(
                                dxp[p % 2][:, par * 64:par * 64 + 64],
                                p16[p % 2][:, par * 64:par * 64 + 64],
                                dttm[:, 2 * p + par:2 * p + par + 1], None, ALU.mult),
                                waits=(w0 if par == 0 else []), inc=s_v)
                        vdx[p] = vdx0
                        B.dr("v")
                        w0 = [(s_s, sv2)]
                        if p >= 2:
                            w0.append((s_t, tH[p - 2]))
                        for par in range(2):
                            vdxw0 = B.E("v", lambda e, p=p, par=par: e.tensor_scalar(
                                dxw[p % 2][:, par * 64:par * 64 + 64],
                                dxp[p % 2][:, par * 64:par * 64 + 64],
                                wdtm[:, 2 * p + par:2 * p + par + 1], None, ALU.mult),
                                waits=(w0 if par == 0 else []), inc=s_v)
                        vdxw[p] = vdxw0
                        B.dr("v")
                    if s >= 2:
                        hh = s - 2
                        if hh % 2 == 1:
                            p2 = hh // 2
                            ybk2 = 4 + p2 % 2
                            vv = B.E("v", lambda e, p2=p2, ybk2=ybk2, qo=qo:
                                     e.scalar_tensor_tensor(
                                yq[:, p2 * 128:p2 * 128 + 128], xbc[p2][:, qo:qo + 128],
                                P("DrepP", 16)[:, p2:p2 + 1], ps[ybk2][:, 0:128],
                                ALU.mult, ALU.add), waits=[(s_t, tY[hh])], inc=s_v)
                            vevac[2 * p2] = vevac[2 * p2 + 1] = vv
                            guard(ybk2, s_v, vv)
                        if hh % 2 == 1:
                            p = hh // 2
                            hbk = 2 + p % 2
                            vst = None
                            for par in range(2):
                                vst = B.E("v", lambda e, p=p, par=par, hbk=hbk:
                                          e.scalar_tensor_tensor(
                                    hstate[:, (2 * p + par) * 64:(2 * p + par) * 64 + 64],
                                    hstate[:, (2 * p + par) * 64:(2 * p + par) * 64 + 64],
                                    dAt[:, 2 * p + par:2 * p + par + 1],
                                    ps[hbk][0:64, 384 + par * 64:384 + par * 64 + 64],
                                    ALU.mult, ALU.add),
                                    waits=([(s_t, tH[p]), (s_s, svA)] if par == 0 else []),
                                    inc=s_v)
                                B.dr("v")
                            vstate[p] = vst
                            guard(hbk, s_v, vst)

                    if s == 26 and q + 1 < DBG_NQ[0]:
                        vset_ms[q + 1] = emit_prefix(
                            q + 1, [(s_s, dt_done), (s_t, tD[0]), (s_s, sv4)])
                    if s < 32 and "srelu" not in DBG_SKIP:
                        sD[s] = B.E("s", lambda e, s=s: e.activation(
                            Dex[s % 2], e1[s % 2], AF.Exp),
                            waits=[(s_v, ve1[s])], inc=s_s)
                    elif s < 32:
                        bk = s % 2
                        w0 = [(s_t, tslot[s]), (s_s, sv1)]
                        if s >= 2:
                            w0.append((s_v, vG[s - 2]))
                        B.E("s", lambda e, bk=bk, s=s: e.activation(
                            e1[s % 2], ps[bk][:, 0:128], AF.Relu, scale=-1.0,
                            bias=nctm[:, s:s + 1]), waits=w0, inc=s_s)
                        B.dr("s")
                        sD[s] = B.E("s", lambda e, s=s: e.activation(
                            Dex[s % 2], e1[s % 2], AF.Exp, scale=-1.0), inc=s_s)
                vstate_prev = dict(vstate)
                vdx_prev = {14: vdx[14], 15: vdx[15]}
                guard(0, s_v, vCd[30])
                guard(1, s_v, vCd[31])
                guard(0, s_s, sD[30])
                guard(1, s_s, sD[31])
                if "srelu" not in DBG_SKIP:
                    guard(0, s_v, ve1[30])
                    guard(1, s_v, ve1[31])

                if "dumpyn" in DBG_SKIP and q == 0 and sc == NTT - 1:
                    gd(dbg[:, 10240:10240 + 2048], yq[:, :], waits=[(s_v, s_v.n)], inc=g_init)
                    B.W("v", [(g_init, g_init.n)])
                B.dr("v")
                vgate = {}
                for c in range(NC16):
                    vgate[c] = B.E("v", lambda e, c=c, q=q: e.tensor_tensor(
                        yq[:, c * 128:(c + 1) * 128], yq[:, c * 128:(c + 1) * 128],
                        zsilu[:, c * TT + q * 128:c * TT + q * 128 + 128], ALU.mult),
                        waits=([(s_v, vevac[30]), (s_v, vevac[31])] if c == 0 else []),
                        inc=s_v)
                    B.dr("v")
                if "dumpyn" in DBG_SKIP and q == 0 and sc == NTT - 1:
                    gd(dbg[:, 12288:12288 + 2048], yq[:, :], waits=[(s_v, s_v.n)], inc=g_init)
                    B.W("v", [(g_init, g_init.n)])
                    B.W("s", [(g_init, g_init.n)])
                rms_mm = []
                for c in range(NC16):
                    sl = c % 2
                    w0 = [(s_v, vgate[c])]
                    if c >= 2:
                        w0.append((s_t, rms_mm[c - 2]))
                    elif rms_prev:
                        w0.append((s_t, rms_prev))
                    sv = B.E("s", lambda e, c=c, sl=sl: e.activation(
                        sqq[sl], yq[:, c * 128:(c + 1) * 128], AF.Square),
                        waits=w0, inc=s_s)
                    rmsm = t_mm(0, ps[0][:, 0:128], ones16[:, :], sqq[sl],
                                start=(c == 0), stop=(c == NC16 - 1),
                                waits=[(s_s, sv)], inc=s_t)
                    rms_mm.append(rmsm)
                rms_prev = rms_mm[-1]
                sv = B.E("s", lambda e: e.activation(rstd2, ps[0][:, 0:128], AF.Ln,
                                                     scale=1.0 / DI, bias=epsb[:, 0:1]),
                         waits=[(s_t, rms_mm[-1])], inc=s_s)
                B.dr("s")
                sv = B.E("s", lambda e: e.activation(rstd2, rstd2, AF.Exp, scale=-0.5), inc=s_s)
                guard(0, s_s, sv)
                for c in range(NC16):
                    B.E("v", lambda e, c=c, q=q: e.scalar_tensor_tensor(
                        yn[:, c * TT + q * 128:c * TT + q * 128 + 128],
                        yq[:, c * 128:(c + 1) * 128], Pc("nw", c), rstd2,
                        ALU.mult, ALU.mult), waits=([(s_s, sv)] if c == 0 else []), inc=s_v)
                B.dr("v")
                if "dumpyn" in DBG_SKIP and q == 0 and sc == NTT - 1:
                    gd(dbg[:, 0:8192], yn[:, :], waits=[(s_v, s_v.n)], inc=g_init)
                    gd(dbg[:, 8192:8192 + 2048], yq[:, :], waits=[(s_v, s_v.n)], inc=g_init)
                    B.W("v", [(g_init, g_init.n)])
                    B.W("s", [(g_init, g_init.n)])
                if "dumpyn01" in DBG_SKIP and sc == 0 and q == 1:
                    gd(dbg[:, 0:8192], yn[:, :], waits=[(s_v, s_v.n)], inc=g_init)
                    B.W("v", [(g_init, g_init.n)])
                    B.W("s", [(g_init, g_init.n)])
                    B.W("t", [(g_init, g_init.n)])
            vyn = s_v.n
            sv4_prev = sv4

            for dc in range(NC8):
                stop_.queue(dc * 2048, 2048)
            for dc in range(NC8):
                sl, (sem, dv) = stop_.unit(sc * NC8 + dc)
                bank = dc % 4
                for kc in range(16):
                    w0 = []
                    if kc == 0:
                        w0 = [(sem, dv), (s_v, vyn)]
                    t_mm(bank, ps[bank][:, 0:TT],
                         wk32[sl][:, kc * 128:(kc + 1) * 128],
                         yn[:, kc * TT:(kc + 1) * TT], start=(kc == 0), stop=(kc == 15),
                         waits=w0, inc=(s_t if kc == 15 else None))
                tmv = s_t.n
                stop_.consumed(sc * NC8 + dc, s_t, tmv)
                vv = B.E("v", lambda e, b=bank, dc=dc: e.tensor_tensor(
                    u[:, dc * TT:(dc + 1) * TT], ps[b][:, 0:TT],
                    u[:, dc * TT:(dc + 1) * TT], ALU.add),
                    waits=[(s_t, tmv)], inc=s_v)
                guard(bank, s_v, vv)
                B.dr("v")
            vb_r = vbuf.rearrange("p (c t) -> p c t", c=NC8)
            yd(vb_r[:, :, sc * TT:(sc + 1) * TT],
               u.rearrange("p (c t) -> p c t", c=NC8),
               waits=[(s_v, s_v.n)], inc=y_vw)
            if sc == 3:
                vrst = B.E("v", lambda e: e.memset(hstate[:, :], 0.0),
                           waits=[(s_t, s_t.n)], inc=s_v)
                B.dr("v")
                vstate_prev = {p: vrst for p in range(16)}

    def mlp_stage():
        a16 = Bump(NA16 - 1024)
        vt = []
        for _ in range(2):
            o = a16.take(NC8 * TT); vt.append(A16[:, o:o + NC8 * TT])
        h2b = []
        for _ in range(2):
            o = a16.take(NC8 * TT); h2b.append(A16[:, o:o + NC8 * TT])
        act = []
        for _ in range(NC32):
            o = a16.take(TT)
            act.append(A16[:, o:o + TT])
        a32 = Bump(LN_BASE - 512, base=256)
        o = a32.take(TT); ev = A32[:, o:o + TT]

        st1 = Stream(wk8, y_k8, "m1", [(s_t, s_t.n)])
        st2 = Stream(wk32, y_k32, "m2", [(s_t, s_t.n)])
        vb_r = vbuf.rearrange("p (c t) -> p c t", c=NC8)
        vwall = y_vw.n
        vt_cons = {0: None, 1: None}
        vsem = [y_vr, y_vr2]
        vload = {}
        for tt in range(2):
            w0 = [(y_vw, vwall)]
            vload[tt] = (vsem[tt % 2], yd(vt[tt % 2].rearrange("p (c t) -> p c t", c=NC8),
                         vb_r[:, :, tt * TT:(tt + 1) * TT], waits=w0, inc=vsem[tt % 2]))
        B.W("v", [vload[0]])
        B.W("s", [vload[0]])
        B.W("t", [vload[0], (s_v, s_v.n), (s_s, s_s.n)])
        vln = {}
        hid_end = {}
        emit_ln(lambda c: vt[0][:, c * TT:(c + 1) * TT],
                lambda c: h2b[0][:, c * TT:(c + 1) * TT], NC8, TT, aff=("ln2w", "ln2b"),
                src_ready=vload[0])
        vln[0] = s_v.n
        for tt in range(NTT):
            sl = tt % 2
            h2 = h2b[sl]
            for gi in range(16):
                st1.queue(gi * 2048, 2048)
            for fc in range(NC32):
                gi = fc // 2
                slw, (sem, dv) = st1.unit(tt * 16 + gi)
                boff = (fc % 2) * 1024
                bank = fc % 4
                for kc in range(8):
                    w0 = []
                    if kc == 0:
                        w0 = [(sem, dv), (s_v, vln[tt])]
                    t_mm(bank, ps[bank][:, 0:TT],
                         wk8[slw][:, boff + kc * 128:boff + kc * 128 + 128],
                         h2[:, kc * TT:(kc + 1) * TT], start=(kc == 0), stop=(kc == 7),
                         waits=w0, inc=(s_t if kc == 7 else None))
                tmv = s_t.n
                if fc % 2 == 1:
                    st1.consumed(tt * 16 + gi, s_t, tmv)
                sv = B.E("s", lambda e, b=bank, fc=fc: e.activation(
                    act[fc][:, :], ps[b][:, 0:TT], AF.Gelu, bias=Pc("mb1", fc)),
                    waits=[(s_t, tmv)], inc=s_s)
                guard(bank, s_s, sv)
            vact = s_s.n
            hid_end[tt] = s_t.n
            if tt + 1 < NTT:
                emit_ln(lambda c, tt=tt: vt[(tt + 1) % 2][:, c * TT:(c + 1) * TT],
                        lambda c, tt=tt: h2b[(tt + 1) % 2][:, c * TT:(c + 1) * TT],
                        NC8, TT, aff=("ln2w", "ln2b"),
                        src_ready=vload[tt + 1],
                        out_guard=(s_t, hid_end.get(tt - 1, 0)))
                vln[tt + 1] = s_v.n
            for hdc in range(16):
                st2.queue(hdc * 2048, 2048)
            for dc in range(NC8):
                bank = dc % 4
                for kc in range(32):
                    hdc = dc * 2 + kc // 16
                    slw, (sem, dv) = st2.unit(tt * 16 + hdc)
                    w0 = []
                    if kc % 16 == 0:
                        w0 = [(sem, dv)]
                    if kc == 0:
                        w0 = w0 + [(s_s, vact)]
                    t_mm(bank, ps[bank][:, 0:TT],
                         wk32[slw][:, (kc % 16) * 128:(kc % 16) * 128 + 128],
                         act[kc][:, :], start=(kc == 0), stop=(kc == 31),
                         waits=w0, inc=(s_t if kc in (15, 31) else None))
                    if kc == 15:
                        st2.consumed(tt * 16 + dc * 2, s_t, s_t.n)
                tmv = s_t.n
                st2.consumed(tt * 16 + dc * 2 + 1, s_t, tmv)
                vv = B.E("v", lambda e, b=bank, dc=dc: e.tensor_scalar(
                    ev, ps[b][:, 0:TT], Pc("mb2", dc), None, ALU.add),
                    waits=[(s_t, tmv)], inc=s_v)
                guard(bank, s_v, vv)
                B.dr("v")
                B.E("v", lambda e, dc=dc, sl=sl: e.tensor_tensor(
                    ev, ev, vt[sl][:, dc * TT:(dc + 1) * TT], ALU.add), inc=s_v)
                B.dr("v")
                B.E("v", lambda e, dc=dc, tt=tt: e.tensor_tensor(
                    x_sl(dc, tt), ev, x_sl(dc, tt), ALU.add), inc=s_v)
                B.dr("v")
            vt_cons[sl] = (s_v, s_v.n)
            if tt + 2 < NTT:
                vload[tt + 2] = (vsem[sl], yd(vt[sl].rearrange("p (c t) -> p c t", c=NC8),
                                 vb_r[:, :, (tt + 2) * TT:(tt + 3) * TT],
                                 waits=[(y_vw, vwall), vt_cons[sl]], inc=vsem[sl]))

    if "ffn1" in stages:
        ffn_stage("ffn1", "ffn2", final=False)
    if "mamba" in stages:
        mamba_stage()
    if "mlp" in stages:
        mlp_stage()
    if "ffn2" in stages:
        ffn_stage("ffn1", "ffn2", final=True)
        B.W("y", [(y_out, y_out.n)])
    else:
        B.W("g", [(s_v, s_v.n), (s_t, s_t.n), (s_s, s_s.n)])
        for c in range(NC8):
            gd(xout[:, c * T:(c + 1) * T], x[:, c * T:(c + 1) * T], inc=g_init)
        B.W("y", [(g_init, g_init.n)])

    from contextlib import ExitStack
    with ExitStack() as ctx:
        blk = ctx.enter_context(nc.Block())
        B.finalize(blk)
    return nc


def make_inputs(I):
    ones16, iden, tri, ind, sel127 = build_consts()
    Wt = np.ascontiguousarray(np.asarray(I["L0_attn_in_proj_w"], np.float32).T)
    Wpad = np.zeros((D, DIPP), np.float32)
    Wpad[:, 0:4096] = Wt[:, 0:4096]
    Wpad[:, 32 * 128:32 * 128 + 64] = Wt[:, 4096:4160]
    Wpad[:, 33 * 128:33 * 128 + 64] = Wt[:, 4160:4224]
    Wpad[:, 34 * 128:34 * 128 + 32] = Wt[:, 4224:4256]
    cw = np.asarray(I["L0_attn_conv_w"], np.float32)
    cw_pad = np.zeros((18 * 128, 4), np.float32)
    cw_pad[0:2048] = cw[0:2048]
    cw_pad[16 * 128:16 * 128 + 64] = cw[2048:2112]
    cw_pad[17 * 128:17 * 128 + 64] = cw[2112:2176]
    cb = np.asarray(I["L0_attn_conv_b"], np.float32)
    cb_pad = np.zeros((18 * 128,), np.float32)
    cb_pad[0:2048] = cb[0:2048]
    cb_pad[16 * 128:16 * 128 + 64] = cb[2048:2112]
    cb_pad[17 * 128:17 * 128 + 64] = cb[2112:2176]
    common = {
        "wffn1": pack_w(np.asarray(I["L0_ffn_W1"], np.float32)),
        "wffn2": pack_w(np.asarray(I["L0_ffn_W2"], np.float32)),
        "wip": pack_w(Wpad),
        "wop": pack_w(np.ascontiguousarray(np.asarray(I["L0_attn_out_proj_w"], np.float32).T)),
        "wm1": pack_w(np.ascontiguousarray(np.asarray(I["L0_attn_mlp_w1"], np.float32).T)),
        "wm2": pack_w(np.ascontiguousarray(np.asarray(I["L0_attn_mlp_w2"], np.float32).T)),
        "c_ones": ones16, "c_iden": iden, "c_tri": tri, "c_ind": ind,
        "c_sel": sel127,
        "p_ln1": np.concatenate([fm_vec(I["L0_attn_ln1_w"], 8), fm_vec(I["L0_attn_ln1_b"], 8)], 1),
        "p_ln2": np.concatenate([fm_vec(I["L0_attn_ln2_w"], 8), fm_vec(I["L0_attn_ln2_b"], 8)], 1),
        "p_b1": fm_vec(I["L0_ffn_b1"], 32),
        "p_b2": fm_vec(I["L0_ffn_b2"], 8),
        "p_th": fm_vec(I["L0_ffn_quantum_phases"], 32),
        "p_wr": fm_vec(I["L0_ffn_quantum_weights_real"], 32),
        "p_wi": fm_vec(I["L0_ffn_quantum_weights_imag"], 32),
        "p_mb1": fm_vec(I["L0_attn_mlp_b1"], 32),
        "p_mb2": fm_vec(I["L0_attn_mlp_b2"], 8),
        "p_nw": fm_vec(I["L0_attn_norm_w"], 16),
        "p_cw": np.ascontiguousarray(cw_pad.reshape(18, 128, 4).transpose(1, 0, 2).reshape(128, 72)),
        "p_cb": fm_vec(cb_pad, 18),
        "p_dtb": np.asarray(I["L0_attn_dt_bias"], np.float32).reshape(32, 1),
        "p_al": np.asarray(I["L0_attn_A_log"], np.float32).reshape(32, 1),
        "p_D": np.ascontiguousarray(np.broadcast_to(
            np.asarray(I["L0_attn_D"], np.float32).reshape(1, 32), (128, 32))),
        "p_DP": _mk_DP(I),
    }
    xfull = np.asarray(I["L0_attn_x"], np.float32)
    in_maps = []
    for c in range(NCORE):
        xc = xfull[2 * c:2 * c + 2].reshape(T, D)
        arr = np.ascontiguousarray(xc.reshape(T, NC8, 128).transpose(2, 1, 0).reshape(128, NC8 * T))
        m = dict(common)
        m["xin"] = arr
        in_maps.append(m)
    return in_maps


def unpack_out(res):
    outs = []
    for c in range(NCORE):
        O = np.asarray(res[c]["xout"])
        xc = O.reshape(128, NC8, T).transpose(2, 1, 0).reshape(T, D)
        outs.append(xc.reshape(2, LSEQ, D))
    return np.concatenate(outs, 0).astype(np.float32)


def kernel(**inputs):
    in_maps = make_inputs(inputs)
    nc = build_program()
    res = run_bass_kernel_spmd(nc, in_maps, core_ids=list(range(NCORE)))
    return unpack_out(res.results)


# version 15
# speedup vs baseline: 1.0158x; 1.0158x over previous
import math
import numpy as np
import concourse.bass as bass
import concourse.mybir as mybir
from concourse.bass_utils import run_bass_kernel_spmd

F32 = mybir.dt.float32
F16 = mybir.dt.float16
AF = mybir.ActivationFunctionType
ALU = mybir.AluOpType

D = 1024
DI = 2048
DST = 64
HD = 64
NH = 32
DIPP = 35 * 128
DFF = 4096
LSEQ = 2048
NSEQ = 2
T = NSEQ * LSEQ
TT = 512
NTT = T // TT
Q = 128
EPS = 1e-5
NCORE = 8
NC8, NC16, NC18, NC32 = 8, 16, 18, 32
DBG_SKIP = set()
DBG_NQ = [4]
OFFS = {}
XBCW = 515


class Ctr:
    def __init__(self, handle, name):
        self.h = handle
        self.name = name
        self.n = 0


class Bump:
    def __init__(self, cap, base=0):
        self.cap = cap
        self.top = base

    def take(self, w, align=8):
        self.top = (self.top + align - 1) // align * align
        o = self.top
        self.top += w
        assert self.top <= self.cap, f"arena overflow: {self.top} > {self.cap}"
        return o


class KB:
    SERIAL = []

    def __init__(self, nc):
        self.nc = nc
        self.ops = {e: [] for e in ("t", "s", "v", "g", "y")}
        self.esem = {}

    def sem(self, name):
        return Ctr(self.nc.alloc_semaphore(name), name)

    def E(self, e, fn, waits=(), inc=None, inci=1):
        waits = tuple(waits) + tuple((c, c.n) for c in KB.SERIAL)
        ws = tuple((c, v) for (c, v) in waits if v > 0)
        if inc is None and e in self.esem:
            inc = self.esem[e]
        val = None
        if inc is not None:
            inc.n += inci
            val = inc.n
        self.ops[e].append((ws, fn, (inc.h if inc else None), inci))
        return val

    def W(self, e, waits):
        self.ops[e].append((tuple((c, v) for (c, v) in waits if v > 0), None, None, 0))

    def dr(self, e):
        if e in self.esem:
            c = self.esem[e]
            self.ops[e].append((((c, c.n),), None, None, 0))
        else:
            self.ops[e].append(((), lambda eng: eng.drain(), None, 0))

    def finalize(self, blk):
        def runner(ename):
            def body(eng):
                for ws, fn, semh, inci in self.ops[ename]:
                    for (c, v) in ws:
                        eng.wait_ge(c.h, v)
                    if fn is not None:
                        ins = fn(eng)
                        if semh is not None:
                            ins.then_inc(semh, inci)
            return body
        blk.tensor(runner("t"))
        blk.scalar(runner("s"))
        blk.vector(runner("v"))
        blk.gpsimd(runner("g"))
        blk.sync(runner("y"))


def pack_w(Wm):
    Din, Dout = Wm.shape
    nK = Din // 128
    assert Dout % 128 == 0
    cols = []
    for mc in range(Dout // 128):
        blk = Wm[:, mc * 128:(mc + 1) * 128]
        for kc in range(nK):
            cols.append(blk[kc * 128:(kc + 1) * 128, :])
    return np.ascontiguousarray(np.concatenate(cols, axis=1), dtype=np.float32)


def fm_vec(v, nch):
    return np.ascontiguousarray(np.asarray(v, np.float32).reshape(nch, 128).T)


def _mk_DP(I):
    Dv = np.asarray(I["L0_attn_D"], np.float32).reshape(32)
    DP = np.zeros((128, 16), np.float32)
    for p in range(16):
        DP[0:64, p] = Dv[2 * p]
        DP[64:128, p] = Dv[2 * p + 1]
    return DP


def build_consts():
    ones16 = np.ones((128, 128), np.float32)
    iden = np.eye(128, dtype=np.float32)
    tri = np.triu(np.ones((128, 128), np.float32))
    ind = np.zeros((32, 32 * 128), np.float32)
    for h in range(32):
        ind[h, h * 128:(h + 1) * 128] = 1.0
    sel127 = np.zeros((128, 64), np.float32)
    sel127[127, :] = 1.0
    return ones16, iden, tri, ind, sel127


def build_program(stages=("ffn1", "mamba", "mlp", "ffn2")):
    nc = bass.Bass("TRN2", target_bir_lowering=False)
    dram = lambda n, sh, dt, k="ExternalInput": nc.dram_tensor(n, sh, dt, kind=k).ap()

    xin = dram("xin", [128, NC8 * T], F32)
    wsrc = {
        "ffn1": dram("wffn1", [128, 8 * DFF], F32),
        "ffn2": dram("wffn2", [128, 32 * D], F32),
        "ip": dram("wip", [128, 8 * DIPP], F32),
        "op": dram("wop", [128, 16 * D], F32),
        "m1": dram("wm1", [128, 8 * DFF], F32),
        "m2": dram("wm2", [128, 32 * D], F32),
    }
    c_ones = dram("c_ones", [128, 128], F32)
    c_iden = dram("c_iden", [128, 128], F32)
    c_tri = dram("c_tri", [128, 128], F32)
    c_ind = dram("c_ind", [32, 4096], F32)
    c_sel = dram("c_sel", [128, 64], F32)
    psrc = {n: dram("p_" + n, sh, F32) for n, sh in [
        ("ln1", [128, 16]), ("ln2", [128, 16]), ("b1", [128, 32]),
        ("b2", [128, 8]), ("th", [128, 32]), ("wr", [128, 32]),
        ("wi", [128, 32]), ("mb1", [128, 32]), ("mb2", [128, 8]),
        ("nw", [128, 16]), ("cw", [128, 72]), ("cb", [128, 18]),
        ("D", [128, 32]), ("DP", [128, 16])]}
    p_dtb = dram("p_dtb", [32, 1], F32)
    p_al = dram("p_al", [32, 1], F32)

    xout = dram("xout", [128, NC8 * T], F32, k="ExternalOutput")
    cache = {
        "ffn1": dram("cffn1", [128, 8 * DFF], F16, k="Internal"),
        "ffn2": dram("cffn2", [128, 32 * D], F16, k="Internal"),
        "ip": dram("cip", [128, 8 * DIPP], F16, k="Internal"),
        "op": dram("cop", [128, 16 * D], F16, k="Internal"),
        "m1": dram("cm1", [128, 8 * DFF], F16, k="Internal"),
        "m2": dram("cm2", [128, 32 * D], F16, k="Internal"),
    }
    vbuf = dram("vbuf", [128, NC8 * T], F16, k="Internal")
    dbg = dram("dbg", [128, 16384], F32, k="Internal")

    B = KB(nc)
    al = nc.alloc_sbuf_tensor

    x = al("x", [128, NC8 * T], F16)
    ones16 = al("ones16", [128, 128], F16)
    iden16 = al("iden16", [128, 128], F16)
    iden32 = al("iden32", [32, 32], F32)
    tri32 = al("tri32", [128, 128], F32)
    ind16 = al("ind16", [32, 4096], F16)
    hl2 = al("hl2", [32, 384], F16)
    sel16 = al("sel16", [128, 64], F16)
    prm = al("prm", [128, 504], F32)
    prm32r = al("prm32r", [32, 4], F32)
    hstate = al("hstate", [64, NH * HD], F16)
    zer32 = al("zer32", [32, 128], F32)
    zer16 = al("zer16", [64, 64], F16)
    zer16h = al("zer16h", [64, 64], F16)
    epsb = al("epsb", [128, 1], F32)
    oneb = al("oneb", [128, 1], F32)
    eps2b = al("eps2b", [128, 1], F32)
    wk8 = [al(f"wk8{i}", [128, 2048], F16) for i in range(3)]
    wk32 = [al(f"wk32{i}", [128, 2048], F16) for i in range(3)]

    PV = {}
    _pb = Bump(504)
    for name, w in [("ln1w", 8), ("ln1b", 8), ("ln2w", 8), ("ln2b", 8),
                    ("b1", 32), ("b2", 8), ("th", 32), ("wr", 32), ("wi", 32),
                    ("mb1", 32), ("mb2", 8), ("nw", 16), ("cw", 72), ("cb", 18),
                    ("Drep", 32), ("b01s", 32), ("b01c", 32), ("Ac", 32),
                    ("Bc", 32), ("hb2", 8), ("DrepP", 16)]:
        PV[name] = _pb.take(w, align=1)
    P = lambda name, w: prm[:, PV[name]:PV[name] + w]
    Pc = lambda name, c: prm[:, PV[name] + c:PV[name] + c + 1]

    NA16 = 38912
    NA32 = 6912
    A16 = al("arena16", [128, NA16], F16)
    A32 = al("arena32", [128, NA32], F32)
    CAST_OFF = 29696
    caststg = [A16[:, CAST_OFF + i * 4096:CAST_OFF + (i + 1) * 4096] for i in range(2)]

    ps = [nc.alloc_psum_tensor(f"ps{i}", [128, 512], F32) for i in range(6)]
    p16 = [nc.alloc_psum_tensor(f"p16{i}", [128, 1024], F16) for i in range(2)]

    s_t = B.sem("s_t")
    s_s = B.sem("s_s")
    s_v = B.sem("s_v")
    g_init = B.sem("g_init")
    g_cast = [B.sem("g_cast0"), B.sem("g_cast1")]
    y_c = B.sem("y_c")
    y_wb = [B.sem("y_wb0"), B.sem("y_wb1")]
    y_k8 = [B.sem(f"y_k8{i}") for i in range(3)]
    y_k32 = [B.sem(f"y_k32{i}") for i in range(3)]
    y_out = B.sem("y_out")
    y_vw = B.sem("y_vw")
    y_vr = B.sem("y_vr")
    y_vr2 = B.sem("y_vr2")
    g_xt = [B.sem(f"g_xt{i}") for i in range(NTT)]

    B.esem = {"t": s_t, "s": s_s, "v": s_v}
    if DBG_SKIP and "globser" in DBG_SKIP:
        KB.SERIAL = [s_t, s_s, s_v, g_init, y_c] + y_k8 + y_k32 + y_wb + g_cast + [y_vw, y_vr, y_out]
    else:
        KB.SERIAL = []
    bankg = {i: [] for i in range(8)}

    def t_mm(bank_id, out_ap, lhsT, rhs, start, stop, waits=(), inc=None, tile_position=None):
        g = tuple(bankg[bank_id]) if start else ()
        if start:
            bankg[bank_id] = []
        return B.E("t", lambda e, a=out_ap, l=lhsT, r=rhs, st=start, sp=stop, tp=tile_position:
                   e.matmul(a, l, r, start=st, stop=sp, tile_position=tp),
                   waits=tuple(waits) + g, inc=inc)

    def t_tr(bank_id, out_ap, in_ap, ident, waits=(), inc=None):
        g = tuple(bankg[bank_id])
        bankg[bank_id] = []
        return B.E("t", lambda e, a=out_ap, i=in_ap, d=ident:
                   e.transpose(a, i, d), waits=tuple(waits) + g, inc=inc)

    def guard(bank_id, ctr, val):
        bankg[bank_id].append((ctr, val))

    def yd(dst, src, waits=(), inc=None):
        return B.E("y", lambda e, d=dst, s=src: e.dma_start(out=d, in_=s),
                   waits=waits, inc=inc, inci=16)

    def gd(dst, src, waits=(), inc=None):
        return B.E("g", lambda e, d=dst, s=src: e.dma_start(out=d, in_=s),
                   waits=waits, inc=inc, inci=16)

    yd(iden32[:, :], c_iden[0:32, 0:32], inc=y_c)
    yd(tri32[:, :], c_tri, inc=y_c)
    for nm, key, w in [("ln1w", "ln1", 16), ("ln2w", "ln2", 16), ("b1", "b1", 32),
                       ("b2", "b2", 8), ("th", "th", 32), ("wr", "wr", 32),
                       ("wi", "wi", 32), ("mb1", "mb1", 32), ("mb2", "mb2", 8),
                       ("nw", "nw", 16), ("cw", "cw", 72), ("cb", "cb", 18),
                       ("Drep", "D", 32), ("DrepP", "DP", 16)]:
        yd(P(nm, w), psrc[key], inc=y_c)
    yd(prm32r[:, 0:1], p_dtb, inc=y_c)
    yd(prm32r[:, 1:2], p_al, inc=y_c)
    YC0 = y_c.n

    gd(ones16[:, :], c_ones, inc=g_init)
    gd(iden16[:, :], c_iden, inc=g_init)
    gd(ind16[:, :], c_ind, inc=g_init)
    gd(sel16[:, :], c_sel, inc=g_init)
    GCONST = g_init.n
    xtile_ready = {}
    GI0 = None

    tmpa = A32[:, 0:32]
    tmpb = A32[:, 32:64]
    s0t = A32[:, 64:96]
    c0t = A32[:, 96:128]
    q1 = A32[:, 128:160]
    q2 = A32[:, 160:192]
    B.W("v", [(y_c, YC0)])
    B.E("v", lambda e: e.tensor_scalar(P("b01s", 32), P("b1", 32), 0.1, None, ALU.mult), inc=s_v)
    B.E("v", lambda e: e.tensor_scalar(P("b01c", 32), P("b1", 32), 0.1, math.pi / 2, ALU.mult, ALU.add), inc=s_v)
    B.E("v", lambda e: e.tensor_scalar(P("hb2", 8), P("b2", 8), 0.5, None, ALU.mult), inc=s_v)
    B.E("v", lambda e: e.memset(A16[:, 0:CAST_OFF], 0.0), inc=s_v)
    B.E("v", lambda e: e.memset(A16[:, CAST_OFF + 8192:NA16], 0.0), inc=s_v)
    B.E("v", lambda e: e.memset(A32[:, :], 0.0), inc=s_v)
    B.dr("v")
    B.E("v", lambda e: e.memset(zer32[:, :], 0.0), inc=s_v)
    B.E("v", lambda e: e.memset(epsb[:, :], EPS), inc=s_v)
    B.E("v", lambda e: e.memset(oneb[:, :], 1.0), inc=s_v)
    B.E("v", lambda e: e.memset(eps2b[:, :], EPS * EPS), inc=s_v)
    B.E("v", lambda e: e.memset(zer16[:, :], 0.0), inc=s_v)
    B.E("v", lambda e: e.memset(zer16h[:, :], 0.0), inc=s_v)
    B.E("v", lambda e: e.memset(hstate[:, :], 0.0), inc=s_v)
    VP0 = s_v.n
    B.W("s", [(s_v, VP0), (y_c, YC0)])
    B.E("s", lambda e: e.activation(tmpa, P("th", 32), AF.Sin, scale=1.0 / 32), inc=s_s)
    B.E("s", lambda e: e.activation(s0t, P("th", 32), AF.Sin, scale=1.0 / 16), inc=s_s)
    B.E("s", lambda e: e.activation(prm32r[:, 2:3], prm32r[:, 1:2], AF.Exp), inc=s_s)
    SP0 = s_s.n
    B.W("v", [(s_s, SP0)])
    B.E("v", lambda e: e.tensor_tensor(tmpb, tmpa, tmpa, ALU.mult), inc=s_v)
    B.dr("v")
    B.E("v", lambda e: e.tensor_scalar(c0t, tmpb, -2.0, 1.0, ALU.mult, ALU.add), inc=s_v)
    B.dr("v")
    sc_pair = [(s0t, c0t), (tmpa, tmpb)]
    for it in range(4):
        so, co = sc_pair[it % 2]
        sn, cn = sc_pair[(it + 1) % 2]
        B.E("v", lambda e, so=so, co=co, sn=sn: e.scalar_tensor_tensor(
            sn, so, 2.0, co, ALU.mult, ALU.mult), inc=s_v)
        B.E("v", lambda e, so=so, cn=cn: e.tensor_tensor(cn, so, so, ALU.mult), inc=s_v)
        B.dr("v")
        B.E("v", lambda e, cn=cn: e.tensor_scalar(cn, cn, -2.0, 1.0, ALU.mult, ALU.add), inc=s_v)
        B.dr("v")
    B.E("v", lambda e: e.tensor_scalar(prm32r[:, 2:3], prm32r[:, 2:3], -1.0, None, ALU.mult), inc=s_v)
    B.E("v", lambda e: e.tensor_tensor(q1, c0t, P("wr", 32), ALU.mult), inc=s_v)
    B.E("v", lambda e: e.tensor_tensor(q2, s0t, P("wi", 32), ALU.mult), inc=s_v)
    B.dr("v")
    B.E("v", lambda e: e.tensor_tensor(q1, q1, q2, ALU.add), inc=s_v)
    B.dr("v")
    B.E("v", lambda e: e.tensor_tensor(q1, q1, P("wi", 32), ALU.mult), inc=s_v)
    B.dr("v")
    B.E("v", lambda e: e.tensor_scalar(q1, q1, 0.1, None, ALU.mult), inc=s_v)
    B.dr("v")
    B.E("v", lambda e: e.tensor_tensor(P("Ac", 32), q1, c0t, ALU.mult), inc=s_v)
    B.E("v", lambda e: e.tensor_tensor(P("Bc", 32), q1, s0t, ALU.mult), inc=s_v)
    VPREP = s_v.n
    Aneg = prm32r[:, 2:3]
    dtbias = prm32r[:, 0:1]

    cast_jobs = []
    for key, tot in [("ffn1", 8 * DFF), ("ffn2", 32 * D), ("ip", 8 * DIPP),
                     ("op", 16 * D), ("m1", 8 * DFF), ("m2", 32 * D)]:
        o = 0
        while o < tot:
            w = min(4096, tot - o)
            cast_jobs.append((key, o, w))
            o += w
    cache_ready = {}
    cache_marks = {k: [] for k in cache}
    xq = list(range(NTT))
    def emit_xtile():
        if xq:
            tt = xq.pop(0)
            for c in range(NC8):
                gd(x[:, c * T + tt * TT:c * T + (tt + 1) * TT],
                   xin[:, c * T + tt * TT:c * T + (tt + 1) * TT], inc=g_xt[tt])
            xtile_ready[tt] = (g_xt[tt], g_xt[tt].n)
    emit_xtile()
    for i, (key, o, w) in enumerate(cast_jobs):
        sl = i % 2
        gwait = [(y_wb[sl], y_wb[sl].n)] if i >= 2 else []
        gd(caststg[sl][:, 0:w], wsrc[key][:, o:o + w], waits=gwait, inc=g_cast[sl])
        dv = yd(cache[key][:, o:o + w], caststg[sl][:, 0:w],
                waits=[(g_cast[sl], g_cast[sl].n)], inc=y_wb[sl])
        cache_marks[key].append((o + w, y_wb[sl], dv))
        cache_ready[key] = ((y_wb[0], y_wb[0].n), (y_wb[1], y_wb[1].n))
        emit_xtile()
    while xq:
        emit_xtile()
    GI0 = g_init.n
    CASTS_DONE = ((y_wb[0], y_wb[0].n), (y_wb[1], y_wb[1].n))

    def cache_wait(key, off, w):
        for end, ctr, val in cache_marks[key]:
            if end >= off + w:
                return [(ctr, val)]
        raise AssertionError("no cache mark")

    class Stream:
        def __init__(self, bufs, sems, cachekey, first_waits):
            self.bufs = bufs
            self.sems = sems
            self.key = cachekey
            self.cons = {0: None, 1: None, 2: None}
            self.i = 0
            self.pend = []
            self.emitted = []
            self.first_waits = first_waits
            self.seen = 0

        def queue(self, off, w):
            self.pend.append((off, w))
            self._try_emit()

        def _try_emit(self):
            while self.pend:
                i = self.i
                sl = i % 3
                if i >= 3 and self.cons[sl] is None:
                    break
                off, w = self.pend.pop(0)
                waits = []
                if i < 3:
                    waits += list(self.first_waits)
                waits += cache_wait(self.key, off, w)
                if self.cons[sl] is not None:
                    waits.append(self.cons[sl])
                    self.cons[sl] = None
                dv = yd(self.bufs[sl][:, 0:w], cache[self.key][:, off:off + w],
                        waits=waits, inc=self.sems[sl])
                self.emitted.append((sl, dv, off))
                self.i += 1

        def unit(self, i):
            assert i < len(self.emitted), "unit not yet emitted"
            sl, dv, off = self.emitted[i]
            return sl, (self.sems[sl], dv)

        def consumed(self, i, ctr, val):
            sl, dv, off = self.emitted[i]
            self.cons[sl] = (ctr, val)
            self._try_emit()

    LN_BASE = NA32 - 2048
    LNm = A32[:, LN_BASE:LN_BASE + 512]
    LNmsq = A32[:, LN_BASE + 512:LN_BASE + 1024]
    LNvar = A32[:, LN_BASE + 1024:LN_BASE + 1536]
    LNrstd = A32[:, LN_BASE + 1536:LN_BASE + 2048]
    SQ_BASE = NA16 - 1024
    sqp = [A16[:, SQ_BASE + i * 512:SQ_BASE + (i + 1) * 512] for i in range(2)]

    LNr12 = A32[:, LN_BASE - 512:LN_BASE]

    def emit_ln(src_fn, out_fn, nch, w, aff=None, b_sum=4, b_sq=5, src_ready=None,
                out_guard=None, also_ln2=None):
        nd = float(nch * 128)
        sr = [src_ready] if src_ready else []
        tm = None
        for c in range(nch):
            tm = t_mm(b_sum, ps[b_sum][:, 0:w], ones16[:, :], src_fn(c),
                      start=(c == 0), stop=(c == nch - 1),
                      waits=(sr if c == 0 else []),
                      inc=(s_t if c == nch - 1 else None))
        sqm = []
        for c in range(nch):
            sl = c % 2
            swaits = [(s_t, sqm[c - 2])] if c >= 2 else list(sr)
            sv = B.E("s", lambda e, c=c, sl=sl: e.activation(sqp[sl][:, 0:w], src_fn(c), AF.Square),
                     waits=swaits, inc=s_s)
            tmq = t_mm(b_sq, ps[b_sq][:, 0:w], ones16[:, :], sqp[sl][:, 0:w],
                       start=(c == 0), stop=(c == nch - 1),
                       waits=[(s_s, sv)], inc=s_t)
            sqm.append(tmq)
        vm = B.E("v", lambda e: e.tensor_scalar(LNm[:, 0:w], ps[b_sum][:, 0:w], 1.0 / nd, None, ALU.mult),
                 waits=[(s_t, tm)], inc=s_v)
        guard(b_sum, s_v, vm)
        B.dr("v")
        B.E("v", lambda e: e.tensor_tensor(LNmsq[:, 0:w], LNm[:, 0:w], LNm[:, 0:w], ALU.mult), inc=s_v)
        B.dr("v")
        vv = B.E("v", lambda e: e.scalar_tensor_tensor(LNvar[:, 0:w], ps[b_sq][:, 0:w], 1.0 / nd,
                                                       LNmsq[:, 0:w], ALU.mult, ALU.subtract),
                 waits=[(s_t, sqm[-1])], inc=s_v)
        guard(b_sq, s_v, vv)
        sv = B.E("s", lambda e: e.activation(LNrstd[:, 0:w], LNvar[:, 0:w], AF.Ln, bias=epsb[:, 0:1]),
                 waits=[(s_v, vv)], inc=s_s)
        B.dr("s")
        sv = B.E("s", lambda e: e.activation(LNrstd[:, 0:w], LNrstd[:, 0:w], AF.Exp, scale=-0.5), inc=s_s)
        if also_ln2 is not None:
            sv2b = B.E("s", lambda e: e.activation(LNr12[:, 0:w], LNvar[:, 0:w], AF.Ln,
                                                   scale=1.0 + EPS, bias=eps2b[:, 0:1]),
                       waits=[(s_v, vv)], inc=s_s)
            B.dr("s")
            sv2b = B.E("s", lambda e: e.activation(LNr12[:, 0:w], LNr12[:, 0:w], AF.Exp, scale=-0.5), inc=s_s)
            B.W("v", [(s_s, sv2b)])
        B.W("v", [(s_s, sv)])
        applies = []
        last = None
        for c in range(nch):
            og = [out_guard] if (out_guard is not None and c == 0) else []
            B.E("v", lambda e, c=c: e.tensor_tensor(LNmsq[:, 0:w], src_fn(c), LNm[:, 0:w], ALU.subtract),
                waits=og, inc=s_v)
            B.dr("v")
            if aff is None:
                last = B.E("v", lambda e, c=c: e.tensor_tensor(out_fn(c), LNmsq[:, 0:w], LNrstd[:, 0:w], ALU.mult), inc=s_v)
            else:
                wn, bn = aff
                B.E("v", lambda e: e.tensor_tensor(LNvar[:, 0:w], LNmsq[:, 0:w], LNrstd[:, 0:w], ALU.mult), inc=s_v)
                B.dr("v")
                last = B.E("v", lambda e, c=c, wn=wn, bn=bn: e.tensor_scalar(
                    out_fn(c), LNvar[:, 0:w], Pc(wn, c), Pc(bn, c), ALU.mult, ALU.add), inc=s_v)
            if also_ln2 is not None:
                out2_fn, wn2, bn2 = also_ln2
                B.E("v", lambda e, c=c: e.tensor_tensor(LNvar[:, 0:w], LNmsq[:, 0:w], LNr12[:, 0:w], ALU.mult), inc=s_v)
                B.dr("v")
                last = B.E("v", lambda e, c=c, out2_fn=out2_fn, wn2=wn2, bn2=bn2: e.tensor_scalar(
                    out2_fn(c), LNvar[:, 0:w], Pc(wn2, c), Pc(bn2, c), ALU.mult, ALU.add), inc=s_v)
            B.dr("v")
            applies.append(last)
        return applies

    def x_sl(c, tt, w=TT):
        return x[:, c * T + tt * TT:c * T + tt * TT + w]

    def ffn_stage(key1, key2, final):
        a16 = Bump(CAST_OFF if not final else SQ_BASE)
        a32 = Bump(LN_BASE - 512, base=256)
        xln2 = []
        for _ in range(2):
            xln = []
            for _ in range(NC8):
                o = a16.take(TT)
                xln.append(A16[:, o:o + TT])
            xln2.append(xln)
        act = []
        for _ in range(NC32):
            o = a16.take(TT)
            act.append(A16[:, o:o + TT])
        s1p, c1p = [], []
        for _ in range(2):
            o = a32.take(TT); s1p.append(A32[:, o:o + TT])
        for _ in range(2):
            o = a32.take(TT); c1p.append(A32[:, o:o + TT])
        o = a32.take(TT); v1 = A32[:, o:o + TT]
        o = a32.take(TT); ev = A32[:, o:o + TT]
        if final:
            o = a32.take(1024)
            ost = A32[:, o:o + 1024]
            xo_r = xout.rearrange("p (c t) -> p c t", c=NC8)

        st1 = Stream(wk8, y_k8, key1, [(s_t, s_t.n)])
        st2 = Stream(wk32, y_k32, key2, [(s_t, s_t.n)])
        q_cons = {0: 0, 1: 0}
        B.W("t", [(g_init, GCONST), (y_c, YC0), (s_v, s_v.n), (s_s, s_s.n)])
        B.W("s", [(g_init, GCONST), (y_c, YC0)])
        vln = {}
        hid_end = {}
        if "nopipe" not in DBG_SKIP:
            emit_ln(lambda c: x_sl(c, 0), lambda c: xln2[0][c], NC8, TT,
                    src_ready=xtile_ready[0])
            vln[0] = s_v.n
        for tt in range(NTT):
            xln = xln2[tt % 2]
            if "nopipe" in DBG_SKIP:
                emit_ln(lambda c, tt=tt: x_sl(c, tt), lambda c, tt=tt: xln2[tt % 2][c],
                        NC8, TT, out_guard=(s_t, hid_end.get(tt - 1, 0)))
                vln[tt] = s_v.n
            for gi in range(16):
                st1.queue(gi * 2048, 2048)
            for fc in range(NC32):
                gi = fc // 2
                sl, (sem, dv) = st1.unit(tt * 16 + gi)
                boff = (fc % 2) * 1024
                bank = fc % 4
                for kc in range(8):
                    w0 = []
                    if kc == 0:
                        w0 = [(sem, dv), (s_v, vln[tt])]
                    t_mm(bank, ps[bank][:, 0:TT],
                         wk8[sl][:, boff + kc * 128:boff + kc * 128 + 128],
                         xln[kc][:, :], start=(kc == 0), stop=(kc == 7),
                         waits=w0, inc=(s_t if kc == 7 else None))
                tmv = s_t.n
                if fc % 2 == 1:
                    st1.consumed(tt * 16 + gi, s_t, tmv)
                sw = [(s_t, tmv)]
                if fc >= 2:
                    sw.append((s_v, q_cons[fc % 2]))
                sv1 = B.E("s", lambda e, b=bank, fc=fc: e.activation(
                    s1p[fc % 2], ps[b][:, 0:TT], AF.Sin, bias=Pc("b01s", fc), scale=0.1),
                    waits=sw, inc=s_s)
                sv2 = B.E("s", lambda e, b=bank, fc=fc: e.activation(
                    c1p[fc % 2], ps[b][:, 0:TT], AF.Sin, bias=Pc("b01c", fc), scale=0.1), inc=s_s)
                vv = B.E("v", lambda e, b=bank, fc=fc: e.scalar_tensor_tensor(
                    v1, s1p[fc % 2], Pc("Ac", fc), ps[b][:, 0:TT], ALU.mult, ALU.add),
                    waits=[(s_s, sv2)], inc=s_v)
                guard(bank, s_v, vv)
                B.dr("v")
                vv2 = B.E("v", lambda e, fc=fc: e.scalar_tensor_tensor(
                    v1, c1p[fc % 2], Pc("Bc", fc), v1, ALU.mult, ALU.mult if False else ALU.add), inc=s_v)
                q_cons[fc % 2] = vv2
                B.dr("v")
                B.E("v", lambda e, fc=fc: e.tensor_scalar(
                    act[fc][:, :], v1, Pc("b1", fc), 0.0, ALU.add, ALU.max), inc=s_v)
                B.dr("v")
            vact = s_v.n
            hid_end[tt] = s_t.n
            if tt + 1 < NTT and "nopipe" not in DBG_SKIP:
                emit_ln(lambda c, tt=tt: x_sl(c, tt + 1), lambda c, tt=tt: xln2[(tt + 1) % 2][c],
                        NC8, TT, out_guard=(s_t, hid_end.get(tt - 1, 0)),
                        src_ready=xtile_ready[tt + 1])
                vln[tt + 1] = s_v.n
            for hdc in range(16):
                st2.queue(hdc * 2048, 2048)
            if final:
                og = (y_out, y_out.n)
            for dc in range(NC8):
                bank = dc % 4
                for kc in range(32):
                    hdc = dc * 2 + kc // 16
                    sl, (sem, dv) = st2.unit(tt * 16 + hdc)
                    w0 = []
                    if kc % 16 == 0:
                        w0 = [(sem, dv)]
                    if kc == 0:
                        w0 = w0 + [(s_v, vact)]
                    t_mm(bank, ps[bank][:, 0:TT],
                         wk32[sl][:, (kc % 16) * 128:(kc % 16) * 128 + 128],
                         act[kc][:, :], start=(kc == 0), stop=(kc == 31),
                         waits=w0, inc=(s_t if kc in (15, 31) else None))
                    if kc == 15:
                        st2.consumed(tt * 16 + dc * 2, s_t, s_t.n)
                tmv = s_t.n
                st2.consumed(tt * 16 + dc * 2 + 1, s_t, tmv)
                vv = B.E("v", lambda e, b=bank, dc=dc: e.tensor_scalar(
                    ev, ps[b][:, 0:TT], 0.5, Pc("hb2", dc), ALU.mult, ALU.add),
                    waits=[(s_t, tmv)], inc=s_v)
                guard(bank, s_v, vv)
                B.dr("v")
                if not final:
                    B.E("v", lambda e, dc=dc, tt=tt: e.tensor_tensor(
                        x_sl(dc, tt), ev, x_sl(dc, tt), ALU.add), inc=s_v)
                else:
                    B.E("v", lambda e, dc=dc, tt=tt: e.tensor_tensor(
                        ost[:, (dc % 2) * TT:(dc % 2) * TT + TT], ev, x_sl(dc, tt), ALU.add),
                        waits=([og] if dc % 2 == 0 else []), inc=s_v)
                    if dc % 2 == 1:
                        yd(xo_r[:, dc - 1:dc + 1, tt * TT:(tt + 1) * TT],
                           ost.rearrange("p (c t) -> p c t", c=2),
                           waits=[(s_v, s_v.n)], inc=y_out)
                        og = (y_out, y_out.n)
                B.dr("v")

    def mamba_stage():
        a16 = Bump(NA16 - 1024)
        o = a16.take(NC8 * TT); u = A16[:, o:o + NC8 * TT]
        o = a16.take(NC8 * TT); uln = A16[:, o:o + NC8 * TT]
        o = a16.take(NC16 * TT); zsilu = A16[:, o:o + NC16 * TT]
        o = a16.take(NC18 * XBCW); xbc0 = o; OFFS["xbc"] = o
        xbc = [A16[:, xbc0 + c * XBCW:xbc0 + (c + 1) * XBCW] for c in range(NC18)]
        o = a16.take(NC18 * 3); sbar = A16[:, o:o + NC18 * 3]
        o = a16.take(NC16 * TT); yn = A16[:, o:o + NC16 * TT]; OFFS["yn"] = o
        o = a16.take(256); cumhi16 = A16[0:32, o:o + 128]
        expcum16 = A16[0:32, o + 128:o + 256]
        o = a16.take(128); cumlo16 = A16[0:32, o:o + 128]
        hiP = [cumhi16, hl2[0:32, 0:128]]
        expP16 = [expcum16, hl2[0:32, 128:256]]
        loP = [cumlo16, hl2[0:32, 256:384]]
        o = a16.take(2048); yq = A16[:, o:o + 2048]; OFFS["yq"] = o

        o = a16.take(32); nthi16 = A16[:, o:o + 32]
        o = a16.take(32); ntlo16 = A16[:, o:o + 32]
        o = a16.take(64); btm16 = A16[:, o:o + 64]
        o = a16.take(256); G16P = A16[:, o:o + 256]
        G16 = [A16[:, o:o + 128], A16[:, o + 128:o + 256]]
        Cd16 = []
        for _ in range(2):
            o = a16.take(128); Cd16.append(A16[0:64, o:o + 128])
        dxp = []
        for _ in range(2):
            o = a16.take(128); dxp.append(A16[:, o:o + 128])
        dxw = []
        for _ in range(2):
            o = a16.take(128); dxw.append(A16[:, o:o + 128])
        sqq = []
        for _ in range(2):
            o = a16.take(128); sqq.append(A16[:, o:o + 128])

        a32 = Bump(LN_BASE, base=256)
        cvp = []
        for _ in range(2):
            o = a32.take(512); cvp.append(A32[:, o:o + 512])
        o = a32.take(512); dtb = A32[0:32, o:o + 512]
        o = a32.take(512); dtraw = A32[0:32, o:o + 512]
        o = a32.take(128); acum = A32[0:32, o:o + 128]
        o = a32.take(128); cum = A32[0:32, o:o + 128]
        o = a32.take(128); rvi = A32[0:32, o:o + 128]
        o = a32.take(128); rve = A32[0:32, o:o + 128]
        o = a32.take(32); nctm = A32[:, o:o + 32]
        o = a32.take(32); wdtm = A32[:, o:o + 32]
        o = a32.take(32); dttm = A32[:, o:o + 32]
        o = a32.take(32); dAt = A32[0:64, o:o + 32]
        o = a32.take(256); spm2 = A32[:, o:o + 256]
        spm = A32[:, o:o + 128]
        spmB = A32[:, o + 128:o + 256]
        e1 = []
        for _ in range(2):
            o = a32.take(128); e1.append(A32[:, o:o + 128])
        o = a32.take(256); DexP = A32[:, o:o + 256]
        Dex = [A32[:, o:o + 128], A32[:, o + 128:o + 256]]
        o = a32.take(128); rstd2 = A32[:, o:o + 128]

        stip = Stream(wk8, y_k8, "ip", [(s_t, s_t.n)])
        stop_ = Stream(wk32, y_k32, "op", [(s_t, s_t.n)])
        B.W("t", [(s_v, s_v.n), (s_s, s_s.n), (g_init, GI0), (y_c, YC0)] + list(CASTS_DONE))
        B.W("v", list(CASTS_DONE))
        B.W("s", list(CASTS_DONE) + [(g_init, GI0), (y_c, YC0)])

        B.E("v", lambda e: e.memset(yn[:, :], 0.0), inc=s_v)
        B.dr("v")
        nunit_ip = 18
        vstate_prev = {p: VP0 for p in range(16)}
        rms_prev = 0
        sv4_prev = 0
        for sc in range(NTT):
            seq_start = (sc % 4 == 0)
            if sc > 0:
                B.W("v", [(y_vw, y_vw.n)])
            uln_ap = emit_ln(lambda c, sc=sc: x_sl(c, sc), lambda c: u[:, c * TT:(c + 1) * TT],
                             NC8, TT, also_ln2=(lambda c: uln[:, c * TT:(c + 1) * TT], "ln1w", "ln1b"),
                             src_ready=xtile_ready[sc])
            vuln = s_v.n
            if seq_start:
                B.E("v", lambda e: e.memset(sbar[:, :], 0.0), inc=s_v)
                B.dr("v")
            for c in range(NC18):
                B.E("v", lambda e, c=c: e.tensor_copy(
                    xbc[c][:, 0:3], sbar[:, c * 3:c * 3 + 3]), inc=s_v)
            B.dr("v")
            for ui in range(nunit_ip):
                w = 2048 if ui < 17 else 1024
                stip.queue(ui * 2048, w)
            ip_evac = {}
            dt_done = None
            for oc in range(35):
                ui = oc // 2
                sl, (sem, dv) = stip.unit(sc * nunit_ip + ui)
                boff = (oc % 2) * 1024
                bank = oc % 4
                for kc in range(8):
                    w0 = [(s_v, uln_ap[kc])]
                    if kc == 0:
                        w0.append((sem, dv))
                    t_mm(bank, ps[bank][:, 0:TT],
                         wk8[sl][:, boff + kc * 128:boff + kc * 128 + 128],
                         uln[:, kc * TT:(kc + 1) * TT], start=(kc == 0), stop=(kc == 7),
                         waits=w0, inc=(s_t if kc == 7 else None))
                tmv = s_t.n
                if oc % 2 == 1 or oc == 34:
                    stip.consumed(sc * nunit_ip + ui, s_t, tmv)
                if oc < 16:
                    sv = B.E("s", lambda e, b=bank, oc=oc: e.activation(
                        zsilu[:, oc * TT:(oc + 1) * TT], ps[b][:, 0:TT], AF.Silu),
                        waits=[(s_t, tmv)], inc=s_s)
                elif oc < 34:
                    c = oc - 16
                    sv = B.E("s", lambda e, b=bank, c=c: e.activation(
                        xbc[c][:, 3:XBCW], ps[b][:, 0:TT], AF.Copy),
                        waits=[(s_t, tmv)], inc=s_s)
                    ip_evac[c] = sv
                else:
                    vv = B.E("v", lambda e, b=bank: e.tensor_scalar(
                        dtraw, ps[b][0:32, 0:TT], dtbias, 80.0, ALU.add, ALU.min),
                        waits=[(s_t, tmv)], inc=s_v)
                    sv = B.E("s", lambda e: e.activation(dtraw, dtraw, AF.Exp),
                             waits=[(s_v, vv)], inc=s_s)
                    B.dr("s")
                    sv = B.E("s", lambda e: e.activation(dtb, dtraw, AF.Ln, bias=oneb[0:32, 0:1]),
                             inc=s_s)
                    dt_done = sv
                    guard(bank, s_v, vv)
                if oc < 34:
                    guard(bank, s_s, sv)
            conv_done = {}
            for c in range(NC18):
                cva = cvp[c % 2]
                w0 = [(s_s, ip_evac[c])]
                if c >= 2:
                    w0.append((s_s, conv_done[c - 2]))
                B.E("v", lambda e, c=c, cva=cva: e.tensor_scalar(
                    cva, xbc[c][:, 0:512], Pc("cw", c * 4 + 0), None, ALU.mult),
                    waits=w0, inc=s_v)
                for k in range(1, 4):
                    B.dr("v")
                    B.E("v", lambda e, c=c, k=k, cva=cva: e.scalar_tensor_tensor(
                        cva, xbc[c][:, k:k + 512], Pc("cw", c * 4 + k), cva,
                        ALU.mult, ALU.add), inc=s_v)
                B.dr("v")
                vv2 = B.E("v", lambda e, c=c: e.tensor_copy(
                    sbar[:, c * 3:c * 3 + 3], xbc[c][:, 512:515]), inc=s_v)
                sv = B.E("s", lambda e, c=c, cva=cva: e.activation(
                    xbc[c][:, 3:XBCW], cva, AF.Silu, bias=Pc("cb", c)),
                    waits=[(s_v, vv2)], inc=s_s)
                conv_done[c] = sv
            xs_done = [conv_done[c] for c in range(16)]
            bc_done = max(conv_done[16], conv_done[17])

            def emit_prefix(qq, w_first):
                IB = qq % 2
                B.E("v", lambda e, qq=qq: e.tensor_scalar(
                    acum, dtb[:, qq * 128:(qq + 1) * 128], Aneg, None, ALU.mult),
                    waits=w_first, inc=s_v)
                B.dr("v")
                B.E("v", lambda e: e.tensor_tensor_scan(
                    cum, acum, zer32[:, :], 0.0, ALU.add, ALU.add), inc=s_v)
                B.dr("v")
                B.E("v", lambda e: e.tensor_tensor_scan(
                    rvi[:, 127::-1], acum[:, 127::-1], zer32[:, :], 0.0, ALU.add, ALU.add), inc=s_v)
                B.dr("v")
                B.E("v", lambda e: e.tensor_tensor(rve, rvi, acum, ALU.subtract), inc=s_v)
                B.dr("v")
                B.E("v", lambda e, IB=IB: e.tensor_copy(hiP[IB], cum), inc=s_v)
                B.dr("v")
                vs = B.E("v", lambda e, IB=IB: e.tensor_tensor(
                    loP[IB], cum, hiP[IB], ALU.subtract), inc=s_v)
                B.dr("v")
                return vs

            vset_ms = {}
            vset_ms[0] = emit_prefix(0, [(s_s, dt_done), (s_t, s_t.n), (s_s, sv4_prev)])
            vdx_prev = {}
            for q in range(DBG_NQ[0]):
                if "dumph" in DBG_SKIP and sc == 0:
                    gd(dbg[0:64, q * 2048:q * 2048 + 2048], hstate[:, :],
                       waits=[(s_v, s_v.n), (s_t, s_t.n)], inc=g_init)
                    B.W("v", [(g_init, g_init.n)])
                    B.W("t", [(g_init, g_init.n)])
                if "dumpyn01" in DBG_SKIP and sc == 0 and q == min(2, DBG_NQ[0] - 1) and False:
                    gd(dbg[:, 0:8192], yn[:, :], waits=[(s_v, s_v.n)], inc=g_init)
                    B.W("v", [(g_init, g_init.n)])
                    B.W("s", [(g_init, g_init.n)])
                if "qfence" in DBG_SKIP:
                    B.W("t", [(s_v, s_v.n), (s_s, s_s.n)])
                    B.W("v", [(s_t, s_t.n), (s_s, s_s.n)])
                    B.W("s", [(s_t, s_t.n), (s_v, s_v.n)])
                qo = 3 + q * 128
                vsetup = vset_ms[q]
                t_tr(2, ps[2][:, 0:32], cum, iden32[0:32, 0:32], waits=[(s_v, vsetup)], inc=s_t)
                t_tr(2, ps[2][:, 32:64], rve, iden32[0:32, 0:32], inc=s_t)
                tma = t_tr(2, ps[2][:, 64:96], dtb[:, q * 128:(q + 1) * 128], iden32[0:32, 0:32], inc=s_t)
                w0 = [(s_s, bc_done)]
                if vdx_prev.get(14) is not None:
                    w0.append((s_v, vdx_prev[14]))
                tmb = t_tr(6, p16[0][:, 256:320], xbc[16][0:64, qo:qo + 128], iden16[0:64, 0:64],
                           waits=w0, inc=s_t)
                sv1 = B.E("s", lambda e: e.activation(nctm, ps[2][:, 0:32], AF.Copy),
                          waits=[(s_t, tma)], inc=s_s)
                sv2 = B.E("s", lambda e: e.activation(wdtm, ps[2][:, 32:64], AF.Exp), inc=s_s)
                sv3 = B.E("s", lambda e: e.activation(dttm, ps[2][:, 64:96], AF.Copy), inc=s_s)
                sv4 = B.E("s", lambda e, q=q: e.activation(expP16[q % 2], cum, AF.Exp),
                          waits=[(s_v, vsetup)], inc=s_s)
                vbt = B.E("v", lambda e: e.tensor_copy(btm16, p16[0][:, 256:320]),
                          waits=[(s_t, tmb)], inc=s_v)
                guard(6, s_v, vbt)
                guard(2, s_s, sv3)
                tms = t_mm(2, ps[2][:, 128:256], xbc[16][0:64, qo:qo + 128],
                           xbc[17][0:64, qo:qo + 128], start=True, stop=True,
                           waits=[(s_s, bc_done)], inc=s_t)
                B.E("v", lambda e: e.tensor_tensor(spm, ps[2][:, 128:256], tri32[:, :], ALU.mult),
                    waits=[(s_t, tms)], inc=s_v)
                B.dr("v")
                vsp = B.E("v", lambda e: e.tensor_tensor(spmB, ps[2][:, 128:256], tri32[:, :], ALU.mult),
                          inc=s_v)
                B.dr("v")
                vv = B.E("v", lambda e: e.tensor_copy(nthi16, nctm),
                         waits=[(s_s, sv1)], inc=s_v)
                B.dr("v")
                vnt = B.E("v", lambda e: e.tensor_tensor(ntlo16, nctm, nthi16, ALU.subtract), inc=s_v)
                guard(2, s_v, vsp)
                t_mm(2, ps[2][0:64, 256:288], sel16[:, :], nthi16,
                     start=True, stop=False, waits=[(s_v, vnt)])
                tmd = t_mm(2, ps[2][0:64, 256:288], sel16[:, :], ntlo16,
                           start=False, stop=True, inc=s_t)
                svA = B.E("s", lambda e: e.activation(dAt, ps[2][0:64, 256:288], AF.Exp),
                          waits=[(s_t, tmd)], inc=s_s)
                guard(2, s_s, svA)

                tslot = {}
                tD = {}
                tTR = {}
                tY = {}
                tH = {}
                vG = {}
                vCd = {}
                vdx = {}
                vdxw = {}
                vevac = {}
                vstate = {}
                sD = {}
                ve1 = {}
                for s in range(34):
                    serR = any(f.startswith("serR") and
                               int(f[4:].split("_")[0]) <= s < int(f[4:].split("_")[1])
                               for f in DBG_SKIP)
                    if "serial" in DBG_SKIP or serR:
                        B.W("t", [(s_v, s_v.n), (s_s, s_s.n)])
                        B.W("v", [(s_t, s_t.n), (s_s, s_s.n)])
                        B.W("s", [(s_t, s_t.n), (s_v, s_v.n)])
                    if "sert" in DBG_SKIP:
                        B.W("t", [(s_s, s_s.n)])
                    if "servt" in DBG_SKIP:
                        B.W("t", [(s_v, s_v.n)])
                    if "serv" in DBG_SKIP:
                        B.W("v", [(s_t, s_t.n), (s_s, s_s.n)])
                    if "servT" in DBG_SKIP:
                        B.W("v", [(s_t, s_t.n)])
                    if "servS" in DBG_SKIP:
                        B.W("v", [(s_s, s_s.n)])
                    if "sers" in DBG_SKIP:
                        B.W("s", [(s_t, s_t.n), (s_v, s_v.n)])
                    if s < 32:
                        bk = s % 2
                        w0 = [(s_v, vsetup), (s_s, sv4)]
                        if s >= 2:
                            w0.append((s_v, vCd[s - 2]))
                            w0.append((s_s, sD[s - 2]))
                            if "srelu" not in DBG_SKIP:
                                w0.append((s_v, ve1[s - 2]))
                        t_mm(bk, ps[bk][:, 0:128], ind16[:, s * 128:(s + 1) * 128],
                             hiP[q % 2], start=True, stop=False, waits=w0)
                        t_mm(bk, ps[bk][:, 0:128], ind16[:, s * 128:(s + 1) * 128],
                             loP[q % 2], start=False, stop=True)
                        t_mm(bk, ps[bk][:, 128:256], ind16[:, s * 128:(s + 1) * 128],
                             expP16[q % 2], start=True, stop=True, inc=s_t)
                        tD[s] = s_t.n
                    if s % 2 == 0 and s < 32:
                        p = s // 2
                        pbk = 6 + p % 2
                        w0 = [(s_s, xs_done[p])]
                        if p >= 2:
                            w0.append((s_v, vdx[p - 2]))
                        elif vdx_prev.get(p + 14) is not None:
                            w0.append((s_v, vdx_prev[p + 14]))
                        t_tr(pbk, p16[p % 2][:, 0:128], xbc[p][:, qo:qo + 128],
                             iden16[:, :], waits=w0, inc=s_t)
                        tTR[p] = s_t.n
                    if s >= 2 and (s - 2) % 2 == 0 and s - 2 < 32:
                        vv = B.E("v", lambda e: e.tensor_tensor(
                            G16P, DexP, spm2, ALU.mult),
                            waits=[(s_s, sD[s - 1]), (s_v, vsp)], inc=s_v)
                        vG[s - 2] = vG[s - 1] = vv
                    if s >= 2:
                        hh = s - 2
                        ybk = 4 + (hh // 2) % 2
                        rows = slice(64, 128) if hh % 2 == 1 else slice(0, 64)
                        tp = (0, 64) if hh % 2 == 1 else None
                        w0 = [(s_v, vG[hh])]
                        if hh >= 4:
                            w0.append((s_v, vevac[hh - 4]))
                        skip2 = "y2" in DBG_SKIP
                        skip1 = "y1" in DBG_SKIP
                        if skip1:
                            B.W("t", w0)
                            B.E("t", lambda e, bk2=ybk, rows2=rows, tp2=tp: e.matmul(
                                ps[bk2][rows2, 0:128],
                                zer16h[:, 0:64], zer16[0:64, 0:128],
                                start=True, stop=False, tile_position=tp2))
                        else:
                            t_mm(ybk, ps[ybk][rows, 0:128],
                                 dxp[(hh // 2) % 2][:, (hh % 2) * 64:(hh % 2) * 64 + 64],
                                 G16[hh % 2][:, :], start=True, stop=skip2,
                                 waits=w0, tile_position=tp)
                        if skip2:
                            w1 = [(s_v, vCd[hh]), (s_v, vstate_prev[hh // 2])]
                            t_mm(ybk, ps[ybk][rows, 0:128], hstate[:, hh * 64:(hh + 1) * 64],
                                 Cd16[hh % 2][:, :], start=False, stop=True,
                                 waits=w1, inc=s_t, tile_position=tp) if False else None
                            B.W("t", w1)
                            s_t.n += 0
                            tdummy = B.E("t", lambda e, bk2=ybk, rows2=rows, tp2=tp: e.matmul(
                                ps[bk2][rows2, 256:384], zer16h[:, 0:64], zer16[0:64, 0:128],
                                start=True, stop=True, tile_position=tp2), inc=s_t)
                        else:
                            w1 = [(s_v, vCd[hh]), (s_v, vstate_prev[hh // 2])]
                            t_mm(ybk, ps[ybk][rows, 0:128], hstate[:, hh * 64:(hh + 1) * 64],
                                 Cd16[hh % 2][:, :], start=False, stop=True,
                                 waits=w1, inc=s_t, tile_position=tp)
                        tY[hh] = s_t.n
                        if hh % 2 == 1:
                            p = hh // 2
                            hbk = 2 + p % 2
                            t_mm(hbk, ps[hbk][0:64, 384:512], btm16[:, :],
                                 dxw[p % 2][:, :], start=True, stop=True,
                                 waits=[(s_v, vdxw[p])], inc=s_t)
                            tH[p] = s_t.n
                    tslot[s] = s_t.n

                    if s < 32 and "srelu" not in DBG_SKIP:
                        bk = s % 2
                        w0 = [(s_t, tD[s]), (s_s, sv1)]
                        if s >= 2:
                            w0.append((s_s, sD[s - 2]))
                            w0.append((s_v, vG[s - 2]))
                        ve1[s] = B.E("v", lambda e, bk=bk, s=s: e.tensor_scalar(
                            e1[s % 2], ps[bk][:, 0:128], nctm[:, s:s + 1], 0.0,
                            ALU.subtract, ALU.min), waits=w0, inc=s_v)
                        B.dr("v")
                    if s < 32:
                        bk = s % 2
                        w0 = [(s_t, tD[s])]
                        if s >= 2:
                            w0.append((s_t, tY[s - 2]))
                        vCd[s] = B.E("v", lambda e, bk=bk, s=s, qo=qo: e.tensor_tensor(
                            Cd16[s % 2], xbc[17][0:64, qo:qo + 128], ps[bk][0:64, 128:256],
                            ALU.mult), waits=w0, inc=s_v)
                    if s % 2 == 0 and s < 32:
                        p = s // 2
                        w0 = [(s_t, tslot[s]), (s_s, sv3)]
                        for par in range(2):
                            vdx0 = B.E("v", lambda e, p=p, par=par: e.tensor_scalar(
                                dxp[p % 2][:, par * 64:par * 64 + 64],
                                p16[p % 2][:, par * 64:par * 64 + 64],
                                dttm[:, 2 * p + par:2 * p + par + 1], None, ALU.mult),
                                waits=(w0 if par == 0 else []), inc=s_v)
                        vdx[p] = vdx0
                        B.dr("v")
                        w0 = [(s_s, sv2)]
                        for par in range(2):
                            vdxw0 = B.E("v", lambda e, p=p, par=par: e.tensor_scalar(
                                dxw[p % 2][:, par * 64:par * 64 + 64],
                                dxp[p % 2][:, par * 64:par * 64 + 64],
                                wdtm[:, 2 * p + par:2 * p + par + 1], None, ALU.mult),
                                waits=(w0 if par == 0 else []), inc=s_v)
                        vdxw[p] = vdxw0
                        B.dr("v")
                    if s >= 2:
                        hh = s - 2
                        if hh % 2 == 1:
                            p2 = hh // 2
                            ybk2 = 4 + p2 % 2
                            vv = B.E("v", lambda e, p2=p2, ybk2=ybk2, qo=qo:
                                     e.scalar_tensor_tensor(
                                yq[:, p2 * 128:p2 * 128 + 128], xbc[p2][:, qo:qo + 128],
                                P("DrepP", 16)[:, p2:p2 + 1], ps[ybk2][:, 0:128],
                                ALU.mult, ALU.add), waits=[(s_t, tY[hh])], inc=s_v)
                            vevac[2 * p2] = vevac[2 * p2 + 1] = vv
                            guard(ybk2, s_v, vv)
                        if hh % 2 == 1:
                            p = hh // 2
                            hbk = 2 + p % 2
                            vst = None
                            for par in range(2):
                                vst = B.E("v", lambda e, p=p, par=par, hbk=hbk:
                                          e.scalar_tensor_tensor(
                                    hstate[:, (2 * p + par) * 64:(2 * p + par) * 64 + 64],
                                    hstate[:, (2 * p + par) * 64:(2 * p + par) * 64 + 64],
                                    dAt[:, 2 * p + par:2 * p + par + 1],
                                    ps[hbk][0:64, 384 + par * 64:384 + par * 64 + 64],
                                    ALU.mult, ALU.add),
                                    waits=([(s_t, tH[p]), (s_s, svA)] if par == 0 else []),
                                    inc=s_v)
                                B.dr("v")
                            vstate[p] = vst
                            guard(hbk, s_v, vst)

                    if s == 26 and q + 1 < DBG_NQ[0]:
                        vset_ms[q + 1] = emit_prefix(
                            q + 1, [(s_s, dt_done), (s_t, tD[0]), (s_s, sv4)])
                    if s < 32 and "srelu" not in DBG_SKIP:
                        sD[s] = B.E("s", lambda e, s=s: e.activation(
                            Dex[s % 2], e1[s % 2], AF.Exp),
                            waits=[(s_v, ve1[s])], inc=s_s)
                    elif s < 32:
                        bk = s % 2
                        w0 = [(s_t, tslot[s]), (s_s, sv1)]
                        if s >= 2:
                            w0.append((s_v, vG[s - 2]))
                        B.E("s", lambda e, bk=bk, s=s: e.activation(
                            e1[s % 2], ps[bk][:, 0:128], AF.Relu, scale=-1.0,
                            bias=nctm[:, s:s + 1]), waits=w0, inc=s_s)
                        B.dr("s")
                        sD[s] = B.E("s", lambda e, s=s: e.activation(
                            Dex[s % 2], e1[s % 2], AF.Exp, scale=-1.0), inc=s_s)
                vstate_prev = dict(vstate)
                vdx_prev = {14: vdx[14], 15: vdx[15]}
                guard(0, s_v, vCd[30])
                guard(1, s_v, vCd[31])
                guard(0, s_s, sD[30])
                guard(1, s_s, sD[31])
                if "srelu" not in DBG_SKIP:
                    guard(0, s_v, ve1[30])
                    guard(1, s_v, ve1[31])

                if "dumpyn" in DBG_SKIP and q == 0 and sc == NTT - 1:
                    gd(dbg[:, 10240:10240 + 2048], yq[:, :], waits=[(s_v, s_v.n)], inc=g_init)
                    B.W("v", [(g_init, g_init.n)])
                B.dr("v")
                vgate = {}
                for c in range(NC16):
                    vgate[c] = B.E("v", lambda e, c=c, q=q: e.tensor_tensor(
                        yq[:, c * 128:(c + 1) * 128], yq[:, c * 128:(c + 1) * 128],
                        zsilu[:, c * TT + q * 128:c * TT + q * 128 + 128], ALU.mult),
                        waits=([(s_v, vevac[30]), (s_v, vevac[31])] if c == 0 else []),
                        inc=s_v)
                    B.dr("v")
                if "dumpyn" in DBG_SKIP and q == 0 and sc == NTT - 1:
                    gd(dbg[:, 12288:12288 + 2048], yq[:, :], waits=[(s_v, s_v.n)], inc=g_init)
                    B.W("v", [(g_init, g_init.n)])
                    B.W("s", [(g_init, g_init.n)])
                rms_mm = []
                for c in range(NC16):
                    sl = c % 2
                    w0 = [(s_v, vgate[c])]
                    if c >= 2:
                        w0.append((s_t, rms_mm[c - 2]))
                    elif rms_prev:
                        w0.append((s_t, rms_prev))
                    sv = B.E("s", lambda e, c=c, sl=sl: e.activation(
                        sqq[sl], yq[:, c * 128:(c + 1) * 128], AF.Square),
                        waits=w0, inc=s_s)
                    rmsm = t_mm(4, ps[4][:, 0:128], ones16[:, :], sqq[sl],
                                start=(c == 0), stop=(c == NC16 - 1),
                                waits=[(s_s, sv)], inc=s_t)
                    rms_mm.append(rmsm)
                rms_prev = rms_mm[-1]
                sv = B.E("s", lambda e: e.activation(rstd2, ps[4][:, 0:128], AF.Ln,
                                                     scale=1.0 / DI, bias=epsb[:, 0:1]),
                         waits=[(s_t, rms_mm[-1])], inc=s_s)
                B.dr("s")
                sv = B.E("s", lambda e: e.activation(rstd2, rstd2, AF.Exp, scale=-0.5), inc=s_s)
                guard(4, s_s, sv)
                for c in range(NC16):
                    B.E("v", lambda e, c=c, q=q: e.scalar_tensor_tensor(
                        yn[:, c * TT + q * 128:c * TT + q * 128 + 128],
                        yq[:, c * 128:(c + 1) * 128], Pc("nw", c), rstd2,
                        ALU.mult, ALU.mult), waits=([(s_s, sv)] if c == 0 else []), inc=s_v)
                B.dr("v")
                if "dumpyn" in DBG_SKIP and q == 0 and sc == NTT - 1:
                    gd(dbg[:, 0:8192], yn[:, :], waits=[(s_v, s_v.n)], inc=g_init)
                    gd(dbg[:, 8192:8192 + 2048], yq[:, :], waits=[(s_v, s_v.n)], inc=g_init)
                    B.W("v", [(g_init, g_init.n)])
                    B.W("s", [(g_init, g_init.n)])
                if "dumpyn01" in DBG_SKIP and sc == 0 and q == 1:
                    gd(dbg[:, 0:8192], yn[:, :], waits=[(s_v, s_v.n)], inc=g_init)
                    B.W("v", [(g_init, g_init.n)])
                    B.W("s", [(g_init, g_init.n)])
                    B.W("t", [(g_init, g_init.n)])
            vyn = s_v.n
            sv4_prev = sv4

            for dc in range(NC8):
                stop_.queue(dc * 2048, 2048)
            for dc in range(NC8):
                sl, (sem, dv) = stop_.unit(sc * NC8 + dc)
                bank = dc % 4
                for kc in range(16):
                    w0 = []
                    if kc == 0:
                        w0 = [(sem, dv), (s_v, vyn)]
                    t_mm(bank, ps[bank][:, 0:TT],
                         wk32[sl][:, kc * 128:(kc + 1) * 128],
                         yn[:, kc * TT:(kc + 1) * TT], start=(kc == 0), stop=(kc == 15),
                         waits=w0, inc=(s_t if kc == 15 else None))
                tmv = s_t.n
                stop_.consumed(sc * NC8 + dc, s_t, tmv)
                vv = B.E("v", lambda e, b=bank, dc=dc: e.tensor_tensor(
                    u[:, dc * TT:(dc + 1) * TT], ps[b][:, 0:TT],
                    u[:, dc * TT:(dc + 1) * TT], ALU.add),
                    waits=[(s_t, tmv)], inc=s_v)
                guard(bank, s_v, vv)
                B.dr("v")
            vb_r = vbuf.rearrange("p (c t) -> p c t", c=NC8)
            yd(vb_r[:, :, sc * TT:(sc + 1) * TT],
               u.rearrange("p (c t) -> p c t", c=NC8),
               waits=[(s_v, s_v.n)], inc=y_vw)
            if sc == 3:
                vrst = B.E("v", lambda e: e.memset(hstate[:, :], 0.0),
                           waits=[(s_t, s_t.n)], inc=s_v)
                B.dr("v")
                vstate_prev = {p: vrst for p in range(16)}

    def mlp_stage():
        a16 = Bump(NA16 - 1024)
        vt = []
        for _ in range(2):
            o = a16.take(NC8 * TT); vt.append(A16[:, o:o + NC8 * TT])
        h2b = []
        for _ in range(2):
            o = a16.take(NC8 * TT); h2b.append(A16[:, o:o + NC8 * TT])
        act = []
        for _ in range(NC32):
            o = a16.take(TT)
            act.append(A16[:, o:o + TT])
        a32 = Bump(LN_BASE - 512, base=256)
        o = a32.take(TT); ev = A32[:, o:o + TT]

        st1 = Stream(wk8, y_k8, "m1", [(s_t, s_t.n)])
        st2 = Stream(wk32, y_k32, "m2", [(s_t, s_t.n)])
        vb_r = vbuf.rearrange("p (c t) -> p c t", c=NC8)
        vwall = y_vw.n
        vt_cons = {0: None, 1: None}
        vsem = [y_vr, y_vr2]
        vload = {}
        for tt in range(2):
            w0 = [(y_vw, vwall)]
            vload[tt] = (vsem[tt % 2], yd(vt[tt % 2].rearrange("p (c t) -> p c t", c=NC8),
                         vb_r[:, :, tt * TT:(tt + 1) * TT], waits=w0, inc=vsem[tt % 2]))
        B.W("v", [vload[0]])
        B.W("s", [vload[0]])
        B.W("t", [vload[0], (s_v, s_v.n), (s_s, s_s.n)])
        vln = {}
        hid_end = {}
        emit_ln(lambda c: vt[0][:, c * TT:(c + 1) * TT],
                lambda c: h2b[0][:, c * TT:(c + 1) * TT], NC8, TT, aff=("ln2w", "ln2b"),
                src_ready=vload[0])
        vln[0] = s_v.n
        for tt in range(NTT):
            sl = tt % 2
            h2 = h2b[sl]
            for gi in range(16):
                st1.queue(gi * 2048, 2048)
            for fc in range(NC32):
                gi = fc // 2
                slw, (sem, dv) = st1.unit(tt * 16 + gi)
                boff = (fc % 2) * 1024
                bank = fc % 4
                for kc in range(8):
                    w0 = []
                    if kc == 0:
                        w0 = [(sem, dv), (s_v, vln[tt])]
                    t_mm(bank, ps[bank][:, 0:TT],
                         wk8[slw][:, boff + kc * 128:boff + kc * 128 + 128],
                         h2[:, kc * TT:(kc + 1) * TT], start=(kc == 0), stop=(kc == 7),
                         waits=w0, inc=(s_t if kc == 7 else None))
                tmv = s_t.n
                if fc % 2 == 1:
                    st1.consumed(tt * 16 + gi, s_t, tmv)
                sv = B.E("s", lambda e, b=bank, fc=fc: e.activation(
                    act[fc][:, :], ps[b][:, 0:TT], AF.Gelu, bias=Pc("mb1", fc)),
                    waits=[(s_t, tmv)], inc=s_s)
                guard(bank, s_s, sv)
            vact = s_s.n
            hid_end[tt] = s_t.n
            if tt + 1 < NTT:
                emit_ln(lambda c, tt=tt: vt[(tt + 1) % 2][:, c * TT:(c + 1) * TT],
                        lambda c, tt=tt: h2b[(tt + 1) % 2][:, c * TT:(c + 1) * TT],
                        NC8, TT, aff=("ln2w", "ln2b"),
                        src_ready=vload[tt + 1],
                        out_guard=(s_t, hid_end.get(tt - 1, 0)))
                vln[tt + 1] = s_v.n
            for hdc in range(16):
                st2.queue(hdc * 2048, 2048)
            for dc in range(NC8):
                bank = dc % 4
                for kc in range(32):
                    hdc = dc * 2 + kc // 16
                    slw, (sem, dv) = st2.unit(tt * 16 + hdc)
                    w0 = []
                    if kc % 16 == 0:
                        w0 = [(sem, dv)]
                    if kc == 0:
                        w0 = w0 + [(s_s, vact)]
                    t_mm(bank, ps[bank][:, 0:TT],
                         wk32[slw][:, (kc % 16) * 128:(kc % 16) * 128 + 128],
                         act[kc][:, :], start=(kc == 0), stop=(kc == 31),
                         waits=w0, inc=(s_t if kc in (15, 31) else None))
                    if kc == 15:
                        st2.consumed(tt * 16 + dc * 2, s_t, s_t.n)
                tmv = s_t.n
                st2.consumed(tt * 16 + dc * 2 + 1, s_t, tmv)
                vv = B.E("v", lambda e, b=bank, dc=dc: e.tensor_scalar(
                    ev, ps[b][:, 0:TT], Pc("mb2", dc), None, ALU.add),
                    waits=[(s_t, tmv)], inc=s_v)
                guard(bank, s_v, vv)
                B.dr("v")
                B.E("v", lambda e, dc=dc, sl=sl: e.tensor_tensor(
                    ev, ev, vt[sl][:, dc * TT:(dc + 1) * TT], ALU.add), inc=s_v)
                B.dr("v")
                B.E("v", lambda e, dc=dc, tt=tt: e.tensor_tensor(
                    x_sl(dc, tt), ev, x_sl(dc, tt), ALU.add), inc=s_v)
                B.dr("v")
            vt_cons[sl] = (s_v, s_v.n)
            if tt + 2 < NTT:
                vload[tt + 2] = (vsem[sl], yd(vt[sl].rearrange("p (c t) -> p c t", c=NC8),
                                 vb_r[:, :, (tt + 2) * TT:(tt + 3) * TT],
                                 waits=[(y_vw, vwall), vt_cons[sl]], inc=vsem[sl]))

    if "ffn1" in stages:
        ffn_stage("ffn1", "ffn2", final=False)
    if "mamba" in stages:
        mamba_stage()
    if "mlp" in stages:
        mlp_stage()
    if "ffn2" in stages:
        ffn_stage("ffn1", "ffn2", final=True)
        B.W("y", [(y_out, y_out.n)])
    else:
        B.W("g", [(s_v, s_v.n), (s_t, s_t.n), (s_s, s_s.n)])
        for c in range(NC8):
            gd(xout[:, c * T:(c + 1) * T], x[:, c * T:(c + 1) * T], inc=g_init)
        B.W("y", [(g_init, g_init.n)])

    from contextlib import ExitStack
    with ExitStack() as ctx:
        blk = ctx.enter_context(nc.Block())
        B.finalize(blk)
    return nc


def make_inputs(I):
    ones16, iden, tri, ind, sel127 = build_consts()
    Wt = np.ascontiguousarray(np.asarray(I["L0_attn_in_proj_w"], np.float32).T)
    Wpad = np.zeros((D, DIPP), np.float32)
    Wpad[:, 0:4096] = Wt[:, 0:4096]
    Wpad[:, 32 * 128:32 * 128 + 64] = Wt[:, 4096:4160]
    Wpad[:, 33 * 128:33 * 128 + 64] = Wt[:, 4160:4224]
    Wpad[:, 34 * 128:34 * 128 + 32] = Wt[:, 4224:4256]
    cw = np.asarray(I["L0_attn_conv_w"], np.float32)
    cw_pad = np.zeros((18 * 128, 4), np.float32)
    cw_pad[0:2048] = cw[0:2048]
    cw_pad[16 * 128:16 * 128 + 64] = cw[2048:2112]
    cw_pad[17 * 128:17 * 128 + 64] = cw[2112:2176]
    cb = np.asarray(I["L0_attn_conv_b"], np.float32)
    cb_pad = np.zeros((18 * 128,), np.float32)
    cb_pad[0:2048] = cb[0:2048]
    cb_pad[16 * 128:16 * 128 + 64] = cb[2048:2112]
    cb_pad[17 * 128:17 * 128 + 64] = cb[2112:2176]
    common = {
        "wffn1": pack_w(np.asarray(I["L0_ffn_W1"], np.float32)),
        "wffn2": pack_w(np.asarray(I["L0_ffn_W2"], np.float32)),
        "wip": pack_w(Wpad),
        "wop": pack_w(np.ascontiguousarray(np.asarray(I["L0_attn_out_proj_w"], np.float32).T)),
        "wm1": pack_w(np.ascontiguousarray(np.asarray(I["L0_attn_mlp_w1"], np.float32).T)),
        "wm2": pack_w(np.ascontiguousarray(np.asarray(I["L0_attn_mlp_w2"], np.float32).T)),
        "c_ones": ones16, "c_iden": iden, "c_tri": tri, "c_ind": ind,
        "c_sel": sel127,
        "p_ln1": np.concatenate([fm_vec(I["L0_attn_ln1_w"], 8), fm_vec(I["L0_attn_ln1_b"], 8)], 1),
        "p_ln2": np.concatenate([fm_vec(I["L0_attn_ln2_w"], 8), fm_vec(I["L0_attn_ln2_b"], 8)], 1),
        "p_b1": fm_vec(I["L0_ffn_b1"], 32),
        "p_b2": fm_vec(I["L0_ffn_b2"], 8),
        "p_th": fm_vec(I["L0_ffn_quantum_phases"], 32),
        "p_wr": fm_vec(I["L0_ffn_quantum_weights_real"], 32),
        "p_wi": fm_vec(I["L0_ffn_quantum_weights_imag"], 32),
        "p_mb1": fm_vec(I["L0_attn_mlp_b1"], 32),
        "p_mb2": fm_vec(I["L0_attn_mlp_b2"], 8),
        "p_nw": fm_vec(I["L0_attn_norm_w"], 16),
        "p_cw": np.ascontiguousarray(cw_pad.reshape(18, 128, 4).transpose(1, 0, 2).reshape(128, 72)),
        "p_cb": fm_vec(cb_pad, 18),
        "p_dtb": np.asarray(I["L0_attn_dt_bias"], np.float32).reshape(32, 1),
        "p_al": np.asarray(I["L0_attn_A_log"], np.float32).reshape(32, 1),
        "p_D": np.ascontiguousarray(np.broadcast_to(
            np.asarray(I["L0_attn_D"], np.float32).reshape(1, 32), (128, 32))),
        "p_DP": _mk_DP(I),
    }
    xfull = np.asarray(I["L0_attn_x"], np.float32)
    in_maps = []
    for c in range(NCORE):
        xc = xfull[2 * c:2 * c + 2].reshape(T, D)
        arr = np.ascontiguousarray(xc.reshape(T, NC8, 128).transpose(2, 1, 0).reshape(128, NC8 * T))
        m = dict(common)
        m["xin"] = arr
        in_maps.append(m)
    return in_maps


def unpack_out(res):
    outs = []
    for c in range(NCORE):
        O = np.asarray(res[c]["xout"])
        xc = O.reshape(128, NC8, T).transpose(2, 1, 0).reshape(T, D)
        outs.append(xc.reshape(2, LSEQ, D))
    return np.concatenate(outs, 0).astype(np.float32)


def kernel(**inputs):
    in_maps = make_inputs(inputs)
    nc = build_program()
    res = run_bass_kernel_spmd(nc, in_maps, core_ids=list(range(NCORE)))
    return unpack_out(res.results)


# version 16
# speedup vs baseline: 1.0206x; 1.0047x over previous
import math
import numpy as np
import concourse.bass as bass
import concourse.mybir as mybir
from concourse.bass_utils import run_bass_kernel_spmd

F32 = mybir.dt.float32
F16 = mybir.dt.float16
AF = mybir.ActivationFunctionType
ALU = mybir.AluOpType

D = 1024
DI = 2048
DST = 64
HD = 64
NH = 32
DIPP = 35 * 128
DFF = 4096
LSEQ = 2048
NSEQ = 2
T = NSEQ * LSEQ
TT = 512
NTT = T // TT
Q = 128
EPS = 1e-5
NCORE = 8
NC8, NC16, NC18, NC32 = 8, 16, 18, 32
DBG_SKIP = set()
DBG_NQ = [4]
OFFS = {}
XBCW = 515


class Ctr:
    def __init__(self, handle, name):
        self.h = handle
        self.name = name
        self.n = 0


class Bump:
    def __init__(self, cap, base=0):
        self.cap = cap
        self.top = base

    def take(self, w, align=8):
        self.top = (self.top + align - 1) // align * align
        o = self.top
        self.top += w
        assert self.top <= self.cap, f"arena overflow: {self.top} > {self.cap}"
        return o


class KB:
    SERIAL = []

    def __init__(self, nc):
        self.nc = nc
        self.ops = {e: [] for e in ("t", "s", "v", "g", "y")}
        self.esem = {}

    def sem(self, name):
        return Ctr(self.nc.alloc_semaphore(name), name)

    def E(self, e, fn, waits=(), inc=None, inci=1):
        waits = tuple(waits) + tuple((c, c.n) for c in KB.SERIAL)
        ws = tuple((c, v) for (c, v) in waits if v > 0)
        if inc is None and e in self.esem:
            inc = self.esem[e]
        val = None
        if inc is not None:
            inc.n += inci
            val = inc.n
        self.ops[e].append((ws, fn, (inc.h if inc else None), inci))
        return val

    def W(self, e, waits):
        self.ops[e].append((tuple((c, v) for (c, v) in waits if v > 0), None, None, 0))

    def dr(self, e):
        if e in self.esem:
            c = self.esem[e]
            self.ops[e].append((((c, c.n),), None, None, 0))
        else:
            self.ops[e].append(((), lambda eng: eng.drain(), None, 0))

    def finalize(self, blk):
        def runner(ename):
            def body(eng):
                for ws, fn, semh, inci in self.ops[ename]:
                    for (c, v) in ws:
                        eng.wait_ge(c.h, v)
                    if fn is not None:
                        ins = fn(eng)
                        if semh is not None:
                            ins.then_inc(semh, inci)
            return body
        blk.tensor(runner("t"))
        blk.scalar(runner("s"))
        blk.vector(runner("v"))
        blk.gpsimd(runner("g"))
        blk.sync(runner("y"))


def pack_w(Wm):
    Din, Dout = Wm.shape
    nK = Din // 128
    assert Dout % 128 == 0
    cols = []
    for mc in range(Dout // 128):
        blk = Wm[:, mc * 128:(mc + 1) * 128]
        for kc in range(nK):
            cols.append(blk[kc * 128:(kc + 1) * 128, :])
    return np.ascontiguousarray(np.concatenate(cols, axis=1), dtype=np.float32)


def fm_vec(v, nch):
    return np.ascontiguousarray(np.asarray(v, np.float32).reshape(nch, 128).T)


def _mk_DP(I):
    Dv = np.asarray(I["L0_attn_D"], np.float32).reshape(32)
    DP = np.zeros((128, 16), np.float32)
    for p in range(16):
        DP[0:64, p] = Dv[2 * p]
        DP[64:128, p] = Dv[2 * p + 1]
    return DP


def build_consts():
    ones16 = np.ones((128, 128), np.float32)
    iden = np.eye(128, dtype=np.float32)
    tri = np.triu(np.ones((128, 128), np.float32))
    ind = np.zeros((32, 32 * 128), np.float32)
    for h in range(32):
        ind[h, h * 128:(h + 1) * 128] = 1.0
    sel127 = np.zeros((128, 64), np.float32)
    sel127[127, :] = 1.0
    return ones16, iden, tri, ind, sel127


def build_program(stages=("ffn1", "mamba", "mlp", "ffn2")):
    nc = bass.Bass("TRN2", target_bir_lowering=False)
    dram = lambda n, sh, dt, k="ExternalInput": nc.dram_tensor(n, sh, dt, kind=k).ap()

    xin = dram("xin", [128, NC8 * T], F32)
    wsrc = {
        "ffn1": dram("wffn1", [128, 8 * DFF], F32),
        "ffn2": dram("wffn2", [128, 32 * D], F32),
        "ip": dram("wip", [128, 8 * DIPP], F32),
        "op": dram("wop", [128, 16 * D], F32),
        "m1": dram("wm1", [128, 8 * DFF], F32),
        "m2": dram("wm2", [128, 32 * D], F32),
    }
    c_ones = dram("c_ones", [128, 128], F32)
    c_iden = dram("c_iden", [128, 128], F32)
    c_tri = dram("c_tri", [128, 128], F32)
    c_ind = dram("c_ind", [32, 4096], F32)
    c_sel = dram("c_sel", [128, 64], F32)
    psrc = {n: dram("p_" + n, sh, F32) for n, sh in [
        ("ln1", [128, 16]), ("ln2", [128, 16]), ("b1", [128, 32]),
        ("b2", [128, 8]), ("th", [128, 32]), ("wr", [128, 32]),
        ("wi", [128, 32]), ("mb1", [128, 32]), ("mb2", [128, 8]),
        ("nw", [128, 16]), ("cw", [128, 72]), ("cb", [128, 18]),
        ("D", [128, 32]), ("DP", [128, 16])]}
    p_dtb = dram("p_dtb", [32, 1], F32)
    p_al = dram("p_al", [32, 1], F32)

    xout = dram("xout", [128, NC8 * T], F32, k="ExternalOutput")
    cache = {
        "ffn1": dram("cffn1", [128, 8 * DFF], F16, k="Internal"),
        "ffn2": dram("cffn2", [128, 32 * D], F16, k="Internal"),
        "ip": dram("cip", [128, 8 * DIPP], F16, k="Internal"),
        "op": dram("cop", [128, 16 * D], F16, k="Internal"),
        "m1": dram("cm1", [128, 8 * DFF], F16, k="Internal"),
        "m2": dram("cm2", [128, 32 * D], F16, k="Internal"),
    }
    vbuf = dram("vbuf", [128, NC8 * T], F16, k="Internal")
    dbg = dram("dbg", [128, 16384], F32, k="Internal")

    B = KB(nc)
    al = nc.alloc_sbuf_tensor

    x = al("x", [128, NC8 * T], F16)
    ones16 = al("ones16", [128, 128], F16)
    iden16 = al("iden16", [128, 128], F16)
    iden32 = al("iden32", [32, 32], F32)
    tri32 = al("tri32", [128, 128], F32)
    ind16 = al("ind16", [32, 4096], F16)
    hl2 = al("hl2", [32, 384], F16)
    sel16 = al("sel16", [128, 64], F16)
    prm = al("prm", [128, 504], F32)
    prm32r = al("prm32r", [32, 4], F32)
    hstate = al("hstate", [64, NH * HD], F16)
    zer32 = al("zer32", [32, 128], F32)
    zer16 = al("zer16", [64, 64], F16)
    zer16h = al("zer16h", [64, 64], F16)
    epsb = al("epsb", [128, 1], F32)
    oneb = al("oneb", [128, 1], F32)
    eps2b = al("eps2b", [128, 1], F32)
    wk8 = [al(f"wk8{i}", [128, 2048], F16) for i in range(3)]
    wk32 = [al(f"wk32{i}", [128, 2048], F16) for i in range(3)]

    PV = {}
    _pb = Bump(504)
    for name, w in [("ln1w", 8), ("ln1b", 8), ("ln2w", 8), ("ln2b", 8),
                    ("b1", 32), ("b2", 8), ("th", 32), ("wr", 32), ("wi", 32),
                    ("mb1", 32), ("mb2", 8), ("nw", 16), ("cw", 72), ("cb", 18),
                    ("Drep", 32), ("b01s", 32), ("b01c", 32), ("Ac", 32),
                    ("Bc", 32), ("hb2", 8), ("DrepP", 16)]:
        PV[name] = _pb.take(w, align=1)
    P = lambda name, w: prm[:, PV[name]:PV[name] + w]
    Pc = lambda name, c: prm[:, PV[name] + c:PV[name] + c + 1]

    NA16 = 38912
    NA32 = 6912
    A16 = al("arena16", [128, NA16], F16)
    A32 = al("arena32", [128, NA32], F32)
    CAST_OFF = 29696
    caststg = [A16[:, CAST_OFF + i * 4096:CAST_OFF + (i + 1) * 4096] for i in range(2)]

    ps = [nc.alloc_psum_tensor(f"ps{i}", [128, 512], F32) for i in range(6)]
    p16 = [nc.alloc_psum_tensor(f"p16{i}", [128, 1024], F16) for i in range(2)]

    s_t = B.sem("s_t")
    s_s = B.sem("s_s")
    s_v = B.sem("s_v")
    g_init = B.sem("g_init")
    g_cast = [B.sem("g_cast0"), B.sem("g_cast1")]
    y_c = B.sem("y_c")
    y_wb = [B.sem("y_wb0"), B.sem("y_wb1")]
    y_k8 = [B.sem(f"y_k8{i}") for i in range(3)]
    y_k32 = [B.sem(f"y_k32{i}") for i in range(3)]
    y_out = B.sem("y_out")
    y_vw = B.sem("y_vw")
    y_vr = B.sem("y_vr")
    y_vr2 = B.sem("y_vr2")
    g_xt = [B.sem(f"g_xt{i}") for i in range(NTT)]

    B.esem = {"t": s_t, "s": s_s, "v": s_v}
    if DBG_SKIP and "globser" in DBG_SKIP:
        KB.SERIAL = [s_t, s_s, s_v, g_init, y_c] + y_k8 + y_k32 + y_wb + g_cast + [y_vw, y_vr, y_out]
    else:
        KB.SERIAL = []
    bankg = {i: [] for i in range(8)}

    def t_mm(bank_id, out_ap, lhsT, rhs, start, stop, waits=(), inc=None, tile_position=None):
        g = tuple(bankg[bank_id]) if start else ()
        if start:
            bankg[bank_id] = []
        return B.E("t", lambda e, a=out_ap, l=lhsT, r=rhs, st=start, sp=stop, tp=tile_position:
                   e.matmul(a, l, r, start=st, stop=sp, tile_position=tp),
                   waits=tuple(waits) + g, inc=inc)

    def t_tr(bank_id, out_ap, in_ap, ident, waits=(), inc=None):
        g = tuple(bankg[bank_id])
        bankg[bank_id] = []
        return B.E("t", lambda e, a=out_ap, i=in_ap, d=ident:
                   e.transpose(a, i, d), waits=tuple(waits) + g, inc=inc)

    def guard(bank_id, ctr, val):
        bankg[bank_id].append((ctr, val))

    def yd(dst, src, waits=(), inc=None):
        return B.E("y", lambda e, d=dst, s=src: e.dma_start(out=d, in_=s),
                   waits=waits, inc=inc, inci=16)

    def gd(dst, src, waits=(), inc=None):
        return B.E("g", lambda e, d=dst, s=src: e.dma_start(out=d, in_=s),
                   waits=waits, inc=inc, inci=16)

    yd(iden32[:, :], c_iden[0:32, 0:32], inc=y_c)
    yd(tri32[:, :], c_tri, inc=y_c)
    for nm, key, w in [("ln1w", "ln1", 16), ("ln2w", "ln2", 16), ("b1", "b1", 32),
                       ("b2", "b2", 8), ("th", "th", 32), ("wr", "wr", 32),
                       ("wi", "wi", 32), ("mb1", "mb1", 32), ("mb2", "mb2", 8),
                       ("nw", "nw", 16), ("cw", "cw", 72), ("cb", "cb", 18),
                       ("Drep", "D", 32), ("DrepP", "DP", 16)]:
        yd(P(nm, w), psrc[key], inc=y_c)
    yd(prm32r[:, 0:1], p_dtb, inc=y_c)
    yd(prm32r[:, 1:2], p_al, inc=y_c)
    YC0 = y_c.n

    gd(ones16[:, :], c_ones, inc=g_init)
    gd(iden16[:, :], c_iden, inc=g_init)
    gd(ind16[:, :], c_ind, inc=g_init)
    gd(sel16[:, :], c_sel, inc=g_init)
    GCONST = g_init.n
    xtile_ready = {}
    GI0 = None

    tmpa = A32[:, 0:32]
    tmpb = A32[:, 32:64]
    s0t = A32[:, 64:96]
    c0t = A32[:, 96:128]
    q1 = A32[:, 128:160]
    q2 = A32[:, 160:192]
    B.W("v", [(y_c, YC0)])
    B.E("v", lambda e: e.tensor_scalar(P("b01s", 32), P("b1", 32), 0.1, None, ALU.mult), inc=s_v)
    B.E("v", lambda e: e.tensor_scalar(P("b01c", 32), P("b1", 32), 0.1, math.pi / 2, ALU.mult, ALU.add), inc=s_v)
    B.E("v", lambda e: e.tensor_scalar(P("hb2", 8), P("b2", 8), 0.5, None, ALU.mult), inc=s_v)
    B.E("v", lambda e: e.memset(A16[:, 0:CAST_OFF], 0.0), inc=s_v)
    B.E("v", lambda e: e.memset(A16[:, CAST_OFF + 8192:NA16], 0.0), inc=s_v)
    B.E("v", lambda e: e.memset(A32[:, :], 0.0), inc=s_v)
    B.dr("v")
    B.E("v", lambda e: e.memset(zer32[:, :], 0.0), inc=s_v)
    B.E("v", lambda e: e.memset(epsb[:, :], EPS), inc=s_v)
    B.E("v", lambda e: e.memset(oneb[:, :], 1.0), inc=s_v)
    B.E("v", lambda e: e.memset(eps2b[:, :], EPS * EPS), inc=s_v)
    B.E("v", lambda e: e.memset(zer16[:, :], 0.0), inc=s_v)
    B.E("v", lambda e: e.memset(zer16h[:, :], 0.0), inc=s_v)
    B.E("v", lambda e: e.memset(hstate[:, :], 0.0), inc=s_v)
    VP0 = s_v.n
    B.W("s", [(s_v, VP0), (y_c, YC0)])
    B.E("s", lambda e: e.activation(tmpa, P("th", 32), AF.Sin, scale=1.0 / 32), inc=s_s)
    B.E("s", lambda e: e.activation(s0t, P("th", 32), AF.Sin, scale=1.0 / 16), inc=s_s)
    B.E("s", lambda e: e.activation(prm32r[:, 2:3], prm32r[:, 1:2], AF.Exp), inc=s_s)
    SP0 = s_s.n
    B.W("v", [(s_s, SP0)])
    B.E("v", lambda e: e.tensor_tensor(tmpb, tmpa, tmpa, ALU.mult), inc=s_v)
    B.dr("v")
    B.E("v", lambda e: e.tensor_scalar(c0t, tmpb, -2.0, 1.0, ALU.mult, ALU.add), inc=s_v)
    B.dr("v")
    sc_pair = [(s0t, c0t), (tmpa, tmpb)]
    for it in range(4):
        so, co = sc_pair[it % 2]
        sn, cn = sc_pair[(it + 1) % 2]
        B.E("v", lambda e, so=so, co=co, sn=sn: e.scalar_tensor_tensor(
            sn, so, 2.0, co, ALU.mult, ALU.mult), inc=s_v)
        B.E("v", lambda e, so=so, cn=cn: e.tensor_tensor(cn, so, so, ALU.mult), inc=s_v)
        B.dr("v")
        B.E("v", lambda e, cn=cn: e.tensor_scalar(cn, cn, -2.0, 1.0, ALU.mult, ALU.add), inc=s_v)
        B.dr("v")
    B.E("v", lambda e: e.tensor_scalar(prm32r[:, 2:3], prm32r[:, 2:3], -1.0, None, ALU.mult), inc=s_v)
    B.E("v", lambda e: e.tensor_tensor(q1, c0t, P("wr", 32), ALU.mult), inc=s_v)
    B.E("v", lambda e: e.tensor_tensor(q2, s0t, P("wi", 32), ALU.mult), inc=s_v)
    B.dr("v")
    B.E("v", lambda e: e.tensor_tensor(q1, q1, q2, ALU.add), inc=s_v)
    B.dr("v")
    B.E("v", lambda e: e.tensor_tensor(q1, q1, P("wi", 32), ALU.mult), inc=s_v)
    B.dr("v")
    B.E("v", lambda e: e.tensor_scalar(q1, q1, 0.1, None, ALU.mult), inc=s_v)
    B.dr("v")
    B.E("v", lambda e: e.tensor_tensor(P("Ac", 32), q1, c0t, ALU.mult), inc=s_v)
    B.E("v", lambda e: e.tensor_tensor(P("Bc", 32), q1, s0t, ALU.mult), inc=s_v)
    VPREP = s_v.n
    Aneg = prm32r[:, 2:3]
    dtbias = prm32r[:, 0:1]

    cast_jobs = []
    for key, tot in [("ffn1", 8 * DFF), ("ffn2", 32 * D), ("ip", 8 * DIPP),
                     ("op", 16 * D), ("m1", 8 * DFF), ("m2", 32 * D)]:
        o = 0
        while o < tot:
            w = min(4096, tot - o)
            cast_jobs.append((key, o, w))
            o += w
    cache_ready = {}
    cache_marks = {k: [] for k in cache}
    xq = list(range(NTT))
    def emit_xtile():
        if xq:
            tt = xq.pop(0)
            for c in range(NC8):
                gd(x[:, c * T + tt * TT:c * T + (tt + 1) * TT],
                   xin[:, c * T + tt * TT:c * T + (tt + 1) * TT], inc=g_xt[tt])
            xtile_ready[tt] = (g_xt[tt], g_xt[tt].n)
    emit_xtile()
    for i, (key, o, w) in enumerate(cast_jobs):
        sl = i % 2
        gwait = [(y_wb[sl], y_wb[sl].n)] if i >= 2 else []
        gd(caststg[sl][:, 0:w], wsrc[key][:, o:o + w], waits=gwait, inc=g_cast[sl])
        dv = yd(cache[key][:, o:o + w], caststg[sl][:, 0:w],
                waits=[(g_cast[sl], g_cast[sl].n)], inc=y_wb[sl])
        cache_marks[key].append((o + w, y_wb[sl], dv))
        cache_ready[key] = ((y_wb[0], y_wb[0].n), (y_wb[1], y_wb[1].n))
        emit_xtile()
    while xq:
        emit_xtile()
    GI0 = g_init.n
    CASTS_DONE = ((y_wb[0], y_wb[0].n), (y_wb[1], y_wb[1].n))

    def cache_wait(key, off, w):
        for end, ctr, val in cache_marks[key]:
            if end >= off + w:
                return [(ctr, val)]
        raise AssertionError("no cache mark")

    class Stream:
        def __init__(self, bufs, sems, cachekey, first_waits):
            self.bufs = bufs
            self.sems = sems
            self.key = cachekey
            self.cons = {0: None, 1: None, 2: None}
            self.i = 0
            self.pend = []
            self.emitted = []
            self.first_waits = first_waits
            self.seen = 0

        def queue(self, off, w):
            self.pend.append((off, w))
            self._try_emit()

        def _try_emit(self):
            while self.pend:
                i = self.i
                sl = i % 3
                if i >= 3 and self.cons[sl] is None:
                    break
                off, w = self.pend.pop(0)
                waits = []
                if i < 3:
                    waits += list(self.first_waits)
                waits += cache_wait(self.key, off, w)
                if self.cons[sl] is not None:
                    waits.append(self.cons[sl])
                    self.cons[sl] = None
                dv = yd(self.bufs[sl][:, 0:w], cache[self.key][:, off:off + w],
                        waits=waits, inc=self.sems[sl])
                self.emitted.append((sl, dv, off))
                self.i += 1

        def unit(self, i):
            assert i < len(self.emitted), "unit not yet emitted"
            sl, dv, off = self.emitted[i]
            return sl, (self.sems[sl], dv)

        def consumed(self, i, ctr, val):
            sl, dv, off = self.emitted[i]
            self.cons[sl] = (ctr, val)
            self._try_emit()

    LN_BASE = NA32 - 2048
    LNm = A32[:, LN_BASE:LN_BASE + 512]
    LNmsq = A32[:, LN_BASE + 512:LN_BASE + 1024]
    LNvar = A32[:, LN_BASE + 1024:LN_BASE + 1536]
    LNrstd = A32[:, LN_BASE + 1536:LN_BASE + 2048]
    SQ_BASE = NA16 - 1024
    sqp = [A16[:, SQ_BASE + i * 512:SQ_BASE + (i + 1) * 512] for i in range(2)]

    LNr12 = A32[:, LN_BASE - 512:LN_BASE]

    def emit_ln(src_fn, out_fn, nch, w, aff=None, b_sum=4, b_sq=5, src_ready=None,
                out_guard=None, also_ln2=None):
        nd = float(nch * 128)
        sr = [src_ready] if src_ready else []
        tm = None
        for c in range(nch):
            tm = t_mm(b_sum, ps[b_sum][:, 0:w], ones16[:, :], src_fn(c),
                      start=(c == 0), stop=(c == nch - 1),
                      waits=(sr if c == 0 else []),
                      inc=(s_t if c == nch - 1 else None))
        sqm = []
        for c in range(nch):
            sl = c % 2
            swaits = [(s_t, sqm[c - 2])] if c >= 2 else list(sr)
            sv = B.E("s", lambda e, c=c, sl=sl: e.activation(sqp[sl][:, 0:w], src_fn(c), AF.Square),
                     waits=swaits, inc=s_s)
            tmq = t_mm(b_sq, ps[b_sq][:, 0:w], ones16[:, :], sqp[sl][:, 0:w],
                       start=(c == 0), stop=(c == nch - 1),
                       waits=[(s_s, sv)], inc=s_t)
            sqm.append(tmq)
        vm = B.E("v", lambda e: e.tensor_scalar(LNm[:, 0:w], ps[b_sum][:, 0:w], 1.0 / nd, None, ALU.mult),
                 waits=[(s_t, tm)], inc=s_v)
        guard(b_sum, s_v, vm)
        B.dr("v")
        B.E("v", lambda e: e.tensor_tensor(LNmsq[:, 0:w], LNm[:, 0:w], LNm[:, 0:w], ALU.mult), inc=s_v)
        B.dr("v")
        vv = B.E("v", lambda e: e.scalar_tensor_tensor(LNvar[:, 0:w], ps[b_sq][:, 0:w], 1.0 / nd,
                                                       LNmsq[:, 0:w], ALU.mult, ALU.subtract),
                 waits=[(s_t, sqm[-1])], inc=s_v)
        guard(b_sq, s_v, vv)
        sv = B.E("s", lambda e: e.activation(LNrstd[:, 0:w], LNvar[:, 0:w], AF.Ln, bias=epsb[:, 0:1]),
                 waits=[(s_v, vv)], inc=s_s)
        B.dr("s")
        sv = B.E("s", lambda e: e.activation(LNrstd[:, 0:w], LNrstd[:, 0:w], AF.Exp, scale=-0.5), inc=s_s)
        if also_ln2 is not None:
            sv2b = B.E("s", lambda e: e.activation(LNr12[:, 0:w], LNvar[:, 0:w], AF.Ln,
                                                   scale=1.0 + EPS, bias=eps2b[:, 0:1]),
                       waits=[(s_v, vv)], inc=s_s)
            B.dr("s")
            sv2b = B.E("s", lambda e: e.activation(LNr12[:, 0:w], LNr12[:, 0:w], AF.Exp, scale=-0.5), inc=s_s)
            B.W("v", [(s_s, sv2b)])
        B.W("v", [(s_s, sv)])
        applies = []
        last = None
        for c in range(nch):
            og = [out_guard] if (out_guard is not None and c == 0) else []
            B.E("v", lambda e, c=c: e.tensor_tensor(LNmsq[:, 0:w], src_fn(c), LNm[:, 0:w], ALU.subtract),
                waits=og, inc=s_v)
            B.dr("v")
            if aff is None:
                last = B.E("v", lambda e, c=c: e.tensor_tensor(out_fn(c), LNmsq[:, 0:w], LNrstd[:, 0:w], ALU.mult), inc=s_v)
            else:
                wn, bn = aff
                B.E("v", lambda e: e.tensor_tensor(LNvar[:, 0:w], LNmsq[:, 0:w], LNrstd[:, 0:w], ALU.mult), inc=s_v)
                B.dr("v")
                last = B.E("v", lambda e, c=c, wn=wn, bn=bn: e.tensor_scalar(
                    out_fn(c), LNvar[:, 0:w], Pc(wn, c), Pc(bn, c), ALU.mult, ALU.add), inc=s_v)
            if also_ln2 is not None:
                out2_fn, wn2, bn2 = also_ln2
                B.E("v", lambda e, c=c: e.tensor_tensor(LNvar[:, 0:w], LNmsq[:, 0:w], LNr12[:, 0:w], ALU.mult), inc=s_v)
                B.dr("v")
                last = B.E("v", lambda e, c=c, out2_fn=out2_fn, wn2=wn2, bn2=bn2: e.tensor_scalar(
                    out2_fn(c), LNvar[:, 0:w], Pc(wn2, c), Pc(bn2, c), ALU.mult, ALU.add), inc=s_v)
            B.dr("v")
            applies.append(last)
        return applies

    def x_sl(c, tt, w=TT):
        return x[:, c * T + tt * TT:c * T + tt * TT + w]

    def ffn_stage(key1, key2, final):
        a16 = Bump(CAST_OFF if not final else SQ_BASE)
        a32 = Bump(LN_BASE - 512, base=256)
        xln2 = []
        for _ in range(2):
            xln = []
            for _ in range(NC8):
                o = a16.take(TT)
                xln.append(A16[:, o:o + TT])
            xln2.append(xln)
        act = []
        for _ in range(NC32):
            o = a16.take(TT)
            act.append(A16[:, o:o + TT])
        s1p, c1p = [], []
        for _ in range(2):
            o = a32.take(TT); s1p.append(A32[:, o:o + TT])
        for _ in range(2):
            o = a32.take(TT); c1p.append(A32[:, o:o + TT])
        o = a32.take(TT); v1 = A32[:, o:o + TT]
        o = a32.take(TT); ev = A32[:, o:o + TT]
        if final:
            o = a32.take(1024)
            ost = A32[:, o:o + 1024]
            xo_r = xout.rearrange("p (c t) -> p c t", c=NC8)

        st1 = Stream(wk8, y_k8, key1, [(s_t, s_t.n)])
        st2 = Stream(wk32, y_k32, key2, [(s_t, s_t.n)])
        q_cons = {0: 0, 1: 0}
        B.W("t", [(g_init, GCONST), (y_c, YC0), (s_v, s_v.n), (s_s, s_s.n)])
        B.W("s", [(g_init, GCONST), (y_c, YC0)])
        vln = {}
        hid_end = {}
        if "nopipe" not in DBG_SKIP:
            emit_ln(lambda c: x_sl(c, 0), lambda c: xln2[0][c], NC8, TT,
                    src_ready=xtile_ready[0])
            vln[0] = s_v.n
        for tt in range(NTT):
            xln = xln2[tt % 2]
            if "nopipe" in DBG_SKIP:
                emit_ln(lambda c, tt=tt: x_sl(c, tt), lambda c, tt=tt: xln2[tt % 2][c],
                        NC8, TT, out_guard=(s_t, hid_end.get(tt - 1, 0)))
                vln[tt] = s_v.n
            for gi in range(16):
                st1.queue(gi * 2048, 2048)
            for fc in range(NC32):
                gi = fc // 2
                sl, (sem, dv) = st1.unit(tt * 16 + gi)
                boff = (fc % 2) * 1024
                bank = fc % 4
                for kc in range(8):
                    w0 = []
                    if kc == 0:
                        w0 = [(sem, dv), (s_v, vln[tt])]
                    t_mm(bank, ps[bank][:, 0:TT],
                         wk8[sl][:, boff + kc * 128:boff + kc * 128 + 128],
                         xln[kc][:, :], start=(kc == 0), stop=(kc == 7),
                         waits=w0, inc=(s_t if kc == 7 else None))
                tmv = s_t.n
                if fc % 2 == 1:
                    st1.consumed(tt * 16 + gi, s_t, tmv)
                sw = [(s_t, tmv)]
                if fc >= 2:
                    sw.append((s_v, q_cons[fc % 2]))
                sv1 = B.E("s", lambda e, b=bank, fc=fc: e.activation(
                    s1p[fc % 2], ps[b][:, 0:TT], AF.Sin, bias=Pc("b01s", fc), scale=0.1),
                    waits=sw, inc=s_s)
                sv2 = B.E("s", lambda e, b=bank, fc=fc: e.activation(
                    c1p[fc % 2], ps[b][:, 0:TT], AF.Sin, bias=Pc("b01c", fc), scale=0.1), inc=s_s)
                vv = B.E("v", lambda e, b=bank, fc=fc: e.scalar_tensor_tensor(
                    v1, s1p[fc % 2], Pc("Ac", fc), ps[b][:, 0:TT], ALU.mult, ALU.add),
                    waits=[(s_s, sv2)], inc=s_v)
                guard(bank, s_v, vv)
                B.dr("v")
                vv2 = B.E("v", lambda e, fc=fc: e.scalar_tensor_tensor(
                    v1, c1p[fc % 2], Pc("Bc", fc), v1, ALU.mult, ALU.mult if False else ALU.add), inc=s_v)
                q_cons[fc % 2] = vv2
                B.dr("v")
                B.E("v", lambda e, fc=fc: e.tensor_scalar(
                    act[fc][:, :], v1, Pc("b1", fc), 0.0, ALU.add, ALU.max), inc=s_v)
                B.dr("v")
            vact = s_v.n
            hid_end[tt] = s_t.n
            if tt + 1 < NTT and "nopipe" not in DBG_SKIP:
                emit_ln(lambda c, tt=tt: x_sl(c, tt + 1), lambda c, tt=tt: xln2[(tt + 1) % 2][c],
                        NC8, TT, out_guard=(s_t, hid_end.get(tt - 1, 0)),
                        src_ready=xtile_ready[tt + 1])
                vln[tt + 1] = s_v.n
            for hdc in range(16):
                st2.queue(hdc * 2048, 2048)
            if final:
                og = (y_out, y_out.n)
            for dc in range(NC8):
                bank = dc % 4
                for kc in range(32):
                    hdc = dc * 2 + kc // 16
                    sl, (sem, dv) = st2.unit(tt * 16 + hdc)
                    w0 = []
                    if kc % 16 == 0:
                        w0 = [(sem, dv)]
                    if kc == 0:
                        w0 = w0 + [(s_v, vact)]
                    t_mm(bank, ps[bank][:, 0:TT],
                         wk32[sl][:, (kc % 16) * 128:(kc % 16) * 128 + 128],
                         act[kc][:, :], start=(kc == 0), stop=(kc == 31),
                         waits=w0, inc=(s_t if kc in (15, 31) else None))
                    if kc == 15:
                        st2.consumed(tt * 16 + dc * 2, s_t, s_t.n)
                tmv = s_t.n
                st2.consumed(tt * 16 + dc * 2 + 1, s_t, tmv)
                vv = B.E("v", lambda e, b=bank, dc=dc: e.tensor_scalar(
                    ev, ps[b][:, 0:TT], 0.5, Pc("hb2", dc), ALU.mult, ALU.add),
                    waits=[(s_t, tmv)], inc=s_v)
                guard(bank, s_v, vv)
                B.dr("v")
                if not final:
                    B.E("v", lambda e, dc=dc, tt=tt: e.tensor_tensor(
                        x_sl(dc, tt), ev, x_sl(dc, tt), ALU.add), inc=s_v)
                else:
                    B.E("v", lambda e, dc=dc, tt=tt: e.tensor_tensor(
                        ost[:, (dc % 2) * TT:(dc % 2) * TT + TT], ev, x_sl(dc, tt), ALU.add),
                        waits=([og] if dc % 2 == 0 else []), inc=s_v)
                    if dc % 2 == 1:
                        yd(xo_r[:, dc - 1:dc + 1, tt * TT:(tt + 1) * TT],
                           ost.rearrange("p (c t) -> p c t", c=2),
                           waits=[(s_v, s_v.n)], inc=y_out)
                        og = (y_out, y_out.n)
                B.dr("v")

    def mamba_stage():
        a16 = Bump(NA16 - 1024)
        o = a16.take(NC8 * TT); u = A16[:, o:o + NC8 * TT]
        o = a16.take(NC8 * TT); uln = A16[:, o:o + NC8 * TT]
        o = a16.take(NC16 * TT); zsilu = A16[:, o:o + NC16 * TT]
        o = a16.take(NC18 * XBCW); xbc0 = o; OFFS["xbc"] = o
        xbc = [A16[:, xbc0 + c * XBCW:xbc0 + (c + 1) * XBCW] for c in range(NC18)]
        o = a16.take(NC18 * 3); sbar = A16[:, o:o + NC18 * 3]
        o = a16.take(NC16 * TT); yn = A16[:, o:o + NC16 * TT]; OFFS["yn"] = o
        o = a16.take(256); cumhi16 = A16[0:32, o:o + 128]
        expcum16 = A16[0:32, o + 128:o + 256]
        o = a16.take(128); cumlo16 = A16[0:32, o:o + 128]
        hiP = [cumhi16, hl2[0:32, 0:128]]
        expP16 = [expcum16, hl2[0:32, 128:256]]
        loP = [cumlo16, hl2[0:32, 256:384]]
        o = a16.take(2048); yq = A16[:, o:o + 2048]; OFFS["yq"] = o

        o = a16.take(32); nthi16 = A16[:, o:o + 32]
        o = a16.take(32); ntlo16 = A16[:, o:o + 32]
        o = a16.take(64); btm16 = A16[:, o:o + 64]
        o = a16.take(256); G16P = A16[:, o:o + 256]
        G16 = [A16[:, o:o + 128], A16[:, o + 128:o + 256]]
        Cd16 = []
        for _ in range(2):
            o = a16.take(128); Cd16.append(A16[0:64, o:o + 128])
        dxp = []
        for _ in range(2):
            o = a16.take(128); dxp.append(A16[:, o:o + 128])
        dxw = []
        for _ in range(2):
            o = a16.take(128); dxw.append(A16[:, o:o + 128])
        sqq = []
        for _ in range(2):
            o = a16.take(128); sqq.append(A16[:, o:o + 128])

        a32 = Bump(LN_BASE, base=256)
        cvp = []
        for _ in range(2):
            o = a32.take(512); cvp.append(A32[:, o:o + 512])
        o = a32.take(512); dtb = A32[0:32, o:o + 512]
        o = a32.take(512); dtraw = A32[0:32, o:o + 512]
        o = a32.take(128); acum = A32[0:32, o:o + 128]
        o = a32.take(128); cum = A32[0:32, o:o + 128]
        o = a32.take(128); rvi = A32[0:32, o:o + 128]
        o = a32.take(128); rve = A32[0:32, o:o + 128]
        o = a32.take(32); nctm = A32[:, o:o + 32]
        o = a32.take(32); wdtm = A32[:, o:o + 32]
        o = a32.take(32); dttm = A32[:, o:o + 32]
        o = a32.take(32); dAt = A32[0:64, o:o + 32]
        o = a32.take(256); spm2 = A32[:, o:o + 256]
        spm = A32[:, o:o + 128]
        spmB = A32[:, o + 128:o + 256]
        e1 = []
        for _ in range(2):
            o = a32.take(128); e1.append(A32[:, o:o + 128])
        o = a32.take(256); DexP = A32[:, o:o + 256]
        Dex = [A32[:, o:o + 128], A32[:, o + 128:o + 256]]
        o = a32.take(128); rstd2 = A32[:, o:o + 128]

        stip = Stream(wk8, y_k8, "ip", [(s_t, s_t.n)])
        stop_ = Stream(wk32, y_k32, "op", [(s_t, s_t.n)])
        B.W("t", [(s_v, s_v.n), (s_s, s_s.n), (g_init, GI0), (y_c, YC0)] + list(CASTS_DONE))
        B.W("v", list(CASTS_DONE))
        B.W("s", list(CASTS_DONE) + [(g_init, GI0), (y_c, YC0)])

        B.E("v", lambda e: e.memset(yn[:, :], 0.0), inc=s_v)
        B.dr("v")
        nunit_ip = 18
        vstate_prev = {p: VP0 for p in range(16)}
        rms_prev = 0
        sv4_prev = 0
        for sc in range(NTT):
            seq_start = (sc % 4 == 0)
            if sc > 0:
                B.W("v", [(y_vw, y_vw.n)])
            uln_ap = emit_ln(lambda c, sc=sc: x_sl(c, sc), lambda c: u[:, c * TT:(c + 1) * TT],
                             NC8, TT, also_ln2=(lambda c: uln[:, c * TT:(c + 1) * TT], "ln1w", "ln1b"),
                             src_ready=xtile_ready[sc])
            vuln = s_v.n
            if seq_start:
                B.E("v", lambda e: e.memset(sbar[:, :], 0.0), inc=s_v)
                B.dr("v")
            for c in range(NC18):
                B.E("v", lambda e, c=c: e.tensor_copy(
                    xbc[c][:, 0:3], sbar[:, c * 3:c * 3 + 3]), inc=s_v)
            B.dr("v")
            uorder = list(range(8, 18)) + list(range(0, 8))
            posmap = {ui: i for i, ui in enumerate(uorder)}
            for ui in uorder:
                w = 2048 if ui < 17 else 1024
                stip.queue(ui * 2048, w)
            ip_evac = {}
            dt_done = None
            for oc in list(range(16, 35)) + list(range(0, 16)):
                ui = oc // 2
                sl, (sem, dv) = stip.unit(sc * nunit_ip + posmap[ui])
                boff = (oc % 2) * 1024
                bank = oc % 4
                for kc in range(8):
                    w0 = [(s_v, uln_ap[kc])]
                    if kc == 0:
                        w0.append((sem, dv))
                    t_mm(bank, ps[bank][:, 0:TT],
                         wk8[sl][:, boff + kc * 128:boff + kc * 128 + 128],
                         uln[:, kc * TT:(kc + 1) * TT], start=(kc == 0), stop=(kc == 7),
                         waits=w0, inc=(s_t if kc == 7 else None))
                tmv = s_t.n
                if oc % 2 == 1 or oc == 34:
                    stip.consumed(sc * nunit_ip + posmap[ui], s_t, tmv)
                if oc < 16:
                    sv = B.E("s", lambda e, b=bank, oc=oc: e.activation(
                        zsilu[:, oc * TT:(oc + 1) * TT], ps[b][:, 0:TT], AF.Silu),
                        waits=[(s_t, tmv)], inc=s_s)
                elif oc < 34:
                    c = oc - 16
                    sv = B.E("s", lambda e, b=bank, c=c: e.activation(
                        xbc[c][:, 3:XBCW], ps[b][:, 0:TT], AF.Copy),
                        waits=[(s_t, tmv)], inc=s_s)
                    ip_evac[c] = sv
                else:
                    vv = B.E("v", lambda e, b=bank: e.tensor_scalar(
                        dtraw, ps[b][0:32, 0:TT], dtbias, 80.0, ALU.add, ALU.min),
                        waits=[(s_t, tmv)], inc=s_v)
                    sv = B.E("s", lambda e: e.activation(dtraw, dtraw, AF.Exp),
                             waits=[(s_v, vv)], inc=s_s)
                    B.dr("s")
                    sv = B.E("s", lambda e: e.activation(dtb, dtraw, AF.Ln, bias=oneb[0:32, 0:1]),
                             inc=s_s)
                    dt_done = sv
                    guard(bank, s_v, vv)
                if oc < 34:
                    guard(bank, s_s, sv)
            conv_done = {}
            for c in range(NC18):
                cva = cvp[c % 2]
                w0 = [(s_s, ip_evac[c])]
                if c >= 2:
                    w0.append((s_s, conv_done[c - 2]))
                B.E("v", lambda e, c=c, cva=cva: e.tensor_scalar(
                    cva, xbc[c][:, 0:512], Pc("cw", c * 4 + 0), None, ALU.mult),
                    waits=w0, inc=s_v)
                for k in range(1, 4):
                    B.dr("v")
                    B.E("v", lambda e, c=c, k=k, cva=cva: e.scalar_tensor_tensor(
                        cva, xbc[c][:, k:k + 512], Pc("cw", c * 4 + k), cva,
                        ALU.mult, ALU.add), inc=s_v)
                B.dr("v")
                vv2 = B.E("v", lambda e, c=c: e.tensor_copy(
                    sbar[:, c * 3:c * 3 + 3], xbc[c][:, 512:515]), inc=s_v)
                sv = B.E("s", lambda e, c=c, cva=cva: e.activation(
                    xbc[c][:, 3:XBCW], cva, AF.Silu, bias=Pc("cb", c)),
                    waits=[(s_v, vv2)], inc=s_s)
                conv_done[c] = sv
            xs_done = [conv_done[c] for c in range(16)]
            bc_done = max(conv_done[16], conv_done[17])

            def emit_prefix(qq, w_first):
                IB = qq % 2
                B.E("v", lambda e, qq=qq: e.tensor_scalar(
                    acum, dtb[:, qq * 128:(qq + 1) * 128], Aneg, None, ALU.mult),
                    waits=w_first, inc=s_v)
                B.dr("v")
                B.E("v", lambda e: e.tensor_tensor_scan(
                    cum, acum, zer32[:, :], 0.0, ALU.add, ALU.add), inc=s_v)
                B.dr("v")
                B.E("v", lambda e: e.tensor_tensor_scan(
                    rvi[:, 127::-1], acum[:, 127::-1], zer32[:, :], 0.0, ALU.add, ALU.add), inc=s_v)
                B.dr("v")
                B.E("v", lambda e: e.tensor_tensor(rve, rvi, acum, ALU.subtract), inc=s_v)
                B.dr("v")
                B.E("v", lambda e, IB=IB: e.tensor_copy(hiP[IB], cum), inc=s_v)
                B.dr("v")
                vs = B.E("v", lambda e, IB=IB: e.tensor_tensor(
                    loP[IB], cum, hiP[IB], ALU.subtract), inc=s_v)
                B.dr("v")
                return vs

            vset_ms = {}
            vset_ms[0] = emit_prefix(0, [(s_s, dt_done), (s_t, s_t.n), (s_s, sv4_prev)])
            vdx_prev = {}
            for q in range(DBG_NQ[0]):
                if "dumph" in DBG_SKIP and sc == 0:
                    gd(dbg[0:64, q * 2048:q * 2048 + 2048], hstate[:, :],
                       waits=[(s_v, s_v.n), (s_t, s_t.n)], inc=g_init)
                    B.W("v", [(g_init, g_init.n)])
                    B.W("t", [(g_init, g_init.n)])
                if "dumpyn01" in DBG_SKIP and sc == 0 and q == min(2, DBG_NQ[0] - 1) and False:
                    gd(dbg[:, 0:8192], yn[:, :], waits=[(s_v, s_v.n)], inc=g_init)
                    B.W("v", [(g_init, g_init.n)])
                    B.W("s", [(g_init, g_init.n)])
                if "qfence" in DBG_SKIP:
                    B.W("t", [(s_v, s_v.n), (s_s, s_s.n)])
                    B.W("v", [(s_t, s_t.n), (s_s, s_s.n)])
                    B.W("s", [(s_t, s_t.n), (s_v, s_v.n)])
                qo = 3 + q * 128
                vsetup = vset_ms[q]
                t_tr(2, ps[2][:, 0:32], cum, iden32[0:32, 0:32], waits=[(s_v, vsetup)], inc=s_t)
                t_tr(2, ps[2][:, 32:64], rve, iden32[0:32, 0:32], inc=s_t)
                tma = t_tr(2, ps[2][:, 64:96], dtb[:, q * 128:(q + 1) * 128], iden32[0:32, 0:32], inc=s_t)
                w0 = [(s_s, bc_done)]
                if vdx_prev.get(14) is not None:
                    w0.append((s_v, vdx_prev[14]))
                tmb = t_tr(6, p16[0][:, 256:320], xbc[16][0:64, qo:qo + 128], iden16[0:64, 0:64],
                           waits=w0, inc=s_t)
                sv1 = B.E("s", lambda e: e.activation(nctm, ps[2][:, 0:32], AF.Copy),
                          waits=[(s_t, tma)], inc=s_s)
                sv2 = B.E("s", lambda e: e.activation(wdtm, ps[2][:, 32:64], AF.Exp), inc=s_s)
                sv3 = B.E("s", lambda e: e.activation(dttm, ps[2][:, 64:96], AF.Copy), inc=s_s)
                sv4 = B.E("s", lambda e, q=q: e.activation(expP16[q % 2], cum, AF.Exp),
                          waits=[(s_v, vsetup)], inc=s_s)
                vbt = B.E("v", lambda e: e.tensor_copy(btm16, p16[0][:, 256:320]),
                          waits=[(s_t, tmb)], inc=s_v)
                guard(6, s_v, vbt)
                guard(2, s_s, sv3)
                tms = t_mm(2, ps[2][:, 128:256], xbc[16][0:64, qo:qo + 128],
                           xbc[17][0:64, qo:qo + 128], start=True, stop=True,
                           waits=[(s_s, bc_done)], inc=s_t)
                B.E("v", lambda e: e.tensor_tensor(spm, ps[2][:, 128:256], tri32[:, :], ALU.mult),
                    waits=[(s_t, tms)], inc=s_v)
                B.dr("v")
                vsp = B.E("v", lambda e: e.tensor_tensor(spmB, ps[2][:, 128:256], tri32[:, :], ALU.mult),
                          inc=s_v)
                B.dr("v")
                vv = B.E("v", lambda e: e.tensor_copy(nthi16, nctm),
                         waits=[(s_s, sv1)], inc=s_v)
                B.dr("v")
                vnt = B.E("v", lambda e: e.tensor_tensor(ntlo16, nctm, nthi16, ALU.subtract), inc=s_v)
                guard(2, s_v, vsp)
                t_mm(2, ps[2][0:64, 256:288], sel16[:, :], nthi16,
                     start=True, stop=False, waits=[(s_v, vnt)])
                tmd = t_mm(2, ps[2][0:64, 256:288], sel16[:, :], ntlo16,
                           start=False, stop=True, inc=s_t)
                svA = B.E("s", lambda e: e.activation(dAt, ps[2][0:64, 256:288], AF.Exp),
                          waits=[(s_t, tmd)], inc=s_s)
                guard(2, s_s, svA)

                tslot = {}
                tD = {}
                tTR = {}
                tY = {}
                tH = {}
                vG = {}
                vCd = {}
                vdx = {}
                vdxw = {}
                vevac = {}
                vstate = {}
                sD = {}
                ve1 = {}
                for s in range(34):
                    serR = any(f.startswith("serR") and
                               int(f[4:].split("_")[0]) <= s < int(f[4:].split("_")[1])
                               for f in DBG_SKIP)
                    if "serial" in DBG_SKIP or serR:
                        B.W("t", [(s_v, s_v.n), (s_s, s_s.n)])
                        B.W("v", [(s_t, s_t.n), (s_s, s_s.n)])
                        B.W("s", [(s_t, s_t.n), (s_v, s_v.n)])
                    if "sert" in DBG_SKIP:
                        B.W("t", [(s_s, s_s.n)])
                    if "servt" in DBG_SKIP:
                        B.W("t", [(s_v, s_v.n)])
                    if "serv" in DBG_SKIP:
                        B.W("v", [(s_t, s_t.n), (s_s, s_s.n)])
                    if "servT" in DBG_SKIP:
                        B.W("v", [(s_t, s_t.n)])
                    if "servS" in DBG_SKIP:
                        B.W("v", [(s_s, s_s.n)])
                    if "sers" in DBG_SKIP:
                        B.W("s", [(s_t, s_t.n), (s_v, s_v.n)])
                    if s < 32:
                        bk = s % 2
                        w0 = [(s_v, vsetup), (s_s, sv4)]
                        if s >= 2:
                            w0.append((s_v, vCd[s - 2]))
                            w0.append((s_s, sD[s - 2]))
                            if "srelu" not in DBG_SKIP:
                                w0.append((s_v, ve1[s - 2]))
                        t_mm(bk, ps[bk][:, 0:128], ind16[:, s * 128:(s + 1) * 128],
                             hiP[q % 2], start=True, stop=False, waits=w0)
                        t_mm(bk, ps[bk][:, 0:128], ind16[:, s * 128:(s + 1) * 128],
                             loP[q % 2], start=False, stop=True)
                        t_mm(bk, ps[bk][:, 128:256], ind16[:, s * 128:(s + 1) * 128],
                             expP16[q % 2], start=True, stop=True, inc=s_t)
                        tD[s] = s_t.n
                    if s % 2 == 0 and s < 32:
                        p = s // 2
                        pbk = 6 + p % 2
                        w0 = [(s_s, xs_done[p])]
                        if p >= 2:
                            w0.append((s_v, vdx[p - 2]))
                        elif vdx_prev.get(p + 14) is not None:
                            w0.append((s_v, vdx_prev[p + 14]))
                        t_tr(pbk, p16[p % 2][:, 0:128], xbc[p][:, qo:qo + 128],
                             iden16[:, :], waits=w0, inc=s_t)
                        tTR[p] = s_t.n
                    if s >= 2 and (s - 2) % 2 == 0 and s - 2 < 32:
                        vv = B.E("v", lambda e: e.tensor_tensor(
                            G16P, DexP, spm2, ALU.mult),
                            waits=[(s_s, sD[s - 1]), (s_v, vsp)], inc=s_v)
                        vG[s - 2] = vG[s - 1] = vv
                    if s >= 2:
                        hh = s - 2
                        ybk = 4 + (hh // 2) % 2
                        rows = slice(64, 128) if hh % 2 == 1 else slice(0, 64)
                        tp = (0, 64) if hh % 2 == 1 else None
                        w0 = [(s_v, vG[hh])]
                        if hh >= 4:
                            w0.append((s_v, vevac[hh - 4]))
                        skip2 = "y2" in DBG_SKIP
                        skip1 = "y1" in DBG_SKIP
                        if skip1:
                            B.W("t", w0)
                            B.E("t", lambda e, bk2=ybk, rows2=rows, tp2=tp: e.matmul(
                                ps[bk2][rows2, 0:128],
                                zer16h[:, 0:64], zer16[0:64, 0:128],
                                start=True, stop=False, tile_position=tp2))
                        else:
                            t_mm(ybk, ps[ybk][rows, 0:128],
                                 dxp[(hh // 2) % 2][:, (hh % 2) * 64:(hh % 2) * 64 + 64],
                                 G16[hh % 2][:, :], start=True, stop=skip2,
                                 waits=w0, tile_position=tp)
                        if skip2:
                            w1 = [(s_v, vCd[hh]), (s_v, vstate_prev[hh // 2])]
                            t_mm(ybk, ps[ybk][rows, 0:128], hstate[:, hh * 64:(hh + 1) * 64],
                                 Cd16[hh % 2][:, :], start=False, stop=True,
                                 waits=w1, inc=s_t, tile_position=tp) if False else None
                            B.W("t", w1)
                            s_t.n += 0
                            tdummy = B.E("t", lambda e, bk2=ybk, rows2=rows, tp2=tp: e.matmul(
                                ps[bk2][rows2, 256:384], zer16h[:, 0:64], zer16[0:64, 0:128],
                                start=True, stop=True, tile_position=tp2), inc=s_t)
                        else:
                            w1 = [(s_v, vCd[hh]), (s_v, vstate_prev[hh // 2])]
                            t_mm(ybk, ps[ybk][rows, 0:128], hstate[:, hh * 64:(hh + 1) * 64],
                                 Cd16[hh % 2][:, :], start=False, stop=True,
                                 waits=w1, inc=s_t, tile_position=tp)
                        tY[hh] = s_t.n
                        if hh % 2 == 1:
                            p = hh // 2
                            hbk = 2 + p % 2
                            t_mm(hbk, ps[hbk][0:64, 384:512], btm16[:, :],
                                 dxw[p % 2][:, :], start=True, stop=True,
                                 waits=[(s_v, vdxw[p])], inc=s_t)
                            tH[p] = s_t.n
                    tslot[s] = s_t.n

                    if s < 32 and "srelu" not in DBG_SKIP:
                        bk = s % 2
                        w0 = [(s_t, tD[s]), (s_s, sv1)]
                        if s >= 2:
                            w0.append((s_s, sD[s - 2]))
                            w0.append((s_v, vG[s - 2]))
                        ve1[s] = B.E("v", lambda e, bk=bk, s=s: e.tensor_scalar(
                            e1[s % 2], ps[bk][:, 0:128], nctm[:, s:s + 1], 0.0,
                            ALU.subtract, ALU.min), waits=w0, inc=s_v)
                        B.dr("v")
                    if s < 32:
                        bk = s % 2
                        w0 = [(s_t, tD[s])]
                        if s >= 2:
                            w0.append((s_t, tY[s - 2]))
                        vCd[s] = B.E("v", lambda e, bk=bk, s=s, qo=qo: e.tensor_tensor(
                            Cd16[s % 2], xbc[17][0:64, qo:qo + 128], ps[bk][0:64, 128:256],
                            ALU.mult), waits=w0, inc=s_v)
                    if s % 2 == 0 and s < 32:
                        p = s // 2
                        w0 = [(s_t, tslot[s]), (s_s, sv3)]
                        for par in range(2):
                            vdx0 = B.E("v", lambda e, p=p, par=par: e.tensor_scalar(
                                dxp[p % 2][:, par * 64:par * 64 + 64],
                                p16[p % 2][:, par * 64:par * 64 + 64],
                                dttm[:, 2 * p + par:2 * p + par + 1], None, ALU.mult),
                                waits=(w0 if par == 0 else []), inc=s_v)
                        vdx[p] = vdx0
                        B.dr("v")
                        w0 = [(s_s, sv2)]
                        for par in range(2):
                            vdxw0 = B.E("v", lambda e, p=p, par=par: e.tensor_scalar(
                                dxw[p % 2][:, par * 64:par * 64 + 64],
                                dxp[p % 2][:, par * 64:par * 64 + 64],
                                wdtm[:, 2 * p + par:2 * p + par + 1], None, ALU.mult),
                                waits=(w0 if par == 0 else []), inc=s_v)
                        vdxw[p] = vdxw0
                        B.dr("v")
                    if s >= 2:
                        hh = s - 2
                        if hh % 2 == 1:
                            p2 = hh // 2
                            ybk2 = 4 + p2 % 2
                            vv = B.E("v", lambda e, p2=p2, ybk2=ybk2, qo=qo:
                                     e.scalar_tensor_tensor(
                                yq[:, p2 * 128:p2 * 128 + 128], xbc[p2][:, qo:qo + 128],
                                P("DrepP", 16)[:, p2:p2 + 1], ps[ybk2][:, 0:128],
                                ALU.mult, ALU.add), waits=[(s_t, tY[hh])], inc=s_v)
                            vevac[2 * p2] = vevac[2 * p2 + 1] = vv
                            guard(ybk2, s_v, vv)
                        if hh % 2 == 1:
                            p = hh // 2
                            hbk = 2 + p % 2
                            vst = None
                            for par in range(2):
                                vst = B.E("v", lambda e, p=p, par=par, hbk=hbk:
                                          e.scalar_tensor_tensor(
                                    hstate[:, (2 * p + par) * 64:(2 * p + par) * 64 + 64],
                                    hstate[:, (2 * p + par) * 64:(2 * p + par) * 64 + 64],
                                    dAt[:, 2 * p + par:2 * p + par + 1],
                                    ps[hbk][0:64, 384 + par * 64:384 + par * 64 + 64],
                                    ALU.mult, ALU.add),
                                    waits=([(s_t, tH[p]), (s_s, svA)] if par == 0 else []),
                                    inc=s_v)
                                B.dr("v")
                            vstate[p] = vst
                            guard(hbk, s_v, vst)

                    if s == 26 and q + 1 < DBG_NQ[0]:
                        vset_ms[q + 1] = emit_prefix(
                            q + 1, [(s_s, dt_done), (s_t, tD[0]), (s_s, sv4)])
                    if s < 32 and "srelu" not in DBG_SKIP:
                        sD[s] = B.E("s", lambda e, s=s: e.activation(
                            Dex[s % 2], e1[s % 2], AF.Exp),
                            waits=[(s_v, ve1[s])], inc=s_s)
                    elif s < 32:
                        bk = s % 2
                        w0 = [(s_t, tslot[s]), (s_s, sv1)]
                        if s >= 2:
                            w0.append((s_v, vG[s - 2]))
                        B.E("s", lambda e, bk=bk, s=s: e.activation(
                            e1[s % 2], ps[bk][:, 0:128], AF.Relu, scale=-1.0,
                            bias=nctm[:, s:s + 1]), waits=w0, inc=s_s)
                        B.dr("s")
                        sD[s] = B.E("s", lambda e, s=s: e.activation(
                            Dex[s % 2], e1[s % 2], AF.Exp, scale=-1.0), inc=s_s)
                vstate_prev = dict(vstate)
                vdx_prev = {14: vdx[14], 15: vdx[15]}
                guard(0, s_v, vCd[30])
                guard(1, s_v, vCd[31])
                guard(0, s_s, sD[30])
                guard(1, s_s, sD[31])
                if "srelu" not in DBG_SKIP:
                    guard(0, s_v, ve1[30])
                    guard(1, s_v, ve1[31])

                if "dumpyn" in DBG_SKIP and q == 0 and sc == NTT - 1:
                    gd(dbg[:, 10240:10240 + 2048], yq[:, :], waits=[(s_v, s_v.n)], inc=g_init)
                    B.W("v", [(g_init, g_init.n)])
                B.dr("v")
                vgate = {}
                for c in range(NC16):
                    vgate[c] = B.E("v", lambda e, c=c, q=q: e.tensor_tensor(
                        yq[:, c * 128:(c + 1) * 128], yq[:, c * 128:(c + 1) * 128],
                        zsilu[:, c * TT + q * 128:c * TT + q * 128 + 128], ALU.mult),
                        waits=([(s_v, vevac[30]), (s_v, vevac[31])] if c == 0 else []),
                        inc=s_v)
                    B.dr("v")
                if "dumpyn" in DBG_SKIP and q == 0 and sc == NTT - 1:
                    gd(dbg[:, 12288:12288 + 2048], yq[:, :], waits=[(s_v, s_v.n)], inc=g_init)
                    B.W("v", [(g_init, g_init.n)])
                    B.W("s", [(g_init, g_init.n)])
                rms_mm = []
                for c in range(NC16):
                    sl = c % 2
                    w0 = [(s_v, vgate[c])]
                    if c >= 2:
                        w0.append((s_t, rms_mm[c - 2]))
                    elif rms_prev:
                        w0.append((s_t, rms_prev))
                    sv = B.E("s", lambda e, c=c, sl=sl: e.activation(
                        sqq[sl], yq[:, c * 128:(c + 1) * 128], AF.Square),
                        waits=w0, inc=s_s)
                    rmsm = t_mm(0, ps[0][:, 0:128], ones16[:, :], sqq[sl],
                                start=(c == 0), stop=(c == NC16 - 1),
                                waits=[(s_s, sv)], inc=s_t)
                    rms_mm.append(rmsm)
                rms_prev = rms_mm[-1]
                sv = B.E("s", lambda e: e.activation(rstd2, ps[0][:, 0:128], AF.Ln,
                                                     scale=1.0 / DI, bias=epsb[:, 0:1]),
                         waits=[(s_t, rms_mm[-1])], inc=s_s)
                B.dr("s")
                sv = B.E("s", lambda e: e.activation(rstd2, rstd2, AF.Exp, scale=-0.5), inc=s_s)
                guard(0, s_s, sv)
                for c in range(NC16):
                    B.E("v", lambda e, c=c, q=q: e.scalar_tensor_tensor(
                        yn[:, c * TT + q * 128:c * TT + q * 128 + 128],
                        yq[:, c * 128:(c + 1) * 128], Pc("nw", c), rstd2,
                        ALU.mult, ALU.mult), waits=([(s_s, sv)] if c == 0 else []), inc=s_v)
                B.dr("v")
                if "dumpyn" in DBG_SKIP and q == 0 and sc == NTT - 1:
                    gd(dbg[:, 0:8192], yn[:, :], waits=[(s_v, s_v.n)], inc=g_init)
                    gd(dbg[:, 8192:8192 + 2048], yq[:, :], waits=[(s_v, s_v.n)], inc=g_init)
                    B.W("v", [(g_init, g_init.n)])
                    B.W("s", [(g_init, g_init.n)])
                if "dumpyn01" in DBG_SKIP and sc == 0 and q == 1:
                    gd(dbg[:, 0:8192], yn[:, :], waits=[(s_v, s_v.n)], inc=g_init)
                    B.W("v", [(g_init, g_init.n)])
                    B.W("s", [(g_init, g_init.n)])
                    B.W("t", [(g_init, g_init.n)])
            vyn = s_v.n
            sv4_prev = sv4

            for dc in range(NC8):
                stop_.queue(dc * 2048, 2048)
            for dc in range(NC8):
                sl, (sem, dv) = stop_.unit(sc * NC8 + dc)
                bank = dc % 4
                for kc in range(16):
                    w0 = []
                    if kc == 0:
                        w0 = [(sem, dv), (s_v, vyn)]
                    t_mm(bank, ps[bank][:, 0:TT],
                         wk32[sl][:, kc * 128:(kc + 1) * 128],
                         yn[:, kc * TT:(kc + 1) * TT], start=(kc == 0), stop=(kc == 15),
                         waits=w0, inc=(s_t if kc == 15 else None))
                tmv = s_t.n
                stop_.consumed(sc * NC8 + dc, s_t, tmv)
                vv = B.E("v", lambda e, b=bank, dc=dc: e.tensor_tensor(
                    u[:, dc * TT:(dc + 1) * TT], ps[b][:, 0:TT],
                    u[:, dc * TT:(dc + 1) * TT], ALU.add),
                    waits=[(s_t, tmv)], inc=s_v)
                guard(bank, s_v, vv)
                B.dr("v")
            vb_r = vbuf.rearrange("p (c t) -> p c t", c=NC8)
            yd(vb_r[:, :, sc * TT:(sc + 1) * TT],
               u.rearrange("p (c t) -> p c t", c=NC8),
               waits=[(s_v, s_v.n)], inc=y_vw)
            if sc == 3:
                vrst = B.E("v", lambda e: e.memset(hstate[:, :], 0.0),
                           waits=[(s_t, s_t.n)], inc=s_v)
                B.dr("v")
                vstate_prev = {p: vrst for p in range(16)}

    def mlp_stage():
        a16 = Bump(NA16 - 1024)
        vt = []
        for _ in range(2):
            o = a16.take(NC8 * TT); vt.append(A16[:, o:o + NC8 * TT])
        h2b = []
        for _ in range(2):
            o = a16.take(NC8 * TT); h2b.append(A16[:, o:o + NC8 * TT])
        act = []
        for _ in range(NC32):
            o = a16.take(TT)
            act.append(A16[:, o:o + TT])
        a32 = Bump(LN_BASE - 512, base=256)
        o = a32.take(TT); ev = A32[:, o:o + TT]

        st1 = Stream(wk8, y_k8, "m1", [(s_t, s_t.n)])
        st2 = Stream(wk32, y_k32, "m2", [(s_t, s_t.n)])
        vb_r = vbuf.rearrange("p (c t) -> p c t", c=NC8)
        vwall = y_vw.n
        vt_cons = {0: None, 1: None}
        vsem = [y_vr, y_vr2]
        vload = {}
        for tt in range(2):
            w0 = [(y_vw, vwall)]
            vload[tt] = (vsem[tt % 2], yd(vt[tt % 2].rearrange("p (c t) -> p c t", c=NC8),
                         vb_r[:, :, tt * TT:(tt + 1) * TT], waits=w0, inc=vsem[tt % 2]))
        B.W("v", [vload[0]])
        B.W("s", [vload[0]])
        B.W("t", [vload[0], (s_v, s_v.n), (s_s, s_s.n)])
        vln = {}
        hid_end = {}
        emit_ln(lambda c: vt[0][:, c * TT:(c + 1) * TT],
                lambda c: h2b[0][:, c * TT:(c + 1) * TT], NC8, TT, aff=("ln2w", "ln2b"),
                src_ready=vload[0])
        vln[0] = s_v.n
        for tt in range(NTT):
            sl = tt % 2
            h2 = h2b[sl]
            for gi in range(16):
                st1.queue(gi * 2048, 2048)
            for fc in range(NC32):
                gi = fc // 2
                slw, (sem, dv) = st1.unit(tt * 16 + gi)
                boff = (fc % 2) * 1024
                bank = fc % 4
                for kc in range(8):
                    w0 = []
                    if kc == 0:
                        w0 = [(sem, dv), (s_v, vln[tt])]
                    t_mm(bank, ps[bank][:, 0:TT],
                         wk8[slw][:, boff + kc * 128:boff + kc * 128 + 128],
                         h2[:, kc * TT:(kc + 1) * TT], start=(kc == 0), stop=(kc == 7),
                         waits=w0, inc=(s_t if kc == 7 else None))
                tmv = s_t.n
                if fc % 2 == 1:
                    st1.consumed(tt * 16 + gi, s_t, tmv)
                sv = B.E("s", lambda e, b=bank, fc=fc: e.activation(
                    act[fc][:, :], ps[b][:, 0:TT], AF.Gelu, bias=Pc("mb1", fc)),
                    waits=[(s_t, tmv)], inc=s_s)
                guard(bank, s_s, sv)
            vact = s_s.n
            hid_end[tt] = s_t.n
            if tt + 1 < NTT:
                emit_ln(lambda c, tt=tt: vt[(tt + 1) % 2][:, c * TT:(c + 1) * TT],
                        lambda c, tt=tt: h2b[(tt + 1) % 2][:, c * TT:(c + 1) * TT],
                        NC8, TT, aff=("ln2w", "ln2b"),
                        src_ready=vload[tt + 1],
                        out_guard=(s_t, hid_end.get(tt - 1, 0)))
                vln[tt + 1] = s_v.n
            for hdc in range(16):
                st2.queue(hdc * 2048, 2048)
            for dc in range(NC8):
                bank = dc % 4
                for kc in range(32):
                    hdc = dc * 2 + kc // 16
                    slw, (sem, dv) = st2.unit(tt * 16 + hdc)
                    w0 = []
                    if kc % 16 == 0:
                        w0 = [(sem, dv)]
                    if kc == 0:
                        w0 = w0 + [(s_s, vact)]
                    t_mm(bank, ps[bank][:, 0:TT],
                         wk32[slw][:, (kc % 16) * 128:(kc % 16) * 128 + 128],
                         act[kc][:, :], start=(kc == 0), stop=(kc == 31),
                         waits=w0, inc=(s_t if kc in (15, 31) else None))
                    if kc == 15:
                        st2.consumed(tt * 16 + dc * 2, s_t, s_t.n)
                tmv = s_t.n
                st2.consumed(tt * 16 + dc * 2 + 1, s_t, tmv)
                vv = B.E("v", lambda e, b=bank, dc=dc: e.tensor_scalar(
                    ev, ps[b][:, 0:TT], Pc("mb2", dc), None, ALU.add),
                    waits=[(s_t, tmv)], inc=s_v)
                guard(bank, s_v, vv)
                B.dr("v")
                B.E("v", lambda e, dc=dc, sl=sl: e.tensor_tensor(
                    ev, ev, vt[sl][:, dc * TT:(dc + 1) * TT], ALU.add), inc=s_v)
                B.dr("v")
                B.E("v", lambda e, dc=dc, tt=tt: e.tensor_tensor(
                    x_sl(dc, tt), ev, x_sl(dc, tt), ALU.add), inc=s_v)
                B.dr("v")
            vt_cons[sl] = (s_v, s_v.n)
            if tt + 2 < NTT:
                vload[tt + 2] = (vsem[sl], yd(vt[sl].rearrange("p (c t) -> p c t", c=NC8),
                                 vb_r[:, :, (tt + 2) * TT:(tt + 3) * TT],
                                 waits=[(y_vw, vwall), vt_cons[sl]], inc=vsem[sl]))

    if "ffn1" in stages:
        ffn_stage("ffn1", "ffn2", final=False)
    if "mamba" in stages:
        mamba_stage()
    if "mlp" in stages:
        mlp_stage()
    if "ffn2" in stages:
        ffn_stage("ffn1", "ffn2", final=True)
        B.W("y", [(y_out, y_out.n)])
    else:
        B.W("g", [(s_v, s_v.n), (s_t, s_t.n), (s_s, s_s.n)])
        for c in range(NC8):
            gd(xout[:, c * T:(c + 1) * T], x[:, c * T:(c + 1) * T], inc=g_init)
        B.W("y", [(g_init, g_init.n)])

    from contextlib import ExitStack
    with ExitStack() as ctx:
        blk = ctx.enter_context(nc.Block())
        B.finalize(blk)
    return nc


def make_inputs(I):
    ones16, iden, tri, ind, sel127 = build_consts()
    Wt = np.ascontiguousarray(np.asarray(I["L0_attn_in_proj_w"], np.float32).T)
    Wpad = np.zeros((D, DIPP), np.float32)
    Wpad[:, 0:4096] = Wt[:, 0:4096]
    Wpad[:, 32 * 128:32 * 128 + 64] = Wt[:, 4096:4160]
    Wpad[:, 33 * 128:33 * 128 + 64] = Wt[:, 4160:4224]
    Wpad[:, 34 * 128:34 * 128 + 32] = Wt[:, 4224:4256]
    cw = np.asarray(I["L0_attn_conv_w"], np.float32)
    cw_pad = np.zeros((18 * 128, 4), np.float32)
    cw_pad[0:2048] = cw[0:2048]
    cw_pad[16 * 128:16 * 128 + 64] = cw[2048:2112]
    cw_pad[17 * 128:17 * 128 + 64] = cw[2112:2176]
    cb = np.asarray(I["L0_attn_conv_b"], np.float32)
    cb_pad = np.zeros((18 * 128,), np.float32)
    cb_pad[0:2048] = cb[0:2048]
    cb_pad[16 * 128:16 * 128 + 64] = cb[2048:2112]
    cb_pad[17 * 128:17 * 128 + 64] = cb[2112:2176]
    common = {
        "wffn1": pack_w(np.asarray(I["L0_ffn_W1"], np.float32)),
        "wffn2": pack_w(np.asarray(I["L0_ffn_W2"], np.float32)),
        "wip": pack_w(Wpad),
        "wop": pack_w(np.ascontiguousarray(np.asarray(I["L0_attn_out_proj_w"], np.float32).T)),
        "wm1": pack_w(np.ascontiguousarray(np.asarray(I["L0_attn_mlp_w1"], np.float32).T)),
        "wm2": pack_w(np.ascontiguousarray(np.asarray(I["L0_attn_mlp_w2"], np.float32).T)),
        "c_ones": ones16, "c_iden": iden, "c_tri": tri, "c_ind": ind,
        "c_sel": sel127,
        "p_ln1": np.concatenate([fm_vec(I["L0_attn_ln1_w"], 8), fm_vec(I["L0_attn_ln1_b"], 8)], 1),
        "p_ln2": np.concatenate([fm_vec(I["L0_attn_ln2_w"], 8), fm_vec(I["L0_attn_ln2_b"], 8)], 1),
        "p_b1": fm_vec(I["L0_ffn_b1"], 32),
        "p_b2": fm_vec(I["L0_ffn_b2"], 8),
        "p_th": fm_vec(I["L0_ffn_quantum_phases"], 32),
        "p_wr": fm_vec(I["L0_ffn_quantum_weights_real"], 32),
        "p_wi": fm_vec(I["L0_ffn_quantum_weights_imag"], 32),
        "p_mb1": fm_vec(I["L0_attn_mlp_b1"], 32),
        "p_mb2": fm_vec(I["L0_attn_mlp_b2"], 8),
        "p_nw": fm_vec(I["L0_attn_norm_w"], 16),
        "p_cw": np.ascontiguousarray(cw_pad.reshape(18, 128, 4).transpose(1, 0, 2).reshape(128, 72)),
        "p_cb": fm_vec(cb_pad, 18),
        "p_dtb": np.asarray(I["L0_attn_dt_bias"], np.float32).reshape(32, 1),
        "p_al": np.asarray(I["L0_attn_A_log"], np.float32).reshape(32, 1),
        "p_D": np.ascontiguousarray(np.broadcast_to(
            np.asarray(I["L0_attn_D"], np.float32).reshape(1, 32), (128, 32))),
        "p_DP": _mk_DP(I),
    }
    xfull = np.asarray(I["L0_attn_x"], np.float32)
    in_maps = []
    for c in range(NCORE):
        xc = xfull[2 * c:2 * c + 2].reshape(T, D)
        arr = np.ascontiguousarray(xc.reshape(T, NC8, 128).transpose(2, 1, 0).reshape(128, NC8 * T))
        m = dict(common)
        m["xin"] = arr
        in_maps.append(m)
    return in_maps


def unpack_out(res):
    outs = []
    for c in range(NCORE):
        O = np.asarray(res[c]["xout"])
        xc = O.reshape(128, NC8, T).transpose(2, 1, 0).reshape(T, D)
        outs.append(xc.reshape(2, LSEQ, D))
    return np.concatenate(outs, 0).astype(np.float32)


def kernel(**inputs):
    in_maps = make_inputs(inputs)
    nc = build_program()
    res = run_bass_kernel_spmd(nc, in_maps, core_ids=list(range(NCORE)))
    return unpack_out(res.results)
